# version 13
# baseline (speedup 1.0000x reference)
import math
import numpy as np
import ml_dtypes
from contextlib import ExitStack

import concourse.bass as bass
import concourse.mybir as mybir
from concourse.bass_utils import run_bass_kernel_spmd

F32 = mybir.dt.float32
BF16 = mybir.dt.bfloat16
I32 = mybir.dt.int32
AF = mybir.ActivationFunctionType
ALU = mybir.AluOpType
AX = mybir.AxisListType

B, S, D = 8, 2048, 4096
H, HKV, HD = 32, 8, 128
INTER = 11008
KSEL = 205
NEG = -1.0e9
NEGSEL = -2.0e38
ISQ = 1.0 / math.sqrt(HD)
TGT = 208.5
NQT = S // 128
NHC = D // 128
NIC = INTER // 128
NG_HID = D // 128
NG_INT = INTER // 128
EPS = 1.0e-5
BIGM = 2.0e9
NFALSI = 5

SOFF = [0] * (NQT + 1)
for _t in range(1, NQT):
    SOFF[_t + 1] = SOFF[_t] + 128 * (_t + 1)
SPACK = SOFF[NQT]
POFF = [0] * (NQT + 1)
for _k in range(NQT):
    POFF[_k + 1] = POFF[_k] + (S - 128 * _k)
PPACK = POFF[NQT]


class SemC:
    def __init__(self, h):
        self.h = h
        self.v = 0

    def inc(self, inst, k=1):
        inst.then_inc(self.h, k)
        self.v += k
        return self.v


def host_tables():
    t = {}
    inv = 1.0 / 10000.0 ** (np.arange(0, HD, 2, dtype=np.float64) / HD)
    fr = np.outer(np.arange(S, dtype=np.float64), inv)
    emb = np.concatenate([fr, fr], axis=-1)
    t["cosT"] = np.ascontiguousarray(np.cos(emb).T.astype(np.float32))
    sinF = np.sin(emb).T.astype(np.float32)
    sinF[:64] *= -1.0
    t["sinF"] = np.ascontiguousarray(sinF)

    ql = np.arange(128)[:, None]
    kl = np.arange(128)[None, :]
    t["diagm"] = np.where(kl <= ql, 0.0, NEG).astype(np.float32)
    t["diagmT"] = np.ascontiguousarray(t["diagm"].T)
    t["caus01"] = np.ascontiguousarray(
        (t["diagmT"] == 0.0).astype(np.float32)).astype(ml_dtypes.bfloat16)

    from scipy.special import ndtri
    z0 = np.zeros((128, NQT - 1), np.float32)
    sscale = np.zeros((128, NQT - 1), np.float32)
    keep = np.zeros((128, NQT - 1), np.float32)
    for tt in range(1, NQT):
        valid = 128 * tt + np.arange(128) + 1.0
        f = np.clip(KSEL / valid, 1e-4, 0.9999)
        zz = ndtri(1.0 - f)
        z0[:, tt - 1] = zz
        phi = np.exp(-zz * zz / 2.0) / math.sqrt(2 * math.pi)
        sscale[:, tt - 1] = 0.75 / np.maximum(valid * phi, 1e-6)
        keep[:, tt - 1] = (valid <= KSEL).astype(np.float32)
    t["z0"] = z0
    t["sscale"] = sscale
    t["keep"] = keep
    t["cinvl"] = np.repeat((1.2533141 / (128.0 * np.arange(1, NQT))
                            ).astype(np.float32)[None, :], 128, axis=0)
    t["iota8"] = np.ascontiguousarray(
        np.tile(np.arange(1, 9, dtype=np.float32)[None, :], (128, NQT - 1)))
    t["idb"] = np.eye(128, dtype=np.float32).astype(ml_dtypes.bfloat16)
    t["idf"] = np.eye(128, dtype=np.float32)
    t["onesc"] = np.ones((128, 1), np.float32).astype(ml_dtypes.bfloat16)
    t["onesrf"] = np.ones((1, 128), np.float32)
    shv = (4 * (np.arange(128, dtype=np.int32) % 8))[:, None].astype(np.int32)
    t["shv"] = shv
    t["maskv"] = (np.int32(15) << shv).astype(np.int32)
    t["pw2"] = (2.0 ** (-shv.astype(np.float64))).astype(np.float32)
    return t


def one_hot_ind(g_idx, ngroups):
    n = g_idx.shape[0]
    ind = np.zeros((n, ngroups), np.float32)
    ind[np.arange(n), np.asarray(g_idx, dtype=int)] = 1.0
    return ind.astype(ml_dtypes.bfloat16)


def build_program(gmap_g, gmap_u, gmap_d, phases=("A", "B", "W", "C"), dbg=(), a_rs=4, b_grps=4):
    nc = bass.Bass("TRN2", target_bir_lowering=False, debug=False)

    ein = lambda name, shape, dt: nc.dram_tensor(name, shape, dt, kind="ExternalInput").ap()
    x_nat = ein("x_nat", [S, D], F32)
    x_t = ein("x_t", [D, S], F32)
    wqkv_t = ein("wqkv_t", [48, 128, NHC, 128], F32)
    wo_t = ein("wo_t", [8, 128, NHC, 512], F32)
    g_qw = ein("g_qw", [D // 8, INTER], I32)
    g_sc = ein("g_sc", [NG_HID, INTER], F32)
    g_qz = ein("g_qz", [NG_HID, INTER // 8], I32)
    u_qw = ein("u_qw", [D // 8, INTER], I32)
    u_sc = ein("u_sc", [NG_HID, INTER], F32)
    u_qz = ein("u_qz", [NG_HID, INTER // 8], I32)
    d_qw = ein("d_qw", [INTER // 8, D], I32)
    d_sc = ein("d_sc", [NG_INT, D], F32)
    d_qz = ein("d_qz", [NG_INT, D // 8], I32)
    cosT_i = ein("cosT", [128, S], F32)
    sinF_i = ein("sinF", [128, S], F32)
    diagm_i = ein("diagm", [128, 128], F32)
    diagmT_i = ein("diagmT", [128, 128], F32)
    caus01_i = ein("caus01", [128, 128], BF16)
    z0_i = ein("z0c", [128, NQT - 1], F32)
    sscale_i = ein("sscale", [128, NQT - 1], F32)
    keep_i = ein("keepc", [128, NQT - 1], F32)
    cinvl_i = ein("cinvl", [128, NQT - 1], F32)
    iota8_i = ein("iota8", [128, 8 * (NQT - 1)], F32)
    idb_i = ein("idb", [128, 128], BF16)
    idf_i = ein("idf", [128, 128], F32)
    onesc_i = ein("onesc", [128, 1], BF16)
    onesrf_i = ein("onesrf", [1, 128], F32)
    shv_i = ein("shv", [128, 1], I32)
    maskv_i = ein("maskv", [128, 1], I32)
    pw2_i = ein("pw2", [128, 1], F32)
    indg_i = ein("ind_g", [D, NG_HID], BF16)
    indd_i = ein("ind_d", [INTER, NG_INT], BF16)

    out = nc.dram_tensor("out", [S, D], F32, kind="ExternalOutput").ap()
    dbg_out = {}
    def dtap(nm, shape, dtt):
        if nm in dbg:
            dbg_out[nm] = nc.dram_tensor(nm, shape, dtt, kind="ExternalOutput").ap()
        return dbg_out.get(nm)
    dtap("dq", [H, 128, S], BF16)
    dtap("dk", [HKV, 128, S], BF16)
    dtap("dv", [HKV, S, 128], BF16)
    dtap("dtau", [H, S], F32)
    dtap("dz", [H, S], F32)
    dtap("daT", [NQT, D, 128], BF16)
    dtap("dy1", [S, D], BF16)
    dtap("dw4g", [NIC, D, 128], BF16)
    dtap("dw4d", [INTER, D], BF16)
    dtap("dzsg", [NG_HID, INTER], F32)
    dtap("dzsd", [NG_INT, D], F32)
    dtap("dgt", [NG_HID, 512], F32)
    dtap("dh", [NIC, 128, 512], BF16)

    dri = lambda name, shape, dt: nc.dram_tensor(name, shape, dt, kind="Internal").ap()
    x_bf = dri("x_bf", [D, S], BF16)
    wqkv_bf = dri("wqkv_bf", [48, 128, NHC, 128], BF16)
    wo_bf = dri("wo_bf", [8, 128, NHC, 512], BF16)
    gsc_bf = dri("gsc_bf", [NG_HID, INTER], BF16)
    usc_bf = dri("usc_bf", [NG_HID, INTER], BF16)
    dsc_bf = dri("dsc_bf", [NG_INT, D], BF16)
    qr_dram = dri("qr_dram", [H, 128, S], BF16)
    kr_dram = dri("kr_dram", [HKV, 128, S], BF16)
    v_dram = dri("v_dram", [HKV, S, 128], BF16)
    tau_dram = dri("tau_dram", [H, S], F32)
    aT_dram = dri("aT_dram", [NQT, 128, H, 128], BF16)
    y1_dram = dri("y1_dram", [S, D], BF16)
    a1_dram = dri("a1_dram", [S, D], F32)
    w4g_dram = dri("w4g_dram", [NIC, 128, NHC, 128], BF16)
    w4u_dram = dri("w4u_dram", [NIC, 128, NHC, 128], BF16)
    w4d_dram = dri("w4d_dram", [INTER, D], BF16)

    with ExitStack() as ctx:
        def sb(name, shape, dt, c=None):
            return (c or ctx).enter_context(nc.sbuf_tensor(name, shape, dt))

        def psm(name, shape, dt=F32, c=None):
            return (c or ctx).enter_context(nc.psum_tensor(name, shape, dt))

        def mksem(name):
            return SemC(ctx.enter_context(nc.semaphore(name)))

        s_diagm = sb("s_diagm", [128, 128], F32)
        s_diagmT = sb("s_diagmT", [128, 128], F32)
        s_caus01 = sb("s_caus01", [128, 128], BF16)
        s_z0 = sb("s_z0", [128, NQT - 1], F32)
        s_sscale = sb("s_sscale", [128, NQT - 1], F32)
        s_keep = sb("s_keep", [128, NQT - 1], F32)
        s_cinvl = sb("s_cinvl", [128, NQT - 1], F32)
        s_iota8 = sb("s_iota8", [128, 8 * (NQT - 1)], F32)
        s_idb = sb("s_idb", [128, 128], BF16)
        s_idf = sb("s_idf", [128, 128], F32)
        s_onesc = sb("s_onesc", [128, 1], BF16)
        s_onesrf = sb("s_onesrf", [1, 128], F32)
        s_shv = sb("s_shv", [128, 1], I32)
        s_maskv = sb("s_maskv", [128, 1], I32)
        s_pw2 = sb("s_pw2", [128, 1], F32)

        all_dma_sems = []
        qld = mksem("qld")
        xts = mksem("xts")
        wts = [mksem("wts0"), mksem("wts1")]
        sts = [mksem("sts0"), mksem("sts1")]
        vts = [mksem("vts0"), mksem("vts1")]
        et = mksem("et")
        ev = mksem("ev")
        ea = mksem("ea")
        eg = mksem("eg")
        qst = mksem("qst")
        all_dma_sems.extend([xts, wts[0], wts[1], sts[0], sts[1], vts[0], vts[1]])

        block = ctx.enter_context(nc.Block())

        def on(engname):
            def deco(f):
                getattr(block, engname)(f)
                return None
            return deco

        def phase_barrier():
            marks = [(sm, sm.v) for sm in all_dma_sems + [qld, et, ev, ea, eg, qst]]
            for engname in ("sync", "gpsimd", "tensor", "vector", "scalar"):
                @on(engname)
                def _(eng):
                    for sm, val in marks:
                        if val > 0:
                            eng.wait_ge(sm.h, val)

        @on("sync")
        def _(sy):
            for dst, src in [(s_diagm, diagm_i), (s_diagmT, diagmT_i),
                             (s_caus01, caus01_i),
                             (s_z0, z0_i), (s_sscale, sscale_i), (s_keep, keep_i),
                             (s_cinvl, cinvl_i), (s_iota8, iota8_i),
                             (s_idb, idb_i), (s_idf, idf_i), (s_onesc, onesc_i),
                             (s_onesrf, onesrf_i), (s_shv, shv_i),
                             (s_maskv, maskv_i), (s_pw2, pw2_i)]:
                qld.inc(sy.dma_start(out=dst[:], in_=src), 16)
            sy.wait_ge(qld.h, qld.v)
        CONST_LD = qld.v

        stg_st = [mksem("stgs0"), mksem("stgs1")]
        stg_in = [mksem("stgi0"), mksem("stgi1")]
        all_dma_sems.extend(stg_st + stg_in)

        with ExitStack() as stgctx:
            stg_buf = stgctx.enter_context(nc.sbuf_tensor("stg_buf", [128, 2 * 16384], BF16))
            jobs = []
            for cc in range(32):
                jobs.append((x_t.rearrange("(a p) n -> a p n", p=128)[cc],
                             x_bf.rearrange("(a p) n -> a p n", p=128)[cc]))
            for ic in range(48):
                jobs.append((wqkv_t[ic].rearrange("p a q -> p (a q)"),
                             wqkv_bf[ic].rearrange("p a q -> p (a q)")))
            for cc in range(8):
                jobs.append((wo_t[cc].rearrange("p a q -> p (a q)"),
                             wo_bf[cc].rearrange("p a q -> p (a q)")))
            jobs.append((g_sc, gsc_bf))
            jobs.append((u_sc, usc_bf))
            jobs.append((d_sc, dsc_bf))
            XW_JOBS = 32 + 48

            for ji, (srcap, dstap) in enumerate(jobs):
                par = ji % 2
                npart = srcap.shape[0]
                ncol = srcap.shape[1]

                @on("gpsimd")
                def _(gp, ji=ji, par=par, srcap=srcap, npart=npart, ncol=ncol):
                    if ji >= 2:
                        gp.wait_ge(stg_st[par].h, 16 * (ji // 2))
                    stg_in[par].inc(gp.dma_start(
                        out=stg_buf.ap()[0:npart, par * 16384: par * 16384 + ncol],
                        in_=srcap), 16)

                @on("sync")
                def _(sy, ji=ji, par=par, dstap=dstap, npart=npart, ncol=ncol):
                    sy.wait_ge(stg_in[par].h, stg_in[par].v)
                    stg_st[par].inc(sy.dma_start(
                        out=dstap,
                        in_=stg_buf.ap()[0:npart, par * 16384: par * 16384 + ncol]), 16)
                    if ji == XW_JOBS - 1:
                        sy.wait_ge(stg_st[0].h, 16 * ((XW_JOBS + 1) // 2))
                        sy.wait_ge(stg_st[1].h, 16 * (XW_JOBS // 2))
        XW_STAGED = (16 * ((XW_JOBS + 1) // 2), 16 * (XW_JOBS // 2))
        ALL_STAGED = None

        if "A" in phases:
            with ExitStack() as actx:
                s_cos = sb("s_cos", [128, S], F32, actx)
                s_sinF = sb("s_sinF", [128, S], F32, actx)
                xt_buf = sb("xt_buf", [128, NHC * 512], BF16, actx)
                wt_buf = sb("wt_buf", [128, 2 * D], BF16, actx)
                ta = sb("ta", [128, 512], F32, actx)
                tb = sb("tb", [128, 512], F32, actx)
                stg = sb("stg", [128, 2 * 512], BF16, actx)
                vstg = sb("vstg", [128, 2 * 512], BF16, actx)
                vout = sb("vout", [128, 2 * 512], BF16, actx)
                p_pr = [psm(f"p_pr{i}", [128, 512], F32, actx) for i in range(4)]
                p_vt = psm("p_vt", [128, 128], BF16, actx)

                NICA = H + 2 * HKV


                @on("sync")
                def _(sy):
                    for sm in (stg_st[0], stg_st[1], stg_in[0], stg_in[1]):
                        sy.wait_ge(sm.h, sm.v)
                    qld.inc(sy.dma_start(out=s_cos[:], in_=cosT_i), 16)
                    qld.inc(sy.dma_start(out=s_sinF[:], in_=sinF_i), 16)
                    sy.wait_ge(qld.h, qld.v)
                COS_LD = qld.v

                mm_done = {}
                evac_done = {}
                vstg_free = {0: None, 1: None}
                vt_tr = {}
                vt_evac = {}
                vt_prev = [None]

                for rs in range(a_rs):
                    @on("scalar")
                    def _(sc, rs=rs):
                        if rs == 0:
                            sc.wait_ge(qld.h, CONST_LD)
                            sc.wait_ge(stg_st[0].h, stg_st[0].v)
                            sc.wait_ge(stg_st[1].h, stg_st[1].v)
                            sc.wait_ge(stg_in[0].h, stg_in[0].v)
                            sc.wait_ge(stg_in[1].h, stg_in[1].v)
                        else:
                            sc.wait_ge(et.h, mm_done[rs * NICA - 1])
                        xts.inc(sc.dma_start(
                            out=xt_buf.ap().rearrange("p (c n) -> p c n", n=512),
                            in_=x_bf[:, rs * 512:(rs + 1) * 512]
                                .rearrange("(c p) n -> p c n", p=128)), 16)

                    for ic in range(NICA):
                        u = rs * NICA + ic

                        @on("scalar")
                        def _(sc, u=u, ic=ic):
                            if u >= 2:
                                sc.wait_ge(et.h, mm_done[u - 2])
                            wts[u % 2].inc(sc.dma_start(
                                out=wt_buf.ap()[:, (u % 2) * D:(u % 2 + 1) * D]
                                    .rearrange("p (c i) -> p c i", i=128),
                                in_=wqkv_bf[ic]), 16)

                        @on("tensor")
                        def _(te, rs=rs, u=u):
                            te.wait_ge(xts.h, 16 * (rs + 1))
                            te.wait_ge(wts[u % 2].h, wts[u % 2].v)
                            if u >= 4:
                                sm, val = evac_done[u - 4]
                                te.wait_ge(sm.h, val)
                            pb = p_pr[u % 4]
                            inst = None
                            for dc in range(NHC):
                                inst = te.matmul(
                                    pb[:], wt_buf.ap()[:, (u % 2) * D + dc * 128:(u % 2) * D + (dc + 1) * 128],
                                    xt_buf.ap()[:, dc * 512:(dc + 1) * 512],
                                    start=(dc == 0), stop=(dc == NHC - 1))
                            mm_done[u] = et.inc(inst)

                        if ic < H + HKV:
                            @on("vector")
                            def _(ve, rs=rs, u=u):
                                pb = p_pr[u % 4]
                                ve.wait_ge(et.h, mm_done[u])
                                ve.wait_ge(qld.h, COS_LD)
                                ve.wait_ge(sts[u % 2].h, sts[u % 2].v)
                                ve.wait_ge(ev.h, ev.v)
                                cs = s_cos.ap()[:, rs * 512:(rs + 1) * 512]
                                sn = s_sinF.ap()[:, rs * 512:(rs + 1) * 512]
                                ev.inc(ve.tensor_tensor(out=ta[:], in0=pb[:], in1=cs, op=ALU.mult))
                                ve.tensor_tensor(out=tb[0:64, :], in0=pb[64:128, :],
                                                 in1=sn[0:64, :], op=ALU.mult)
                                ev.inc(ve.tensor_tensor(out=tb[64:128, :], in0=pb[0:64, :],
                                                        in1=sn[64:128, :], op=ALU.mult))
                                ve.wait_ge(ev.h, ev.v)
                                i4 = ve.tensor_tensor(out=stg.ap()[:, (u % 2) * 512:(u % 2 + 1) * 512], in0=ta[:],
                                                      in1=tb[:], op=ALU.add)
                                evac_done[u] = (ev, ev.inc(i4))

                            @on("sync")
                            def _(sy, rs=rs, ic=ic, u=u):
                                sm, val = evac_done[u]
                                sy.wait_ge(sm.h, val)
                                if ic < H:
                                    dst = qr_dram[ic, :, rs * 512:(rs + 1) * 512]
                                else:
                                    dst = kr_dram[ic - H, :, rs * 512:(rs + 1) * 512]
                                sts[u % 2].inc(sy.dma_start(out=dst, in_=stg.ap()[:, (u % 2) * 512:(u % 2 + 1) * 512]), 16)
                        else:
                            kv = ic - H - HKV

                            @on("scalar")
                            def _(sc, u=u):
                                pb = p_pr[u % 4]
                                sc.wait_ge(et.h, mm_done[u])
                                if vstg_free[u % 2] is not None:
                                    sc.wait_ge(et.h, vstg_free[u % 2])
                                i1 = sc.copy(vstg.ap()[:, (u % 2) * 512:(u % 2 + 1) * 512], pb[:])
                                evac_done[u] = (ea, ea.inc(i1))

                            for j4 in range(4):
                                @on("tensor")
                                def _(te, u=u, j4=j4):
                                    if j4 == 0:
                                        sm, val = evac_done[u]
                                        te.wait_ge(sm.h, val)
                                    prev = vt_evac.get((u, j4 - 1), vt_prev[0])
                                    if prev is not None:
                                        te.wait_ge(ev.h, prev)
                                    inst = te.transpose(
                                        p_vt[:], vstg.ap()[:, (u % 2) * 512 + j4 * 128:(u % 2) * 512 + (j4 + 1) * 128],
                                        s_idb[:])
                                    vt_tr[(u, j4)] = et.inc(inst)

                                @on("vector")
                                def _(ve, u=u, j4=j4):
                                    if j4 == 0:
                                        ve.wait_ge(vts[u % 2].h, vts[u % 2].v)
                                    ve.wait_ge(et.h, vt_tr[(u, j4)])
                                    i1 = ve.tensor_copy(
                                        vout.ap()[:, (u % 2) * 512 + j4 * 128:(u % 2) * 512 + (j4 + 1) * 128], p_vt[:])
                                    vt_evac[(u, j4)] = ev.inc(i1)
                            vstg_free[u % 2] = vt_tr[(u, 3)]
                            vt_prev[0] = vt_evac[(u, 3)]

                            @on("sync")
                            def _(sy, u=u, kv=kv, rs=rs):
                                sy.wait_ge(ev.h, vt_evac[(u, 3)])
                                vts[u % 2].inc(sy.dma_start(
                                    out=v_dram[kv, rs * 512:(rs + 1) * 512, :]
                                        .rearrange("(a k) h -> k a h", k=128),
                                    in_=vout.ap()[:, (u % 2) * 512:(u % 2 + 1) * 512].rearrange("k (a h) -> k a h", h=128)), 16)


                if "dq" in dbg:
                    @on("sync")
                    def _(sy):
                        for sm in (sts[0], sts[1], vts[0], vts[1]):
                            sy.wait_ge(sm.h, sm.v)
                        qst.inc(sy.dma_start(out=dbg_out["dq"], in_=qr_dram), 16)
                        qst.inc(sy.dma_start(out=dbg_out["dk"], in_=kr_dram), 16)
                        qst.inc(sy.dma_start(out=dbg_out["dv"], in_=v_dram), 16)


        if "B" in phases:
            with ExitStack() as bctx:
                qr1 = sb("qr1", [128, 2 * S], BF16, bctx)
                kr1 = sb("kr1", [128, 2 * S], BF16, bctx)
                qr3 = sb("qr3", [128, 2 * S], BF16, bctx)
                kr3 = sb("kr3", [128, 2 * S], BF16, bctx)
                v3 = sb("v3", [128, 2 * S], BF16, bctx)
                tbrz = sb("tbrz", [128, S], F32, bctx)
                cjunk = sb("cjunk", [128, 2 * S], BF16, bctx)
                d2 = sb("d2", [128, 2 * S], BF16, bctx)
                exb = sb("exb", [128, S], BF16, bctx)
                aT_stg = sb("aT_stg", [128, S], BF16, bctx)
                zrow = sb("zrow", [1, S], F32, bctx)
                taub = sb("taub", [128, 16], F32, bctx)
                taubT = sb("taubT", [16, 128], F32, bctx)
                sel_names = ["sabs", "sig", "rstep", "tauc", "cntc", "ta_", "ca_",
                             "tb_", "cb_", "hava", "havb", "illa", "illb",
                             "blo_t", "blo_c", "bhi_t", "bhi_c",
                             "tmp0", "tmp1", "tmp2", "tmp3", "jl", "jh",
                             "taulo", "tauhi", "tstar", "ones15", "negs15"]
                SEL = {nm: sb("sel_" + nm, [128, NQT - 1], F32, bctx) for nm in sel_names}
                SELI = {nm: sb("seli_" + nm, [128, NQT - 1], I32, bctx)
                        for nm in ["mi", "mi2"]}
                m8lo = sb("m8lo", [128, 8 * (NQT - 1)], F32, bctx)
                m8hi = sb("m8hi", [128, 8 * (NQT - 1)], F32, bctx)
                ind120 = sb("ind120", [128, 8 * (NQT - 1)], F32, bctx)
                vsel = sb("vsel", [128, 8 * (NQT - 1)], F32, bctx)
                S_pack = sb("S_pack", [128, SPACK], BF16, bctx)
                st_pack = sb("st_pack", [128, PPACK], BF16, bctx)
                p_pack = sb("p_pack", [128, PPACK], BF16, bctx)

                p_s = [psm(f"p_s{i}", [128, 512], F32, bctx) for i in range(2)]
                p_st2 = [psm(f"p_st{i}", [128, 512], F32, bctx) for i in range(2)]
                p_z = psm("p_z", [1, 512], F32, bctx)
                p_av = psm("p_av", [128, 512], F32, bctx)
                p_mis = psm("p_mis", [128, 512], F32, bctx)
                p_tr2 = psm("p_tr2", [16, 128], F32, bctx)
                phase_barrier()

                q1l = [mksem("q1l0"), mksem("q1l1")]
                k1l = [mksem("k1l0"), mksem("k1l1")]
                q3l = [mksem("q3l0"), mksem("q3l1")]
                k3l = [mksem("k3l0"), mksem("k3l1")]
                v3l = [mksem("v3l0"), mksem("v3l1")]
                tbl = mksem("tbl")
                tbo = mksem("tbo")
                ats = mksem("ats")
                all_dma_sems.extend(q1l + k1l + q3l + k3l + v3l + [tbl, tbo, ats])

                GTGT = TGT
                E = SEL

                lm = {
                    "sel_done": None,
                    "b3dve_done": None,
                    "av_done": None,
                    "atstore": None,
                    "tau_tr": None,
                    "tb_used": None,
                    "q1_used": {0: None, 1: None},
                    "k1_used": {0: None, 1: None},
                    "q3_used": {0: None, 1: None},
                    "k3_used": {0: None, 1: None},
                    "v3_used": {0: None, 1: None},
                    "stev": {},
                    "sevac": {},
                    "psfree": {0: None, 1: None},
                    "pstfree": {0: None, 1: None},
                }

                def vsec(fns):
                    @on("vector")
                    def _(ve):
                        for fn in fns:
                            ve.wait_ge(ev.h, ev.v)
                            ev.inc(fn(ve))
                    return ev.v

                def emit_b1_loads(h):
                    par = h % 2
                    kv = h // 4

                    @on("sync")
                    def _(sy, h=h, par=par, kv=kv):
                        if lm["q1_used"][par] is not None:
                            sm, val = lm["q1_used"][par]
                            sy.wait_ge(sm.h, val)
                        q1l[par].inc(sy.dma_start(
                            out=qr1.ap()[:, par * S:(par + 1) * S], in_=qr_dram[h]), 16)
                        if lm["k1_used"][par] is not None:
                            sm, val = lm["k1_used"][par]
                            sy.wait_ge(sm.h, val)
                        k1l[par].inc(sy.dma_start(
                            out=kr1.ap()[:, par * S:(par + 1) * S], in_=kr_dram[kv]), 16)

                def emit_b1_scores(h):
                    par = h % 2
                    Q1 = q1l[par].v
                    K1 = k1l[par].v
                    sevac = {}
                    seq = [0]
                    last_mm = [None]
                    for t in range(1, NQT):
                        L = 128 * (t + 1)
                        nks = (L + 511) // 512
                        for ks in range(nks):
                            c0 = 512 * ks
                            cw = min(512, L - c0)
                            pidx = seq[0] % 2
                            seq[0] += 1

                            @on("tensor")
                            def _(te, t=t, ks=ks, c0=c0, cw=cw, pidx=pidx, par=par, Q1=Q1, K1=K1):
                                te.wait_ge(q1l[par].h, Q1)
                                te.wait_ge(k1l[par].h, K1)
                                if lm["psfree"][pidx] is not None:
                                    sm, val = lm["psfree"][pidx]
                                    te.wait_ge(sm.h, val)
                                inst = te.matmul(
                                    p_s[pidx][:, 0:cw],
                                    qr1.ap()[:, par * S + 128 * t: par * S + 128 * (t + 1)],
                                    kr1.ap()[:, par * S + c0: par * S + c0 + cw],
                                    start=True, stop=True)
                                lm[("mm1", t, ks)] = et.inc(inst)
                                last_mm[0] = lm[("mm1", t, ks)]

                            @on("scalar")
                            def _(sc, t=t, ks=ks, c0=c0, cw=cw, pidx=pidx):
                                sc.wait_ge(et.h, lm[("mm1", t, ks)])
                                if t == 1 and ks == 0 and lm["sel_done"] is not None:
                                    sc.wait_ge(ev.h, lm["sel_done"])
                                sc.wait_ge(ea.h, ea.v)
                                i1 = sc.copy(
                                    S_pack.ap()[:, SOFF[t] + c0: SOFF[t] + c0 + cw],
                                    p_s[pidx][:, 0:cw])
                                sevac[(t, ks)] = (ea, ea.inc(i1))
                                lm["psfree"][pidx] = sevac[(t, ks)]
                    lm["q1_used"][par] = (et, last_mm[0])
                    lm["k1_used"][par] = (et, last_mm[0])

                    @on("vector")
                    def _(ve):
                        ve.wait_ge(ev.h, ev.v)
                        for t in range(1, NQT):
                            L = 128 * (t + 1)
                            sm, val = sevac[(t, (L - 1) // 512)]
                            ve.wait_ge(sm.h, val)
                            sl = S_pack.ap()[:, SOFF[t] + L - 128: SOFF[t] + L]
                            ev.inc(ve.tensor_tensor(out=sl, in0=sl, in1=s_diagm[:], op=ALU.add))
                    DIAG_ALL = ev.v

                    @on("scalar")
                    def _(sc):
                        for t in range(1, NQT):
                            for ks in range((128 * t + 511) // 512):
                                sm, val = sevac[(t, ks)]
                                sc.wait_ge(sm.h, val)
                            sc.wait_ge(ea.h, ea.v)
                            ea.inc(sc.activation(
                                out=d2.ap()[:, S:S + 128 * t],
                                in_=S_pack.ap()[:, SOFF[t]: SOFF[t] + 128 * t],
                                func=AF.Abs, accum_out=SEL["sabs"][:, t - 1:t]))
                    SABS_DONE = ea.v
                    return sevac, DIAG_ALL, SABS_DONE

                def emit_b1_select(h, scores_ret):
                    sevac, DIAG_ALL, SABS_DONE = scores_ret

                    def counts(round_no):
                        @on("vector")
                        def _(ve):
                            if round_no == 0:
                                ve.wait_ge(ea.h, SABS_DONE)
                                ve.wait_ge(ev.h, DIAG_ALL)
                            ve.wait_ge(ev.h, ev.v)
                            for t in range(1, NQT):
                                L = 128 * (t + 1)
                                ev.inc(ve.tensor_scalar(
                                    out=d2.ap()[:, 0:L],
                                    in0=S_pack.ap()[:, SOFF[t]: SOFF[t] + L],
                                    scalar1=SEL["tauc"][:, t - 1:t], scalar2=0.0,
                                    op0=ALU.is_ge, op1=ALU.add,
                                    accum_out=SEL["cntc"][:, t - 1:t]))

                    def update_best():
                        vsec([
                            lambda ve: ve.tensor_scalar(out=E["tmp0"][:], in0=E["cntc"][:], scalar1=KSEL - 0.5, scalar2=None, op0=ALU.is_ge),
                            lambda ve: ve.tensor_tensor(out=E["tmp1"][:], in0=E["cntc"][:], in1=E["blo_c"][:], op=ALU.is_lt),
                            lambda ve: ve.tensor_tensor(out=E["tmp2"][:], in0=E["tmp0"][:], in1=E["tmp1"][:], op=ALU.mult),
                            lambda ve: ve.tensor_scalar(out=SELI["mi"][:], in0=E["tmp2"][:], scalar1=0.5, scalar2=None, op0=ALU.is_ge),
                            lambda ve: ve.copy_predicated(out=E["blo_c"][:], mask=SELI["mi"][:], data=E["cntc"][:]),
                            lambda ve: ve.copy_predicated(out=E["blo_t"][:], mask=SELI["mi"][:], data=E["tauc"][:]),
                            lambda ve: ve.tensor_scalar(out=E["tmp0"][:], in0=E["cntc"][:], scalar1=KSEL - 0.5, scalar2=None, op0=ALU.is_lt),
                            lambda ve: ve.tensor_tensor(out=E["tmp1"][:], in0=E["cntc"][:], in1=E["bhi_c"][:], op=ALU.is_gt),
                            lambda ve: ve.tensor_tensor(out=E["tmp2"][:], in0=E["tmp0"][:], in1=E["tmp1"][:], op=ALU.mult),
                            lambda ve: ve.tensor_scalar(out=SELI["mi"][:], in0=E["tmp2"][:], scalar1=0.5, scalar2=None, op0=ALU.is_ge),
                            lambda ve: ve.copy_predicated(out=E["bhi_c"][:], mask=SELI["mi"][:], data=E["cntc"][:]),
                            lambda ve: ve.copy_predicated(out=E["bhi_t"][:], mask=SELI["mi"][:], data=E["tauc"][:]),
                        ])

                    def update_bracket():
                        vsec([
                            lambda ve: ve.tensor_scalar(out=E["tmp0"][:], in0=E["cntc"][:], scalar1=GTGT, scalar2=None, op0=ALU.is_ge),
                            lambda ve: ve.tensor_scalar(out=E["tmp1"][:], in0=E["tmp0"][:], scalar1=-1.0, scalar2=1.0, op0=ALU.mult, op1=ALU.add),
                            lambda ve: ve.tensor_tensor(out=E["tmp2"][:], in0=E["tmp1"][:], in1=E["hava"][:], op=ALU.mult),
                            lambda ve: ve.tensor_scalar(out=SELI["mi"][:], in0=E["tmp2"][:], scalar1=0.5, scalar2=None, op0=ALU.is_ge),
                            lambda ve: ve.tensor_scalar(out=E["tmp3"][:], in0=E["illa"][:], scalar1=0.5, scalar2=None, op0=ALU.mult),
                            lambda ve: ve.tensor_copy(E["illa"][:], E["ones15"][:]),
                            lambda ve: ve.copy_predicated(out=E["illa"][:], mask=SELI["mi"][:], data=E["tmp3"][:]),
                            lambda ve: ve.tensor_tensor(out=E["tmp2"][:], in0=E["tmp0"][:], in1=E["havb"][:], op=ALU.mult),
                            lambda ve: ve.tensor_scalar(out=SELI["mi"][:], in0=E["tmp2"][:], scalar1=0.5, scalar2=None, op0=ALU.is_ge),
                            lambda ve: ve.tensor_scalar(out=E["tmp3"][:], in0=E["illb"][:], scalar1=0.5, scalar2=None, op0=ALU.mult),
                            lambda ve: ve.tensor_copy(E["illb"][:], E["ones15"][:]),
                            lambda ve: ve.copy_predicated(out=E["illb"][:], mask=SELI["mi"][:], data=E["tmp3"][:]),
                            lambda ve: ve.tensor_scalar(out=SELI["mi"][:], in0=E["tmp0"][:], scalar1=0.5, scalar2=None, op0=ALU.is_ge),
                            lambda ve: ve.copy_predicated(out=E["ta_"][:], mask=SELI["mi"][:], data=E["tauc"][:]),
                            lambda ve: ve.copy_predicated(out=E["ca_"][:], mask=SELI["mi"][:], data=E["cntc"][:]),
                            lambda ve: ve.tensor_scalar(out=SELI["mi2"][:], in0=E["tmp0"][:], scalar1=0.5, scalar2=None, op0=ALU.is_lt),
                            lambda ve: ve.copy_predicated(out=E["tb_"][:], mask=SELI["mi2"][:], data=E["tauc"][:]),
                            lambda ve: ve.copy_predicated(out=E["cb_"][:], mask=SELI["mi2"][:], data=E["cntc"][:]),
                            lambda ve: ve.tensor_tensor(out=E["hava"][:], in0=E["hava"][:], in1=E["tmp0"][:], op=ALU.max),
                            lambda ve: ve.tensor_tensor(out=E["havb"][:], in0=E["havb"][:], in1=E["tmp1"][:], op=ALU.max),
                        ])

                    vsec([
                        lambda ve: ve.memset(E["ones15"][:], 1.0),
                        lambda ve: ve.memset(E["negs15"][:], NEGSEL),
                        lambda ve: ve.memset(E["blo_c"][:], 1.0e9),
                        lambda ve: ve.memset(E["blo_t"][:], 0.0),
                        lambda ve: ve.memset(E["bhi_c"][:], -1.0e9),
                        lambda ve: ve.memset(E["bhi_t"][:], 0.0),
                    ])

                    @on("vector")
                    def _(ve):
                        ve.wait_ge(ea.h, SABS_DONE)
                        ve.wait_ge(ev.h, ev.v)
                        ev.inc(ve.tensor_tensor(out=E["sig"][:], in0=E["sabs"][:], in1=s_cinvl[:], op=ALU.mult))
                        ve.wait_ge(ev.h, ev.v)
                        ev.inc(ve.tensor_tensor(out=E["rstep"][:], in0=s_sscale[:], in1=E["sig"][:], op=ALU.mult))
                        ve.wait_ge(ev.h, ev.v)
                        ev.inc(ve.tensor_tensor(out=E["tauc"][:], in0=E["sig"][:], in1=s_z0[:], op=ALU.mult))
                    counts(0)
                    vsec([
                        lambda ve: ve.tensor_copy(E["ta_"][:], E["tauc"][:]),
                        lambda ve: ve.tensor_copy(E["ca_"][:], E["cntc"][:]),
                        lambda ve: ve.tensor_copy(E["tb_"][:], E["tauc"][:]),
                        lambda ve: ve.tensor_copy(E["cb_"][:], E["cntc"][:]),
                        lambda ve: ve.tensor_scalar(out=E["hava"][:], in0=E["cntc"][:], scalar1=GTGT, scalar2=None, op0=ALU.is_ge),
                        lambda ve: ve.tensor_scalar(out=E["havb"][:], in0=E["cntc"][:], scalar1=GTGT, scalar2=None, op0=ALU.is_lt),
                        lambda ve: ve.memset(E["illa"][:], 1.0),
                        lambda ve: ve.memset(E["illb"][:], 1.0),
                    ])
                    update_best()
                    vsec([
                        lambda ve: ve.tensor_scalar(out=E["tmp0"][:], in0=E["cntc"][:], scalar1=GTGT, scalar2=None, op0=ALU.subtract),
                        lambda ve: ve.tensor_tensor(out=E["tmp1"][:], in0=E["tmp0"][:], in1=E["rstep"][:], op=ALU.mult),
                        lambda ve: ve.tensor_tensor(out=E["tauc"][:], in0=E["tauc"][:], in1=E["tmp1"][:], op=ALU.add),
                    ])
                    counts(1)
                    update_best()
                    update_bracket()
                    for rnd in range(NFALSI):
                        vsec([
                            lambda ve: ve.tensor_scalar(out=E["tmp0"][:], in0=E["ca_"][:], scalar1=GTGT, scalar2=None, op0=ALU.subtract),
                            lambda ve: ve.tensor_tensor(out=E["tmp0"][:], in0=E["tmp0"][:], in1=E["illa"][:], op=ALU.mult),
                            lambda ve: ve.tensor_scalar(out=E["tmp1"][:], in0=E["cb_"][:], scalar1=GTGT, scalar2=None, op0=ALU.subtract),
                            lambda ve: ve.tensor_tensor(out=E["tmp1"][:], in0=E["tmp1"][:], in1=E["illb"][:], op=ALU.mult),
                            lambda ve: ve.tensor_tensor(out=E["tmp2"][:], in0=E["tmp0"][:], in1=E["tmp1"][:], op=ALU.subtract),
                            lambda ve: ve.tensor_scalar(out=E["tmp2"][:], in0=E["tmp2"][:], scalar1=1e-9, scalar2=None, op0=ALU.max),
                            lambda ve: ve.reciprocal(E["tmp2"][:], E["tmp2"][:]),
                            lambda ve: ve.tensor_tensor(out=E["tmp0"][:], in0=E["tmp0"][:], in1=E["tmp2"][:], op=ALU.mult),
                            lambda ve: ve.tensor_tensor(out=E["tmp1"][:], in0=E["tb_"][:], in1=E["ta_"][:], op=ALU.subtract),
                            lambda ve: ve.tensor_tensor(out=E["tmp1"][:], in0=E["tmp1"][:], in1=E["tmp0"][:], op=ALU.mult),
                            lambda ve: ve.tensor_tensor(out=E["tmp1"][:], in0=E["ta_"][:], in1=E["tmp1"][:], op=ALU.add),
                            lambda ve: ve.tensor_scalar(out=E["tmp2"][:], in0=E["cntc"][:], scalar1=GTGT, scalar2=None, op0=ALU.subtract),
                            lambda ve: ve.tensor_tensor(out=E["tmp2"][:], in0=E["tmp2"][:], in1=E["rstep"][:], op=ALU.mult),
                            lambda ve: ve.tensor_tensor(out=E["tmp2"][:], in0=E["tauc"][:], in1=E["tmp2"][:], op=ALU.add),
                            lambda ve: ve.tensor_tensor(out=E["tmp3"][:], in0=E["hava"][:], in1=E["havb"][:], op=ALU.mult),
                            lambda ve: ve.tensor_scalar(out=SELI["mi"][:], in0=E["tmp3"][:], scalar1=0.5, scalar2=None, op0=ALU.is_ge),
                            lambda ve: ve.tensor_copy(E["tauc"][:], E["tmp2"][:]),
                            lambda ve: ve.copy_predicated(out=E["tauc"][:], mask=SELI["mi"][:], data=E["tmp1"][:]),
                        ])
                        counts(2 + rnd)
                        update_best()
                        update_bracket()

                    lo_done = {}
                    m8lo_ev = {}
                    for t in range(1, NQT):
                        L = 128 * (t + 1)
                        sl_ = (t % 2) * S

                        @on("vector")
                        def _(ve, t=t, L=L, sl_=sl_):
                            ve.wait_ge(ev.h, ev.v)
                            lo_done[("cmp", t)] = ev.inc(ve.tensor_scalar(
                                out=d2.ap()[:, sl_:sl_ + L],
                                in0=S_pack.ap()[:, SOFF[t]:SOFF[t] + L],
                                scalar1=E["blo_t"][:, t - 1:t], scalar2=-BIGM,
                                op0=ALU.is_lt, op1=ALU.mult))
                            ve.wait_ge(ev.h, ev.v)
                            ev.inc(ve.tensor_scalar(
                                out=exb.ap()[:, 0:L],
                                in0=S_pack.ap()[:, SOFF[t]:SOFF[t] + L],
                                scalar1=E["bhi_t"][:, t - 1:t], scalar2=-BIGM,
                                op0=ALU.is_ge, op1=ALU.mult))
                            ve.wait_ge(ev.h, ev.v)
                            ev.inc(ve.tensor_tensor(
                                out=exb.ap()[:, 0:L], in0=exb.ap()[:, 0:L],
                                in1=S_pack.ap()[:, SOFF[t]:SOFF[t] + L], op=ALU.add))
                            ve.wait_ge(ev.h, ev.v)
                            ev.inc(ve.max(out=m8hi.ap()[:, 8 * (t - 1):8 * t], in_=exb.ap()[:, 0:L]))
                            if t >= 2:
                                ve.wait_ge(eg.h, lo_done[t - 1])
                                ve.wait_ge(ev.h, ev.v)
                                Lp = 128 * t
                                m8lo_ev[t - 1] = ev.inc(ve.max(
                                    out=m8lo.ap()[:, 8 * (t - 2):8 * (t - 1)],
                                    in_=cjunk.ap()[:, ((t - 1) % 2) * S:((t - 1) % 2) * S + Lp]))

                        @on("gpsimd")
                        def _(gp, t=t, L=L, sl_=sl_):
                            gp.wait_ge(ev.h, lo_done[("cmp", t)])
                            if t >= 3:
                                gp.wait_ge(ev.h, m8lo_ev[t - 2])
                            gp.wait_ge(eg.h, eg.v)
                            lo_done[t] = eg.inc(gp.tensor_tensor(
                                out=cjunk.ap()[:, sl_:sl_ + L],
                                in0=d2.ap()[:, sl_:sl_ + L],
                                in1=S_pack.ap()[:, SOFF[t]:SOFF[t] + L], op=ALU.subtract))

                    @on("vector")
                    def _(ve):
                        t = NQT - 1
                        L = 128 * (t + 1)
                        ve.wait_ge(eg.h, lo_done[t])
                        ve.wait_ge(ev.h, ev.v)
                        m8lo_ev[t] = ev.inc(ve.max(
                            out=m8lo.ap()[:, 8 * (t - 1):8 * t],
                            in_=cjunk.ap()[:, (t % 2) * S:(t % 2) * S + L]))

                    vsec([
                        lambda ve: ve.tensor_scalar(out=E["jl"][:], in0=E["blo_c"][:], scalar1=-(KSEL - 1.0), scalar2=None, op0=ALU.add),
                        lambda ve: ve.tensor_scalar(out=E["jl"][:], in0=E["jl"][:], scalar1=1.0, scalar2=8.0, op0=ALU.max, op1=ALU.min),
                        lambda ve: ve.tensor_scalar(out=E["jh"][:], in0=E["bhi_c"][:], scalar1=-1.0, scalar2=float(KSEL), op0=ALU.mult, op1=ALU.add),
                        lambda ve: ve.tensor_scalar(out=E["jh"][:], in0=E["jh"][:], scalar1=1.0, scalar2=8.0, op0=ALU.max, op1=ALU.min),
                        lambda ve: ve.tensor_tensor(
                            out=ind120.ap().rearrange("p (t j) -> p t j", j=8),
                            in0=s_iota8.ap().rearrange("p (t j) -> p t j", j=8),
                            in1=E["jl"].ap().unsqueeze(2).broadcast_to([128, NQT - 1, 8]),
                            op=ALU.is_equal),
                        lambda ve: ve.tensor_tensor(out=vsel[:], in0=m8lo[:], in1=ind120[:], op=ALU.mult),
                        lambda ve: ve.reduce_sum(E["taulo"][:], vsel.ap().rearrange("p (t j) -> p t j", j=8), axis=AX.X),
                        lambda ve: ve.tensor_scalar(out=E["taulo"][:], in0=E["taulo"][:], scalar1=-1.0, scalar2=None, op0=ALU.mult),
                        lambda ve: ve.tensor_tensor(
                            out=ind120.ap().rearrange("p (t j) -> p t j", j=8),
                            in0=s_iota8.ap().rearrange("p (t j) -> p t j", j=8),
                            in1=E["jh"].ap().unsqueeze(2).broadcast_to([128, NQT - 1, 8]),
                            op=ALU.is_equal),
                        lambda ve: ve.tensor_tensor(out=vsel[:], in0=m8hi[:], in1=ind120[:], op=ALU.mult),
                        lambda ve: ve.reduce_sum(E["tauhi"][:], vsel.ap().rearrange("p (t j) -> p t j", j=8), axis=AX.X),
                        lambda ve: ve.tensor_scalar(out=SELI["mi"][:], in0=E["blo_c"][:], scalar1=1.0e8, scalar2=None, op0=ALU.is_lt),
                        lambda ve: ve.tensor_copy(E["tstar"][:], E["tauhi"][:]),
                        lambda ve: ve.copy_predicated(out=E["tstar"][:], mask=SELI["mi"][:], data=E["taulo"][:]),
                        lambda ve: ve.tensor_scalar(out=SELI["mi"][:], in0=E["bhi_c"][:], scalar1=KSEL - 8.5, scalar2=None, op0=ALU.is_ge),
                        lambda ve: ve.copy_predicated(out=E["tstar"][:], mask=SELI["mi"][:], data=E["tauhi"][:]),
                        lambda ve: ve.tensor_scalar(out=SELI["mi"][:], in0=E["blo_c"][:], scalar1=KSEL + 7.5, scalar2=None, op0=ALU.is_le),
                        lambda ve: ve.copy_predicated(out=E["tstar"][:], mask=SELI["mi"][:], data=E["taulo"][:]),
                        lambda ve: ve.tensor_scalar(out=SELI["mi"][:], in0=s_keep[:], scalar1=0.5, scalar2=None, op0=ALU.is_ge),
                        lambda ve: ve.copy_predicated(out=E["tstar"][:], mask=SELI["mi"][:], data=E["negs15"][:]),
                    ])

                    @on("vector")
                    def _(ve):
                        if lm["tau_tr"] is not None:
                            ve.wait_ge(et.h, lm["tau_tr"])
                        ve.wait_ge(ev.h, ev.v)
                        ev.inc(ve.memset(taub.ap()[:, 0:1], NEGSEL))
                        ve.wait_ge(ev.h, ev.v)
                        ev.inc(ve.tensor_copy(taub.ap()[:, 1:16], E["tstar"][:]))
                    lm["sel_done"] = ev.v
                    TAUB = ev.v

                    @on("tensor")
                    def _(te, h=h):
                        te.wait_ge(ev.h, TAUB)
                        lm["tau_tr"] = et.inc(te.transpose(p_tr2[:], taub[:], s_idf[:]))

                    @on("vector")
                    def _(ve):
                        ve.wait_ge(et.h, lm["tau_tr"])
                        ve.wait_ge(ev.h, ev.v)
                        ev.inc(ve.tensor_copy(taubT[:], p_tr2[:]))
                    TAUTT = ev.v

                    @on("sync")
                    def _(sy, h=h):
                        sy.wait_ge(ev.h, TAUTT)
                        tbo.inc(sy.dma_start(
                            out=tau_dram[h].rearrange("(t q) -> t q", q=128),
                            in_=taubT[:]), 16)
                        sy.wait_ge(tbo.h, tbo.v)
                        if lm["tb_used"] is not None:
                            sy.wait_ge(eg.h, lm["tb_used"])
                        tbl.inc(sy.dma_start(
                            out=tbrz[:],
                            in_=tau_dram[h:h + 1, :].broadcast_to([128, S])), 16)
                    if "dtau" in dbg:
                        @on("sync")
                        def _(sy, h=h):
                            qst.inc(sy.dma_start(out=dbg_out["dtau"][h:h + 1], in_=tau_dram[h:h + 1]), 16)

                def emit_b3_loads(h):
                    par = h % 2
                    kv = h // 4

                    @on("sync")
                    def _(sy, h=h, par=par, kv=kv):
                        for key, sem_l, buf, src in [
                            ("q3_used", q3l, qr3, qr_dram[h]),
                            ("k3_used", k3l, kr3, kr_dram[kv]),
                        ]:
                            if lm[key][par] is not None:
                                sm, val = lm[key][par]
                                sy.wait_ge(sm.h, val)
                            sem_l[par].inc(sy.dma_start(
                                out=buf.ap()[:, par * S:(par + 1) * S], in_=src), 16)
                        if lm["v3_used"][par] is not None:
                            sm, val = lm["v3_used"][par]
                            sy.wait_ge(sm.h, val)
                        v3l[par].inc(sy.dma_start(
                            out=v3.ap()[:, par * S:(par + 1) * S]
                                .rearrange("p (t u) -> p t u", u=128),
                            in_=v_dram[kv].rearrange("(t p) u -> p t u", p=128)), 16)

                def emit_b3_compute(h):
                    par = h % 2
                    Q3 = q3l[par].v
                    K3 = k3l[par].v
                    V3 = v3l[par].v
                    TBLD = tbl.v
                    st_ev = {}
                    seq2 = [0]
                    last_mm = [None]
                    for kt in range(NQT):
                        q0 = 128 * kt
                        w = S - q0
                        ncs = (w + 511) // 512
                        for cs_ in range(ncs):
                            c0 = q0 + 512 * cs_
                            cw = min(512, S - c0)
                            pidx = seq2[0] % 2
                            seq2[0] += 1

                            @on("tensor")
                            def _(te, kt=kt, cs_=cs_, c0=c0, cw=cw, pidx=pidx, par=par, Q3=Q3, K3=K3):
                                te.wait_ge(q3l[par].h, Q3)
                                te.wait_ge(k3l[par].h, K3)
                                if lm["pstfree"][pidx] is not None:
                                    sm, val = lm["pstfree"][pidx]
                                    te.wait_ge(sm.h, val)
                                inst = te.matmul(
                                    p_st2[pidx][:, 0:cw],
                                    kr3.ap()[:, par * S + 128 * kt: par * S + 128 * (kt + 1)],
                                    qr3.ap()[:, par * S + c0: par * S + c0 + cw],
                                    start=True, stop=True)
                                lm[("mm3", kt, cs_)] = et.inc(inst)
                                last_mm[0] = lm[("mm3", kt, cs_)]

                            @on("scalar")
                            def _(sc, kt=kt, cs_=cs_, c0=c0, cw=cw, pidx=pidx, q0=q0):
                                sc.wait_ge(et.h, lm[("mm3", kt, cs_)])
                                if kt == 0 and cs_ == 0 and lm["av_done"] is not None:
                                    sc.wait_ge(et.h, lm["av_done"])
                                sc.wait_ge(ea.h, ea.v)
                                i1 = sc.copy(
                                    st_pack.ap()[:, POFF[kt] + (c0 - q0): POFF[kt] + (c0 - q0) + cw],
                                    p_st2[pidx][:, 0:cw])
                                st_ev[(kt, cs_)] = (ea, ea.inc(i1))
                                lm["pstfree"][pidx] = st_ev[(kt, cs_)]
                    lm["q3_used"][par] = (et, last_mm[0])
                    lm["k3_used"][par] = (et, last_mm[0])

                    pm = {}

                    for kt in range(NQT):
                        q0 = 128 * kt
                        w = S - q0
                        csl = (kt % 2) * S

                        @on("vector")
                        def _(ve, kt=kt, q0=q0, w=w, csl=csl):
                            if kt == 0:
                                ve.wait_ge(tbl.h, TBLD)
                            for cs_ in range((w + 511) // 512):
                                sm, val = st_ev[(kt, cs_)]
                                ve.wait_ge(sm.h, val)
                            if kt >= 2 and pm.get((kt - 2, "pm")) is not None:
                                ve.wait_ge(eg.h, pm[(kt - 2, "pm")])
                            ve.wait_ge(ev.h, ev.v)
                            ev.inc(ve.tensor_tensor(
                                out=cjunk.ap()[:, csl:csl + w],
                                in0=st_pack.ap()[:, POFF[kt]: POFF[kt] + w],
                                in1=tbrz.ap()[:, q0:S], op=ALU.is_ge))
                            ve.wait_ge(ev.h, ev.v)
                            pm[(kt, "msk")] = ev.inc(ve.tensor_tensor(
                                out=cjunk.ap()[:, csl:csl + 128],
                                in0=cjunk.ap()[:, csl:csl + 128],
                                in1=s_caus01[:], op=ALU.mult))

                        @on("scalar")
                        def _(sc, kt=kt, w=w):
                            sc.wait_ge(ea.h, ea.v)
                            pm[(kt, "exp")] = (ea, ea.inc(sc.activation(
                                out=p_pack.ap()[:, POFF[kt]: POFF[kt] + w],
                                in_=st_pack.ap()[:, POFF[kt]: POFF[kt] + w],
                                func=AF.Exp, scale=ISQ)))

                        @on("gpsimd")
                        def _(gp, kt=kt, w=w, csl=csl):
                            gp.wait_ge(ev.h, pm[(kt, "msk")])
                            sm, val = pm[(kt, "exp")]
                            gp.wait_ge(sm.h, val)
                            gp.wait_ge(eg.h, eg.v)
                            pm[(kt, "pm")] = eg.inc(gp.tensor_tensor(
                                out=st_pack.ap()[:, POFF[kt]: POFF[kt] + w],
                                in0=p_pack.ap()[:, POFF[kt]: POFF[kt] + w],
                                in1=cjunk.ap()[:, csl:csl + w], op=ALU.mult))

                    z_ev = {}
                    z_done = {}
                    for s4 in range(4):
                        @on("tensor")
                        def _(te, s4=s4):
                            for kt in range(4 * s4 + 4):
                                te.wait_ge(eg.h, pm[(kt, "pm")])
                            if s4 > 0:
                                sm, val = z_ev[s4 - 1]
                                te.wait_ge(sm.h, val)
                            nkt = 4 * s4 + 4
                            inst = None
                            for kt in range(nkt):
                                q0 = 128 * kt
                                off = 512 * s4 - q0
                                last = (kt == nkt - 1)
                                if off >= 0:
                                    inst = te.matmul(
                                        p_z[:, 0:512], s_onesc[:],
                                        st_pack.ap()[:, POFF[kt] + off: POFF[kt] + off + 512],
                                        start=(kt == 0), stop=last)
                                else:
                                    wv = 512 * (s4 + 1) - q0
                                    inst = te.matmul(
                                        p_z[:, 512 - wv:512], s_onesc[:],
                                        st_pack.ap()[:, POFF[kt]: POFF[kt] + wv],
                                        start=False, stop=last)
                            z_done[s4] = et.inc(inst)

                        @on("scalar")
                        def _(sc, s4=s4):
                            sc.wait_ge(et.h, z_done[s4])
                            sc.wait_ge(ea.h, ea.v)
                            z_ev[s4] = (ea, ea.inc(sc.activation(
                                out=zrow.ap()[:, 512 * s4:512 * s4 + 512],
                                in_=p_z[:, 0:512], func=AF.Ln)))

                    @on("scalar")
                    def _(sc):
                        sm, val = z_ev[3]
                        sc.wait_ge(sm.h, val)
                        sc.wait_ge(ea.h, ea.v)
                        ea.inc(sc.activation(out=zrow[:], in_=zrow[:],
                                             func=AF.Exp, scale=-1.0))
                    RECIP = ea.v
                    if "dz" in dbg:
                        @on("sync")
                        def _(sy, h=h):
                            sy.wait_ge(ea.h, RECIP)
                            qst.inc(sy.dma_start(out=dbg_out["dz"][h:h + 1], in_=zrow[:]), 16)

                    rz_ev = {}
                    rz_tr = {}
                    for s4 in range(4):
                        @on("tensor")
                        def _(te, s4=s4):
                            te.wait_ge(ea.h, RECIP)
                            if "dz" in dbg:
                                te.wait_ge(qst.h, qst.v)
                            if s4 > 0:
                                sm, val = rz_ev[s4 - 1]
                                te.wait_ge(sm.h, val)
                            rz_tr[s4] = et.inc(te.matmul(
                                p_mis[:], s_onesrf[:],
                                zrow.ap()[:, 512 * s4: 512 * s4 + 512],
                                start=True, stop=True))

                        @on("scalar")
                        def _(sc, s4=s4):
                            sc.wait_ge(et.h, rz_tr[s4])
                            sc.wait_ge(ea.h, ea.v)
                            rz_ev[s4] = (ea, ea.inc(sc.copy(
                                tbrz.ap()[:, 512 * s4: 512 * s4 + 512], p_mis[:])))

                    @on("gpsimd")
                    def _(gp):
                        sm, val = rz_ev[3]
                        gp.wait_ge(sm.h, val)
                        for kt in range(NQT):
                            q0 = 128 * kt
                            w = S - q0
                            gp.wait_ge(eg.h, eg.v)
                            eg.inc(gp.tensor_tensor(
                                out=p_pack.ap()[:, POFF[kt]: POFF[kt] + w],
                                in0=st_pack.ap()[:, POFF[kt]: POFF[kt] + w],
                                in1=tbrz.ap()[:, q0:S], op=ALU.mult))
                    P_NORM = eg.v
                    lm["tb_used"] = P_NORM

                    av_ev = {}
                    av_tr = {}
                    for s4 in range(4):
                        @on("tensor")
                        def _(te, s4=s4, par=par, V3=V3):
                            te.wait_ge(eg.h, P_NORM)
                            te.wait_ge(v3l[par].h, V3)
                            if s4 > 0:
                                sm, val = av_ev[s4 - 1]
                                te.wait_ge(sm.h, val)
                            nkt = 4 * s4 + 4
                            inst = None
                            for kt in range(nkt):
                                q0 = 128 * kt
                                off = 512 * s4 - q0
                                last = (kt == nkt - 1)
                                vsl = v3.ap()[:, par * S + kt * 128: par * S + (kt + 1) * 128]
                                if off >= 0:
                                    inst = te.matmul(
                                        p_av[:, 0:512], vsl,
                                        p_pack.ap()[:, POFF[kt] + off: POFF[kt] + off + 512],
                                        start=(kt == 0), stop=last)
                                else:
                                    wv = 512 * (s4 + 1) - q0
                                    inst = te.matmul(
                                        p_av[:, 512 - wv:512], vsl,
                                        p_pack.ap()[:, POFF[kt]: POFF[kt] + wv],
                                        start=False, stop=last)
                            av_tr[s4] = et.inc(inst)

                        @on("scalar")
                        def _(sc, s4=s4):
                            sc.wait_ge(et.h, av_tr[s4])
                            if s4 == 0:
                                sc.wait_ge(ats.h, ats.v)
                            sc.wait_ge(ea.h, ea.v)
                            av_ev[s4] = (ea, ea.inc(sc.copy(
                                aT_stg.ap()[:, 512 * s4: 512 * s4 + 512], p_av[:])))
                    lm["av_done"] = av_tr[3]
                    lm["v3_used"][par] = (et, av_tr[3])

                    @on("sync")
                    def _(sy, h=h):
                        sm, val = av_ev[3]
                        sy.wait_ge(sm.h, val)
                        ats.inc(sy.dma_start(
                            out=aT_dram[:, :, h, :].rearrange("t p q -> p t q"),
                            in_=aT_stg.ap().rearrange("p (t q) -> p t q", q=128)), 16)

                emit_b1_loads(0)
                sr = emit_b1_scores(0)
                emit_b1_select(0, sr)
                emit_b3_loads(0)
                for h in range(H):
                    if h + 1 < H:
                        emit_b1_loads(h + 1)
                        sr = emit_b1_scores(h + 1)
                        emit_b3_loads(h + 1)
                    emit_b3_compute(h)
                    if h + 1 < H:
                        emit_b1_select(h + 1, sr)

                if "daT" in dbg:
                    @on("sync")
                    def _(sy):
                        sy.wait_ge(ats.h, ats.v)
                        qst.inc(sy.dma_start(out=dbg_out["daT"], in_=aT_dram), 16)

        if "W" in phases:
            with ExitStack() as wctx:
                aTq = sb("aTq", [128, 4 * NHC * 128], BF16, wctx)
                wo_buf = sb("wo_buf", [128, NHC * 512], BF16, wctx)
                a1g = sb("a1g", [128, 4 * D], F32, wctx)
                xw_buf = sb("xw_buf", [128, 2 * 512], F32, wctx)
                y1_stg = sb("y1_stg", [128, 2 * D], BF16, wctx)
                lnv = sb("lnv", [128, 8], F32, wctx)
                ajunk2 = sb("ajunk2", [128, D], BF16, wctx)
                p_w = [psm(f"p_w{i}", [128, 512], F32, wctx) for i in range(2)]
                phase_barrier()

                wol = mksem("wol")
                aql = mksem("aql")
                xwl = [mksem("xwl0"), mksem("xwl1")]
                y1s = [mksem("y1s0"), mksem("y1s1")]
                all_dma_sems.extend([wol, aql] + xwl + y1s)


                wo_mm = {}
                wo_mm_last = [None]
                xw_free = {}
                pw_free = {0: None, 1: None}
                a1_ev = {}
                seqw = [0]
                for qtg in range(4):
                    qts = [4 * qtg + i for i in range(4)]

                    @on("sync")
                    def _(sy, qtg=qtg, qts=qts):
                        if qtg > 0 and wo_mm_last[0] is not None:
                            sy.wait_ge(et.h, wo_mm_last[0])
                        for i, qt in enumerate(qts):
                            aql.inc(sy.dma_start(
                                out=aTq.ap()[:, i * NHC * 128:(i + 1) * NHC * 128]
                                    .rearrange("p (c q) -> p c q", q=128),
                                in_=aT_dram[qt]), 16)
                        sy.wait_ge(aql.h, aql.v)
                    AQ_LD = aql.v

                    for js in range(8):
                        @on("sync")
                        def _(sy, qtg=qtg, js=js):
                            if wo_mm_last[0] is not None and (qtg > 0 or js > 0):
                                sy.wait_ge(et.h, wo_mm_last[0])
                            sy.wait_ge(stg_st[0].h, stg_st[0].v)
                            sy.wait_ge(stg_st[1].h, stg_st[1].v)
                            wol.inc(sy.dma_start(
                                out=wo_buf.ap().rearrange("p (c j) -> p c j", j=512),
                                in_=wo_bf[js]), 16)
                            sy.wait_ge(wol.h, wol.v)
                        WO_LD = wol.v

                        for i, qt in enumerate(qts):
                            pidx = seqw[0] % 2
                            seqw[0] += 1

                            @on("sync")
                            def _(sy, qt=qt, js=js, pidx=pidx):
                                if xw_free.get(pidx) is not None:
                                    sm, val = xw_free[pidx]
                                    sy.wait_ge(sm.h, val)
                                xwl[pidx].inc(sy.dma_start(
                                    out=xw_buf.ap()[:, pidx * 512:(pidx + 1) * 512],
                                    in_=x_nat[qt * 128:(qt + 1) * 128, js * 512:(js + 1) * 512]), 16)

                            @on("tensor")
                            def _(te, i=i, js=js, pidx=pidx):
                                te.wait_ge(wol.h, WO_LD)
                                te.wait_ge(aql.h, AQ_LD)
                                if pw_free[pidx] is not None:
                                    sm, val = pw_free[pidx]
                                    te.wait_ge(sm.h, val)
                                inst = None
                                for dc in range(NHC):
                                    inst = te.matmul(
                                        p_w[pidx][:],
                                        aTq.ap()[:, i * NHC * 128 + dc * 128: i * NHC * 128 + (dc + 1) * 128],
                                        wo_buf.ap()[:, dc * 512:(dc + 1) * 512],
                                        start=(dc == 0), stop=(dc == NHC - 1))
                                wo_mm[(i, js)] = et.inc(inst)
                                wo_mm_last[0] = wo_mm[(i, js)]

                            @on("vector")
                            def _(ve, i=i, js=js, pidx=pidx):
                                ve.wait_ge(et.h, wo_mm[(i, js)])
                                ve.wait_ge(xwl[pidx].h, xwl[pidx].v)
                                ve.wait_ge(ev.h, ev.v)
                                i1 = ve.tensor_tensor(
                                    out=a1g.ap()[:, i * D + js * 512: i * D + (js + 1) * 512],
                                    in0=p_w[pidx][:],
                                    in1=xw_buf.ap()[:, pidx * 512:(pidx + 1) * 512],
                                    op=ALU.add)
                                lm = ev.inc(i1)
                                pw_free[pidx] = (ev, lm)
                                xw_free[pidx] = (ev, lm)
                                a1_ev[(i, js)] = lm

                    for i, qt in enumerate(qts):
                        @on("scalar")
                        def _(sc, i=i):
                            for js in range(8):
                                sc.wait_ge(ev.h, a1_ev[(i, js)])
                            sc.wait_ge(ea.h, ea.v)
                            a1sl = a1g.ap()[:, i * D:(i + 1) * D]
                            ea.inc(sc.activation(out=ajunk2[:], in_=a1sl,
                                                 func=AF.Identity, accum_out=lnv[:, 0:1]))
                            sc.wait_ge(ea.h, ea.v)
                            ea.inc(sc.activation(out=ajunk2[:], in_=a1sl,
                                                 func=AF.Square, accum_out=lnv[:, 1:2]))
                        STATS = ea.v

                        @on("vector")
                        def _(ve):
                            ve.wait_ge(ea.h, STATS)
                            ve.wait_ge(ev.h, ev.v)
                            ev.inc(ve.tensor_scalar(out=lnv[:, 2:3], in0=lnv[:, 0:1],
                                                    scalar1=1.0 / D, scalar2=None, op0=ALU.mult))
                            ve.wait_ge(ev.h, ev.v)
                            ev.inc(ve.tensor_scalar(out=lnv[:, 3:4], in0=lnv[:, 1:2],
                                                    scalar1=1.0 / D, scalar2=None, op0=ALU.mult))
                            ve.wait_ge(ev.h, ev.v)
                            ev.inc(ve.tensor_tensor(out=lnv[:, 4:5], in0=lnv[:, 2:3],
                                                    in1=lnv[:, 2:3], op=ALU.mult))
                            ve.wait_ge(ev.h, ev.v)
                            ev.inc(ve.tensor_tensor(out=lnv[:, 5:6], in0=lnv[:, 3:4],
                                                    in1=lnv[:, 4:5], op=ALU.subtract))
                            ve.wait_ge(ev.h, ev.v)
                            ev.inc(ve.tensor_scalar(out=lnv[:, 5:6], in0=lnv[:, 5:6],
                                                    scalar1=EPS, scalar2=None, op0=ALU.add))
                            ve.wait_ge(ev.h, ev.v)
                            ev.inc(ve.reciprocal(lnv[:, 6:7], lnv[:, 5:6]))
                        RECV = ev.v

                        @on("scalar")
                        def _(sc):
                            sc.wait_ge(ev.h, RECV)
                            sc.wait_ge(ea.h, ea.v)
                            ea.inc(sc.activation(out=lnv[:, 6:7], in_=lnv[:, 6:7], func=AF.Sqrt))
                        RSTD = ea.v

                        @on("vector")
                        def _(ve):
                            ve.wait_ge(ea.h, RSTD)
                            ve.wait_ge(ev.h, ev.v)
                            ev.inc(ve.tensor_tensor(out=lnv[:, 7:8], in0=lnv[:, 2:3],
                                                    in1=lnv[:, 6:7], op=ALU.mult))
                            ve.wait_ge(ev.h, ev.v)
                            ev.inc(ve.tensor_scalar(out=lnv[:, 7:8], in0=lnv[:, 7:8],
                                                    scalar1=-1.0, scalar2=None, op0=ALU.mult))
                        NBIAS = ev.v

                        @on("scalar")
                        def _(sc, i=i, qt=qt):
                            sc.wait_ge(ev.h, NBIAS)
                            pidx = qt % 2
                            sc.wait_ge(y1s[pidx].h, y1s[pidx].v)
                            sc.wait_ge(ea.h, ea.v)
                            ea.inc(sc.activation(
                                out=y1_stg.ap()[:, pidx * D:(pidx + 1) * D],
                                in_=a1g.ap()[:, i * D:(i + 1) * D],
                                func=AF.Identity, bias=lnv[:, 7:8], scale=lnv[:, 6:7]))
                        Y1A = ea.v

                        @on("sync")
                        def _(sy, qt=qt):
                            sy.wait_ge(ea.h, Y1A)
                            pidx = qt % 2
                            y1s[pidx].inc(sy.dma_start(
                                out=y1_dram[qt * 128:(qt + 1) * 128, :],
                                in_=y1_stg.ap()[:, pidx * D:(pidx + 1) * D]), 16)

                if "dy1" in dbg:
                    @on("sync")
                    def _(sy):
                        for sm in (y1s[0], y1s[1]):
                            sy.wait_ge(sm.h, sm.v)
                        qst.inc(sy.dma_start(out=dbg_out["dy1"], in_=y1_dram), 16)

        if "C" in phases:
            zsg_dram = dri("zsg_dram", [NG_HID, INTER], BF16)
            zsu_dram = dri("zsu_dram", [NG_HID, INTER], BF16)
            zsd_dram = dri("zsd_dram", [NG_INT, D], BF16)
            phase_barrier()
            qwl = [mksem(f"qwl{i}") for i in range(4)]
            srl = mksem("srl")
            w4s_ = [mksem("w4s0"), mksem("w4s1")]
            zql = mksem("zql")
            zst = mksem("zst")
            all_dma_sems.extend(qwl + [srl] + w4s_ + [zql, zst])

            with ExitStack() as c0z:
                z4b = sb("z4b", [NG_INT, INTER], I32, c0z)
                qzb = sb("qzb", [NG_INT, INTER // 8], I32, c0z)
                scb = sb("scb", [NG_INT, INTER], F32, c0z)
                zout = sb("zout", [NG_INT, INTER], BF16, c0z)
                for qz_i, sc_i, zdst, ng, nn in [
                    (g_qz, g_sc, zsg_dram, NG_HID, INTER),
                    (u_qz, u_sc, zsu_dram, NG_HID, INTER),
                    (d_qz, d_sc, zsd_dram, NG_INT, D),
                ]:
                    @on("sync")
                    def _(sy, qz_i=qz_i, sc_i=sc_i, ng=ng, nn=nn):
                        sy.wait_ge(zst.h, zst.v)
                        sy.wait_ge(ev.h, ev.v)
                        zql.inc(sy.dma_start(out=qzb[0:ng, 0:nn // 8], in_=qz_i), 16)
                        zql.inc(sy.dma_start(out=scb[0:ng, 0:nn], in_=sc_i), 16)
                        sy.wait_ge(zql.h, zql.v)
                    ZQ_LD = zql.v

                    @on("vector")
                    def _(ve, ng=ng, nn=nn):
                        ve.wait_ge(zql.h, ZQ_LD)
                        for j in range(8):
                            ve.wait_ge(ev.h, ev.v)
                            ev.inc(ve.tensor_scalar(
                                out=z4b.ap()[0:ng, 0:nn].rearrange("p (n e) -> p n e", e=8)[:, :, j],
                                in0=qzb[0:ng, 0:nn // 8], scalar1=4 * j, scalar2=15,
                                op0=ALU.logical_shift_right, op1=ALU.bitwise_and))
                        ve.wait_ge(ev.h, ev.v)
                        ev.inc(ve.tensor_scalar(
                            out=scb[0:ng, 0:nn], in0=scb[0:ng, 0:nn],
                            scalar1=-1.0, scalar2=None, op0=ALU.mult))
                        ve.wait_ge(ev.h, ev.v)
                        ev.inc(ve.tensor_scalar(
                            out=z4b[0:ng, 0:nn], in0=z4b[0:ng, 0:nn],
                            scalar1=1, scalar2=None, op0=ALU.add))
                        ve.wait_ge(ev.h, ev.v)
                        ev.inc(ve.tensor_tensor(
                            out=zout[0:ng, 0:nn], in0=z4b[0:ng, 0:nn],
                            in1=scb[0:ng, 0:nn], op=ALU.mult))
                    Z_EV = ev.v

                    @on("sync")
                    def _(sy, zdst=zdst, ng=ng, nn=nn):
                        sy.wait_ge(ev.h, Z_EV)
                        zst.inc(sy.dma_start(out=zdst, in_=zout[0:ng, 0:nn]), 16)

            phase_barrier()
            with ExitStack() as c0w:
                qwr = sb("qwr", [128, 2 * 5504], I32, c0w)
                nibb = sb("nibb", [128, 5504], I32, c0w)
                srep = sb("srep", [128, 2 * INTER], BF16, c0w)
                srow = sb("srow", [1, 2 * INTER], BF16, c0w)
                onesr_b = sb("onesr_b", [1, 128], BF16, c0w)
                w4o = sb("w4o", [128, 2 * 5504], BF16, c0w)
                p_sr = [psm(f"p_sr{i}", [128, 512], F32, c0w) for i in range(2)]
                qw_free = {}
                srow_free = {}
                srep_free = {}
                psr_free = {}
                sr_bcast = {}
                sr_done = {}
                seqc = [0]
                seqk = [0]

                @on("vector")
                def _(ve):
                    ve.wait_ge(ev.h, ev.v)
                    ev.inc(ve.memset(onesr_b[:], 1.0))
                ONES_B = ev.v

                def dq_matrix(qw_i, sc_i, wdst, nkc, ncols, gmap, tiled):
                    halves = [(0, 5504), (5504, 5504)] if ncols == INTER else [(0, ncols)]
                    for kc in range(nkc):
                        kpar = seqk[0] % 2
                        seqk[0] += 1

                        @on("sync")
                        def _(sy, kc=kc, sc_i=sc_i, ncols=ncols, gmap=gmap, kpar=kpar):
                            if kc == 0:
                                sy.wait_ge(stg_st[0].h, stg_st[0].v)
                                sy.wait_ge(stg_st[1].h, stg_st[1].v)
                            if srow_free.get(kpar) is not None:
                                sm, val = srow_free[kpar]
                                sy.wait_ge(sm.h, val)
                            srl.inc(sy.dma_start(
                                out=srow[0:1, kpar * INTER: kpar * INTER + ncols],
                                in_=sc_i[gmap[kc]:gmap[kc] + 1, :]), 16)
                        SR_LD = srl.v

                        nch = (ncols + 511) // 512
                        for ch in range(nch):
                            c0_ = 512 * ch
                            cw_ = min(512, ncols - c0_)

                            @on("tensor")
                            def _(te, kpar=kpar, c0_=c0_, cw_=cw_, ch=ch, SR_LD=SR_LD):
                                te.wait_ge(srl.h, SR_LD)
                                if ch == 0:
                                    te.wait_ge(qld.h, CONST_LD)
                                    te.wait_ge(ev.h, ONES_B)
                                pp = ch % 2
                                if psr_free.get(pp) is not None:
                                    sm, val = psr_free[pp]
                                    te.wait_ge(sm.h, val)
                                inst = te.matmul(
                                    p_sr[pp][:, 0:cw_], onesr_b[:],
                                    srow[0:1, kpar * INTER + c0_: kpar * INTER + c0_ + cw_],
                                    start=True, stop=True)
                                sr_bcast[(kpar, ch)] = et.inc(inst)

                            @on("vector")
                            def _(ve, kpar=kpar, c0_=c0_, cw_=cw_, ch=ch, nch=nch):
                                ve.wait_ge(et.h, sr_bcast[(kpar, ch)])
                                if ch == 0 and srep_free.get(kpar) is not None:
                                    sm, val = srep_free[kpar]
                                    ve.wait_ge(sm.h, val)
                                ve.wait_ge(ev.h, ev.v)
                                i1 = ve.tensor_scalar(
                                    out=srep[:, kpar * INTER + c0_: kpar * INTER + c0_ + cw_],
                                    in0=p_sr[ch % 2][:, 0:cw_],
                                    scalar1=s_pw2[:], scalar2=None, op0=ALU.mult)
                                v = ev.inc(i1)
                                psr_free[ch % 2] = (ev, v)
                                if ch == nch - 1:
                                    sr_done[kpar] = v
                                    srow_free[kpar] = (ev, v)
                        SR_ADJ = sr_done[kpar]

                        for off, hw in halves:
                            pidx = seqc[0] % 2
                            vp = seqc[0] % 2
                            seqc[0] += 1

                            @on("sync" if pidx % 2 == 0 else "scalar")
                            def _(eng, kc=kc, off=off, hw=hw, pidx=pidx, qw_i=qw_i):
                                if qw_free.get(pidx) is not None:
                                    sm, val = qw_free[pidx]
                                    eng.wait_ge(sm.h, val)
                                qwl[pidx].inc(eng.dma_start(
                                    out=qwr.ap()[:, pidx * 5504: pidx * 5504 + hw],
                                    in_=qw_i[16 * kc:16 * kc + 16, off:off + hw]
                                        .unsqueeze(1).broadcast_to([16, 8, hw])), 16)

                            @on("vector")
                            def _(ve, off=off, hw=hw, pidx=pidx, vp=vp, kpar=kpar, SR_ADJ=SR_ADJ):
                                ve.wait_ge(qwl[pidx].h, qwl[pidx].v)
                                ve.wait_ge(ev.h, SR_ADJ)
                                ve.wait_ge(w4s_[vp].h, w4s_[vp].v)
                                ve.wait_ge(ev.h, ev.v)
                                ev.inc(ve.tensor_scalar(
                                    out=nibb.ap()[:, 0:hw],
                                    in0=qwr.ap()[:, pidx * 5504: pidx * 5504 + hw],
                                    scalar1=s_maskv[:], scalar2=None,
                                    op0=ALU.bitwise_and))
                                qw_free[pidx] = (ev, ev.v)
                                ve.wait_ge(ev.h, ev.v)
                                i2 = ve.tensor_tensor(
                                    out=w4o.ap()[:, vp * 5504: vp * 5504 + hw],
                                    in0=nibb.ap()[:, 0:hw],
                                    in1=srep[:, kpar * INTER + off: kpar * INTER + off + hw],
                                    op=ALU.mult)
                                srep_free[kpar] = (ev, ev.inc(i2))

                            @on("sync")
                            def _(sy, kc=kc, off=off, hw=hw, vp=vp, kpar=kpar, wdst=wdst, tiled=tiled):
                                sm, val = srep_free[kpar]
                                sy.wait_ge(sm.h, val)
                                if tiled:
                                    ic0 = off // 128
                                    nic = hw // 128
                                    w4s_[vp].inc(sy.dma_start(
                                        out=wdst[ic0:ic0 + nic, :, kc, :]
                                            .rearrange("a p q -> p a q"),
                                        in_=w4o.ap()[:, vp * 5504: vp * 5504 + hw]
                                            .rearrange("p (a q) -> p a q", q=128)), 16)
                                else:
                                    w4s_[vp].inc(sy.dma_start(
                                        out=wdst[128 * kc:128 * kc + 128, :],
                                        in_=w4o.ap()[:, vp * 5504: vp * 5504 + hw]), 16)

                dq_matrix(g_qw, gsc_bf, w4g_dram, NHC, INTER, gmap_g, True)
                dq_matrix(u_qw, usc_bf, w4u_dram, NHC, INTER, gmap_u, True)
                dq_matrix(d_qw, dsc_bf, w4d_dram, NG_INT, D, gmap_d, False)

            if "dw4g" in dbg:
                @on("sync")
                def _(sy):
                    for sm in w4s_:
                        sy.wait_ge(sm.h, sm.v)
                    sy.wait_ge(zst.h, zst.v)
                    qst.inc(sy.dma_start(out=dbg_out["dw4g"], in_=w4g_dram), 16)
                    qst.inc(sy.dma_start(out=dbg_out["dw4d"], in_=w4d_dram), 16)
                    qst.inc(sy.dma_start(out=dbg_out["dzsg"], in_=zsg_dram), 16)
                    qst.inc(sy.dma_start(out=dbg_out["dzsd"], in_=zsd_dram), 16)

            with ExitStack() as cctx:
                phase_barrier()
                y1tl = mksem("y1tl")
                wgl = [mksem("wgl0"), mksem("wgl1")]
                wul = [mksem("wul0"), mksem("wul1")]
                wdl = [mksem(f"wdl{i}") for i in range(4)]
                y1l = [mksem("y1l0"), mksem("y1l1")]
                zgl = [mksem("zgl0"), mksem("zgl1")]
                zdl = mksem("zdl")
                indl = mksem("indl")
                y2s = [mksem("y2s0"), mksem("y2s1")]
                all_dma_sems.extend([y1tl] + wgl + wul + wdl + y1l + zgl + [zdl, indl] + y2s)

                s_indg = sb("s_indg", [128, NHC * NG_HID], BF16, cctx)
                s_indd = sb("s_indd", [128, NIC * NG_INT], BF16, cctx)
                wg_buf = sb("wg_buf", [128, 2 * D], BF16, cctx)
                wu_buf = sb("wu_buf", [128, 2 * D], BF16, cctx)
                wd_buf = sb("wd_buf", [128, 4 * 512], BF16, cctx)
                h_buf = sb("h_buf", [128, NIC * 512], BF16, cctx)
                sg_buf = sb("sg_buf", [128, 512], BF16, cctx)
                gt_s = sb("gt_s", [NG_HID, 512], BF16, cctx)
                gd_s = sb("gd_s", [NG_INT, 512], BF16, cctx)
                zs_buf = sb("zs_buf", [NG_HID, 2 * 2 * 128], BF16, cctx)
                zsd_buf = sb("zsd_buf", [NG_INT, 2 * 512], BF16, cctx)
                y1r_buf = sb("y1r_buf", [128, 2 * 512], BF16, cctx)
                lnv2 = sb("lnv2", [128, 8], F32, cctx)
                p_cg = psm("p_cg", [128, 512], F32, cctx)
                p_cu = psm("p_cu", [128, 512], F32, cctx)
                p_gt = psm("p_gt", [NG_HID, 512], F32, cctx)
                p_gd = psm("p_gd", [NG_INT, 512], F32, cctx)
                p_d4 = [psm(f"p_d{i}", [128, 512], F32, cctx) for i in range(4)]

                @on("sync")
                def _(sy):
                    indl.inc(sy.dma_start(
                        out=s_indg.ap().rearrange("p (c g) -> p c g", g=NG_HID),
                        in_=indg_i.rearrange("(c p) g -> p c g", p=128)), 16)
                    indl.inc(sy.dma_start(
                        out=s_indd.ap().rearrange("p (c g) -> p c g", g=NG_INT),
                        in_=indd_i.rearrange("(c p) g -> p c g", p=128)), 16)
                    sy.wait_ge(indl.h, indl.v)
                IND_LD = indl.v

                c3_done = [None]
                gt_mm = [None]; gt_ev = [None]; gt_free = [None]
                cg_mm = [None]; cu_mm = [None]; sil_ea = [None]
                pcg_free = [None]; pcu_free = [None]
                gd_mm = [None]; gd_ev = [None]
                wd_mm = [None]; dcor_mm = [None]
                wfree = {}; h_done = {}; wdfree = {}; pd_free = {}; y1rfree = {}
                zs_free = {}; zsd_free = {}

                for rs in range(4):
                    a2_ev = {}
                    with ExitStack() as rsctx:
                        c2ctx = ExitStack()
                        y1t_buf = c2ctx.enter_context(nc.sbuf_tensor(f"y1t_{rs}", [128, NHC * 512], BF16))

                        @on("sync")
                        def _(sy, rs=rs):
                            if c3_done[0] is not None:
                                sy.wait_ge(et.h, c3_done[0])
                            for dc in range(NHC):
                                y1tl.inc(sy.dma_start(
                                    out=y1t_buf.ap()[:, dc * 512:(dc + 1) * 512],
                                    in_=y1_dram[rs * 512:(rs + 1) * 512, dc * 128:(dc + 1) * 128],
                                    transpose=True), 16)
                            sy.wait_ge(y1tl.h, y1tl.v)
                        Y1T_LD = y1tl.v

                        @on("tensor")
                        def _(te, rs=rs):
                            te.wait_ge(y1tl.h, Y1T_LD)
                            te.wait_ge(indl.h, IND_LD)
                            if gt_free[0] is not None:
                                sm, val = gt_free[0]
                                te.wait_ge(sm.h, val)
                            inst = None
                            for dc in range(NHC):
                                inst = te.matmul(
                                    p_gt[:], s_indg.ap()[:, dc * NG_HID:(dc + 1) * NG_HID],
                                    y1t_buf.ap()[:, dc * 512:(dc + 1) * 512],
                                    start=(dc == 0), stop=(dc == NHC - 1))
                            gt_mm[0] = et.inc(inst)

                        @on("vector")
                        def _(ve):
                            ve.wait_ge(et.h, gt_mm[0])
                            ve.wait_ge(ev.h, ev.v)
                            gt_ev[0] = ev.inc(ve.tensor_copy(gt_s[:], p_gt[:]))
                        if "dgt" in dbg and rs == 0:
                            @on("sync")
                            def _(sy):
                                sy.wait_ge(ev.h, gt_ev[0])
                                qst.inc(sy.dma_start(out=dbg_out["dgt"], in_=gt_s[:]), 16)

                        for ic in range(NIC):
                            pidx = ic % 2
                            for wl, wbuf, wdr, which in [(wgl, wg_buf, w4g_dram, "g"),
                                                         (wul, wu_buf, w4u_dram, "u")]:
                                @on("sync")
                                def _(sy, ic=ic, pidx=pidx, wl=wl, wbuf=wbuf, wdr=wdr, which=which):
                                    if wfree.get((which, pidx)) is not None:
                                        sm, val = wfree[(which, pidx)]
                                        sy.wait_ge(sm.h, val)
                                    wl[pidx].inc(sy.dma_start(
                                        out=wbuf.ap()[:, pidx * D:(pidx + 1) * D]
                                            .rearrange("p (c q) -> p c q", q=128),
                                        in_=wdr[ic]), 16)

                            @on("sync")
                            def _(sy, ic=ic, pidx=pidx):
                                if zs_free.get(pidx) is not None:
                                    sm, val = zs_free[pidx]
                                    sy.wait_ge(sm.h, val)
                                zgl[pidx].inc(sy.dma_start(
                                    out=zs_buf[0:NG_HID, pidx * 256: pidx * 256 + 128],
                                    in_=zsg_dram[:, ic * 128:(ic + 1) * 128]), 16)
                                zgl[pidx].inc(sy.dma_start(
                                    out=zs_buf[0:NG_HID, pidx * 256 + 128: pidx * 256 + 256],
                                    in_=zsu_dram[:, ic * 128:(ic + 1) * 128]), 16)

                            @on("tensor")
                            def _(te, rs=rs, ic=ic, pidx=pidx):
                                te.wait_ge(wgl[pidx].h, wgl[pidx].v)
                                te.wait_ge(wul[pidx].h, wul[pidx].v)
                                te.wait_ge(zgl[pidx].h, zgl[pidx].v)
                                te.wait_ge(ev.h, gt_ev[0])
                                if pcg_free[0] is not None:
                                    sm, val = pcg_free[0]
                                    te.wait_ge(sm.h, val)
                                inst = None
                                for dc in range(NHC):
                                    inst = te.matmul(
                                        p_cg[:], wg_buf.ap()[:, pidx * D + dc * 128: pidx * D + (dc + 1) * 128],
                                        y1t_buf.ap()[:, dc * 512:(dc + 1) * 512],
                                        start=(dc == 0), stop=False)
                                inst = te.matmul(
                                    p_cg[:], zs_buf[0:NG_HID, pidx * 256: pidx * 256 + 128],
                                    gt_s[:], start=False, stop=True)
                                cg_mm[0] = et.inc(inst)
                                if pcu_free[0] is not None:
                                    sm, val = pcu_free[0]
                                    te.wait_ge(sm.h, val)
                                inst = None
                                for dc in range(NHC):
                                    inst = te.matmul(
                                        p_cu[:], wu_buf.ap()[:, pidx * D + dc * 128: pidx * D + (dc + 1) * 128],
                                        y1t_buf.ap()[:, dc * 512:(dc + 1) * 512],
                                        start=(dc == 0), stop=False)
                                inst = te.matmul(
                                    p_cu[:], zs_buf[0:NG_HID, pidx * 256 + 128: pidx * 256 + 256],
                                    gt_s[:], start=False, stop=True)
                                cu_mm[0] = et.inc(inst)
                                wfree[("g", pidx)] = (et, cg_mm[0])
                                wfree[("u", pidx)] = (et, cu_mm[0])
                                zs_free[pidx] = (et, cu_mm[0])

                            @on("scalar")
                            def _(sc, ic=ic):
                                sc.wait_ge(et.h, cg_mm[0])
                                sc.wait_ge(ea.h, ea.v)
                                sil_ea[0] = ea.inc(sc.activation(out=sg_buf[:], in_=p_cg[:], func=AF.Silu))
                                pcg_free[0] = (ea, sil_ea[0])

                            @on("vector")
                            def _(ve, ic=ic):
                                ve.wait_ge(ea.h, sil_ea[0])
                                ve.wait_ge(et.h, cu_mm[0])
                                ve.wait_ge(ev.h, ev.v)
                                i1 = ve.tensor_tensor(
                                    out=h_buf.ap()[:, ic * 512:(ic + 1) * 512],
                                    in0=sg_buf[:], in1=p_cu[:], op=ALU.mult)
                                hev = ev.inc(i1)
                                pcu_free[0] = (ev, hev)
                                h_done[ic] = hev

                            @on("tensor")
                            def _(te, ic=ic):
                                te.wait_ge(ev.h, h_done[ic])
                                inst = te.matmul(
                                    p_gd[:], s_indd.ap()[:, ic * NG_INT:(ic + 1) * NG_INT],
                                    h_buf.ap()[:, ic * 512:(ic + 1) * 512],
                                    start=(ic == 0), stop=(ic == NIC - 1))
                                gd_mm[0] = et.inc(inst)

                        @on("vector")
                        def _(ve):
                            ve.wait_ge(et.h, gd_mm[0])
                            ve.wait_ge(ev.h, ev.v)
                            gd_ev[0] = ev.inc(ve.tensor_copy(gd_s[:], p_gd[:]))
                            gt_free[0] = (ev, gd_ev[0])

                        if "dh" in dbg and rs == 0:
                            @on("sync")
                            def _(sy):
                                sy.wait_ge(ev.h, ev.v)
                                qst.inc(sy.dma_start(
                                    out=dbg_out["dh"],
                                    in_=h_buf.ap().rearrange("p (a q) -> p a q", q=512)), 16)

                        c2ctx.close()
                        a2g = sb(f"a2g_{rs}", [128, 4 * D], BF16, rsctx)
                        y2_stg = sb(f"y2stg_{rs}", [128, D], F32, rsctx)
                        for js in range(8):
                            @on("sync")
                            def _(sy, js=js):
                                pz = js % 2
                                if zsd_free.get(pz) is not None:
                                    sm, val = zsd_free[pz]
                                    sy.wait_ge(sm.h, val)
                                zdl.inc(sy.dma_start(
                                    out=zsd_buf[0:NG_INT, (js % 2) * 512:(js % 2) * 512 + 512],
                                    in_=zsd_dram[:, js * 512:(js + 1) * 512]), 16)
                            ZD_LD = zdl.v

                            for ic in range(NIC):
                                pidx = ic % 4

                                @on("sync")
                                def _(sy, rs=rs, js=js, ic=ic, pidx=pidx):
                                    if wdfree.get(pidx) is not None:
                                        sm, val = wdfree[pidx]
                                        sy.wait_ge(sm.h, val)
                                    wdl[pidx].inc(sy.dma_start(
                                        out=wd_buf.ap()[:, pidx * 512:(pidx + 1) * 512],
                                        in_=w4d_dram[128 * ic:128 * ic + 128, 512 * js:512 * js + 512]), 16)

                                @on("tensor")
                                def _(te, js=js, ic=ic, pidx=pidx):
                                    te.wait_ge(wdl[pidx].h, wdl[pidx].v)
                                    if ic == 0:
                                        te.wait_ge(ev.h, gd_ev[0])
                                        for rt in range(4):
                                            if pd_free.get(rt) is not None:
                                                sm, val = pd_free[rt]
                                                te.wait_ge(sm.h, val)
                                    inst = None
                                    for rt in range(4):
                                        inst = te.matmul(
                                            p_d4[rt][:],
                                            h_buf.ap()[:, ic * 512 + rt * 128: ic * 512 + (rt + 1) * 128],
                                            wd_buf.ap()[:, pidx * 512:(pidx + 1) * 512],
                                            start=(ic == 0), stop=False)
                                    wd_mm[0] = et.inc(inst)
                                    wdfree[pidx] = (et, wd_mm[0])

                            @on("tensor")
                            def _(te, js=js, ZD_LD=ZD_LD):
                                te.wait_ge(zdl.h, ZD_LD)
                                inst = None
                                for rt in range(4):
                                    inst = te.matmul(
                                        p_d4[rt][:], gd_s[0:NG_INT, rt * 128:(rt + 1) * 128],
                                        zsd_buf[0:NG_INT, (js % 2) * 512:(js % 2) * 512 + 512],
                                        start=False, stop=True)
                                dcor_mm[0] = et.inc(inst)
                                c3_done[0] = dcor_mm[0]
                                zsd_free[js % 2] = (et, dcor_mm[0])

                            for rt in range(4):
                                pidx = (js * 4 + rt) % 2

                                @on("sync")
                                def _(sy, rs=rs, js=js, rt=rt, pidx=pidx):
                                    if y1rfree.get(pidx) is not None:
                                        sm, val = y1rfree[pidx]
                                        sy.wait_ge(sm.h, val)
                                    y1l[pidx].inc(sy.dma_start(
                                        out=y1r_buf.ap()[:, pidx * 512:(pidx + 1) * 512],
                                        in_=y1_dram[rs * 512 + rt * 128: rs * 512 + (rt + 1) * 128,
                                                    512 * js:512 * js + 512]), 16)

                                @on("vector")
                                def _(ve, js=js, rt=rt, pidx=pidx):
                                    ve.wait_ge(et.h, dcor_mm[0])
                                    ve.wait_ge(y1l[pidx].h, y1l[pidx].v)
                                    ve.wait_ge(ev.h, ev.v)
                                    i1 = ve.tensor_tensor(
                                        out=a2g.ap()[:, rt * D + 512 * js: rt * D + 512 * js + 512],
                                        in0=p_d4[rt][:],
                                        in1=y1r_buf.ap()[:, pidx * 512:(pidx + 1) * 512],
                                        op=ALU.add)
                                    lm = ev.inc(i1)
                                    pd_free[rt] = (ev, lm)
                                    y1rfree[pidx] = (ev, lm)
                                    a2_ev[(js, rt)] = lm

                        for rt in range(4):
                            @on("scalar")
                            def _(sc, rt=rt):
                                for js in range(8):
                                    sc.wait_ge(ev.h, a2_ev[(js, rt)])
                                sc.wait_ge(y2s[0].h, y2s[0].v)
                                sc.wait_ge(ea.h, ea.v)
                                a2sl = a2g.ap()[:, rt * D:(rt + 1) * D]
                                ea.inc(sc.activation(out=y2_stg[:], in_=a2sl,
                                                     func=AF.Identity, accum_out=lnv2[:, 0:1]))
                                sc.wait_ge(ea.h, ea.v)
                                ea.inc(sc.activation(out=y2_stg[:], in_=a2sl,
                                                     func=AF.Square, accum_out=lnv2[:, 1:2]))
                            STATS2 = ea.v

                            @on("vector")
                            def _(ve):
                                ve.wait_ge(ea.h, STATS2)
                                ve.wait_ge(ev.h, ev.v)
                                ev.inc(ve.tensor_scalar(out=lnv2[:, 2:3], in0=lnv2[:, 0:1], scalar1=1.0 / D, scalar2=None, op0=ALU.mult))
                                ve.wait_ge(ev.h, ev.v)
                                ev.inc(ve.tensor_scalar(out=lnv2[:, 3:4], in0=lnv2[:, 1:2], scalar1=1.0 / D, scalar2=None, op0=ALU.mult))
                                ve.wait_ge(ev.h, ev.v)
                                ev.inc(ve.tensor_tensor(out=lnv2[:, 4:5], in0=lnv2[:, 2:3], in1=lnv2[:, 2:3], op=ALU.mult))
                                ve.wait_ge(ev.h, ev.v)
                                ev.inc(ve.tensor_tensor(out=lnv2[:, 5:6], in0=lnv2[:, 3:4], in1=lnv2[:, 4:5], op=ALU.subtract))
                                ve.wait_ge(ev.h, ev.v)
                                ev.inc(ve.tensor_scalar(out=lnv2[:, 5:6], in0=lnv2[:, 5:6], scalar1=EPS, scalar2=None, op0=ALU.add))
                                ve.wait_ge(ev.h, ev.v)
                                ev.inc(ve.reciprocal(lnv2[:, 6:7], lnv2[:, 5:6]))
                            RECV2 = ev.v

                            @on("scalar")
                            def _(sc):
                                sc.wait_ge(ev.h, RECV2)
                                sc.wait_ge(ea.h, ea.v)
                                ea.inc(sc.activation(out=lnv2[:, 6:7], in_=lnv2[:, 6:7], func=AF.Sqrt))
                            RSTD2 = ea.v

                            @on("vector")
                            def _(ve):
                                ve.wait_ge(ea.h, RSTD2)
                                ve.wait_ge(ev.h, ev.v)
                                ev.inc(ve.tensor_tensor(out=lnv2[:, 7:8], in0=lnv2[:, 2:3], in1=lnv2[:, 6:7], op=ALU.mult))
                                ve.wait_ge(ev.h, ev.v)
                                ev.inc(ve.tensor_scalar(out=lnv2[:, 7:8], in0=lnv2[:, 7:8], scalar1=-1.0, scalar2=None, op0=ALU.mult))
                            NB2 = ev.v

                            @on("scalar")
                            def _(sc, rt=rt):
                                sc.wait_ge(ev.h, NB2)
                                sc.wait_ge(y2s[0].h, y2s[0].v)
                                sc.wait_ge(ea.h, ea.v)
                                ea.inc(sc.activation(
                                    out=y2_stg[:],
                                    in_=a2g.ap()[:, rt * D:(rt + 1) * D],
                                    func=AF.Identity, bias=lnv2[:, 7:8], scale=lnv2[:, 6:7]))
                            Y2A = ea.v

                            @on("sync")
                            def _(sy, rs=rs, rt=rt):
                                sy.wait_ge(ea.h, Y2A)
                                y2s[0].inc(sy.dma_start(
                                    out=out[rs * 512 + rt * 128: rs * 512 + (rt + 1) * 128, :],
                                    in_=y2_stg[:]), 16)
                        @on("sync")
                        def _(sy):
                            sy.wait_ge(y2s[0].h, y2s[0].v)

        @on("sync")
        def _(sy):
            for sm in [qld] + all_dma_sems:
                if sm.v > 0:
                    sy.wait_ge(sm.h, sm.v)
            if qst.v > 0:
                sy.wait_ge(qst.h, qst.v)

    return nc


def kernel(**inputs):
    hs = np.ascontiguousarray(inputs["L0_attn_hidden_states"])
    Wq = inputs["L0_attn_Wq"]; Wk = inputs["L0_attn_Wk"]
    Wv = inputs["L0_attn_Wv"]; Wo = inputs["L0_attn_Wo"]
    gi_g = np.asarray(inputs["L0_ffn_gate_g_idx"])
    gi_u = np.asarray(inputs["L0_ffn_up_g_idx"])
    gi_d = np.asarray(inputs["L0_ffn_down_g_idx"])

    def tile_groups(gi):
        g2 = gi.reshape(-1, 128)
        assert (g2 == g2[:, :1]).all(), "g_idx not tile-constant"
        return g2[:, 0].astype(int)

    gmap_g, gmap_u, gmap_d = tile_groups(gi_g), tile_groups(gi_u), tile_groups(gi_d)
    t = host_tables()
    nc = build_program(gmap_g, gmap_u, gmap_d)

    base = make_base_inputs(inputs, t, gi_g, gi_d)
    in_maps = []
    for b in range(B):
        xb = np.ascontiguousarray(hs[b])
        m = dict(base)
        m["x_nat"] = xb
        m["x_t"] = np.ascontiguousarray(xb.T)
        in_maps.append(m)

    res = run_bass_kernel_spmd(nc, in_maps, core_ids=list(range(B)))
    return np.stack([np.asarray(r["out"], dtype=np.float32) for r in res.results])


def make_base_inputs(inputs, t, gi_g, gi_d):
    Wq = inputs["L0_attn_Wq"]; Wk = inputs["L0_attn_Wk"]
    Wv = inputs["L0_attn_Wv"]; Wo = inputs["L0_attn_Wo"]
    wqkvT = np.concatenate([Wq.T, Wk.T, Wv.T], axis=1)
    wqkv_t = np.ascontiguousarray(
        wqkvT.reshape(NHC, 128, 48, 128).transpose(2, 1, 0, 3))
    wo_t = np.ascontiguousarray(
        Wo.T.reshape(NHC, 128, 8, 512).transpose(2, 1, 0, 3))
    return {
        "wqkv_t": wqkv_t, "wo_t": wo_t,
        "g_qw": inputs["L0_ffn_gate_qweight"], "g_sc": inputs["L0_ffn_gate_scales"],
        "g_qz": inputs["L0_ffn_gate_qzeros"],
        "u_qw": inputs["L0_ffn_up_qweight"], "u_sc": inputs["L0_ffn_up_scales"],
        "u_qz": inputs["L0_ffn_up_qzeros"],
        "d_qw": inputs["L0_ffn_down_qweight"], "d_sc": inputs["L0_ffn_down_scales"],
        "d_qz": inputs["L0_ffn_down_qzeros"],
        "cosT": t["cosT"], "sinF": t["sinF"], "diagm": t["diagm"], "diagmT": t["diagmT"],
        "caus01": t["caus01"],
        "z0c": t["z0"], "sscale": t["sscale"], "keepc": t["keep"], "cinvl": t["cinvl"],
        "iota8": t["iota8"], "idb": t["idb"], "idf": t["idf"],
        "onesc": t["onesc"], "onesrf": t["onesrf"], "shv": t["shv"],
        "maskv": t["maskv"], "pw2": t["pw2"],
        "ind_g": one_hot_ind(gi_g, NG_HID), "ind_d": one_hot_ind(gi_d, NG_INT),
    }


# version 14
# speedup vs baseline: 1.1500x; 1.1500x over previous
import math
import numpy as np
import ml_dtypes
from contextlib import ExitStack

import concourse.bass as bass
import concourse.mybir as mybir
from concourse.bass_utils import run_bass_kernel_spmd

F32 = mybir.dt.float32
BF16 = mybir.dt.bfloat16
I32 = mybir.dt.int32
AF = mybir.ActivationFunctionType
ALU = mybir.AluOpType
AX = mybir.AxisListType

B, S, D = 8, 2048, 4096
H, HKV, HD = 32, 8, 128
INTER = 11008
KSEL = 205
NEG = -1.0e9
NEGSEL = -2.0e38
ISQ = 1.0 / math.sqrt(HD)
TGT = 208.5
NQT = S // 128
NHC = D // 128
NIC = INTER // 128
NG_HID = D // 128
NG_INT = INTER // 128
EPS = 1.0e-5
BIGM = 2.0e9
NFALSI = 5

SOFF = [0] * (NQT + 1)
for _t in range(1, NQT):
    SOFF[_t + 1] = SOFF[_t] + 128 * (_t + 1)
SPACK = SOFF[NQT]
POFF = [0] * (NQT + 1)
for _k in range(NQT):
    POFF[_k + 1] = POFF[_k] + (S - 128 * _k)
PPACK = POFF[NQT]


class SemC:
    def __init__(self, h):
        self.h = h
        self.v = 0

    def inc(self, inst, k=1):
        inst.then_inc(self.h, k)
        self.v += k
        return self.v


def host_tables():
    t = {}
    inv = 1.0 / 10000.0 ** (np.arange(0, HD, 2, dtype=np.float64) / HD)
    fr = np.outer(np.arange(S, dtype=np.float64), inv)
    emb = np.concatenate([fr, fr], axis=-1)
    t["cosT"] = np.ascontiguousarray(np.cos(emb).T.astype(np.float32))
    sinF = np.sin(emb).T.astype(np.float32)
    sinF[:64] *= -1.0
    t["sinF"] = np.ascontiguousarray(sinF)

    ql = np.arange(128)[:, None]
    kl = np.arange(128)[None, :]
    t["diagm"] = np.where(kl <= ql, 0.0, NEG).astype(np.float32)
    t["diagmT"] = np.ascontiguousarray(t["diagm"].T)
    t["caus01"] = np.ascontiguousarray(
        (t["diagmT"] == 0.0).astype(np.float32)).astype(ml_dtypes.bfloat16)

    from scipy.special import ndtri
    z0 = np.zeros((128, NQT - 1), np.float32)
    sscale = np.zeros((128, NQT - 1), np.float32)
    keep = np.zeros((128, NQT - 1), np.float32)
    for tt in range(1, NQT):
        valid = 128 * tt + np.arange(128) + 1.0
        f = np.clip(KSEL / valid, 1e-4, 0.9999)
        zz = ndtri(1.0 - f)
        z0[:, tt - 1] = zz
        phi = np.exp(-zz * zz / 2.0) / math.sqrt(2 * math.pi)
        sscale[:, tt - 1] = 0.75 / np.maximum(valid * phi, 1e-6)
        keep[:, tt - 1] = (valid <= KSEL).astype(np.float32)
    t["z0"] = z0
    t["sscale"] = sscale
    t["keep"] = keep
    t["cinvl"] = np.repeat((1.2533141 / (128.0 * np.arange(1, NQT))
                            ).astype(np.float32)[None, :], 128, axis=0)
    t["iota8"] = np.ascontiguousarray(
        np.tile(np.arange(1, 9, dtype=np.float32)[None, :], (128, NQT - 1)))
    t["idb"] = np.eye(128, dtype=np.float32).astype(ml_dtypes.bfloat16)
    t["idf"] = np.eye(128, dtype=np.float32)
    t["onesc"] = np.ones((128, 1), np.float32).astype(ml_dtypes.bfloat16)
    t["onesrf"] = np.ones((1, 128), np.float32)
    shv = (4 * (np.arange(128, dtype=np.int32) % 8))[:, None].astype(np.int32)
    t["shv"] = shv
    t["maskv"] = (np.int32(15) << shv).astype(np.int32)
    t["pw2"] = (2.0 ** (-shv.astype(np.float64))).astype(np.float32)
    return t


def one_hot_ind(g_idx, ngroups):
    n = g_idx.shape[0]
    ind = np.zeros((n, ngroups), np.float32)
    ind[np.arange(n), np.asarray(g_idx, dtype=int)] = 1.0
    return ind.astype(ml_dtypes.bfloat16)


def build_program(gmap_g, gmap_u, gmap_d, phases=("A", "B", "W", "C"), dbg=(), a_rs=4, b_grps=4):
    nc = bass.Bass("TRN2", target_bir_lowering=False, debug=False)

    ein = lambda name, shape, dt: nc.dram_tensor(name, shape, dt, kind="ExternalInput").ap()
    x_nat = ein("x_nat", [S, D], F32)
    x_t = ein("x_t", [D, S], F32)
    wqkv_t = ein("wqkv_t", [48, 128, NHC, 128], F32)
    wo_t = ein("wo_t", [8, 128, NHC, 512], F32)
    g_qw = ein("g_qw", [D // 8, INTER], I32)
    g_sc = ein("g_sc", [NG_HID, INTER], F32)
    g_qz = ein("g_qz", [NG_HID, INTER // 8], I32)
    u_qw = ein("u_qw", [D // 8, INTER], I32)
    u_sc = ein("u_sc", [NG_HID, INTER], F32)
    u_qz = ein("u_qz", [NG_HID, INTER // 8], I32)
    d_qw = ein("d_qw", [INTER // 8, D], I32)
    d_sc = ein("d_sc", [NG_INT, D], F32)
    d_qz = ein("d_qz", [NG_INT, D // 8], I32)
    cosT_i = ein("cosT", [128, S], F32)
    sinF_i = ein("sinF", [128, S], F32)
    diagm_i = ein("diagm", [128, 128], F32)
    diagmT_i = ein("diagmT", [128, 128], F32)
    caus01_i = ein("caus01", [128, 128], BF16)
    z0_i = ein("z0c", [128, NQT - 1], F32)
    sscale_i = ein("sscale", [128, NQT - 1], F32)
    keep_i = ein("keepc", [128, NQT - 1], F32)
    cinvl_i = ein("cinvl", [128, NQT - 1], F32)
    iota8_i = ein("iota8", [128, 8 * (NQT - 1)], F32)
    idb_i = ein("idb", [128, 128], BF16)
    idf_i = ein("idf", [128, 128], F32)
    onesc_i = ein("onesc", [128, 1], BF16)
    onesrf_i = ein("onesrf", [1, 128], F32)
    shv_i = ein("shv", [128, 1], I32)
    maskv_i = ein("maskv", [128, 1], I32)
    pw2_i = ein("pw2", [128, 1], F32)
    indg_i = ein("ind_g", [D, NG_HID], BF16)
    indd_i = ein("ind_d", [INTER, NG_INT], BF16)

    out = nc.dram_tensor("out", [S, D], F32, kind="ExternalOutput").ap()
    dbg_out = {}
    def dtap(nm, shape, dtt):
        if nm in dbg:
            dbg_out[nm] = nc.dram_tensor(nm, shape, dtt, kind="ExternalOutput").ap()
        return dbg_out.get(nm)
    dtap("dq", [H, 128, S], BF16)
    dtap("dk", [HKV, 128, S], BF16)
    dtap("dv", [HKV, S, 128], BF16)
    dtap("dtau", [H, S], F32)
    dtap("dz", [H, S], F32)
    dtap("daT", [NQT, D, 128], BF16)
    dtap("dy1", [S, D], BF16)
    dtap("dw4g", [NIC, D, 128], BF16)
    dtap("dw4d", [INTER, D], BF16)
    dtap("dzsg", [NG_HID, INTER], F32)
    dtap("dzsd", [NG_INT, D], F32)
    dtap("dgt", [NG_HID, 512], F32)
    dtap("dh", [NIC, 128, 512], BF16)

    dri = lambda name, shape, dt: nc.dram_tensor(name, shape, dt, kind="Internal").ap()
    x_bf = dri("x_bf", [D, S], BF16)
    wqkv_bf = dri("wqkv_bf", [48, 128, NHC, 128], BF16)
    wo_bf = dri("wo_bf", [8, 128, NHC, 512], BF16)
    gsc_bf = dri("gsc_bf", [NG_HID, INTER], BF16)
    usc_bf = dri("usc_bf", [NG_HID, INTER], BF16)
    dsc_bf = dri("dsc_bf", [NG_INT, D], BF16)
    qr_dram = dri("qr_dram", [H, 128, S], BF16)
    kr_dram = dri("kr_dram", [HKV, 128, S], BF16)
    v_dram = dri("v_dram", [HKV, S, 128], BF16)
    tau_dram = dri("tau_dram", [H, S], F32)
    aT_dram = dri("aT_dram", [NQT, 128, H, 128], BF16)
    y1_dram = dri("y1_dram", [S, D], BF16)
    a1_dram = dri("a1_dram", [S, D], F32)
    w4g_dram = dri("w4g_dram", [NIC, 128, NHC, 128], BF16)
    w4u_dram = dri("w4u_dram", [NIC, 128, NHC, 128], BF16)
    w4d_dram = dri("w4d_dram", [INTER, D], BF16)

    with ExitStack() as ctx:
        def sb(name, shape, dt, c=None):
            return (c or ctx).enter_context(nc.sbuf_tensor(name, shape, dt))

        def psm(name, shape, dt=F32, c=None):
            return (c or ctx).enter_context(nc.psum_tensor(name, shape, dt))

        def mksem(name):
            return SemC(ctx.enter_context(nc.semaphore(name)))

        s_diagm = sb("s_diagm", [128, 128], F32)
        s_diagmT = sb("s_diagmT", [128, 128], F32)
        s_caus01 = sb("s_caus01", [128, 128], BF16)
        s_z0 = sb("s_z0", [128, NQT - 1], F32)
        s_sscale = sb("s_sscale", [128, NQT - 1], F32)
        s_keep = sb("s_keep", [128, NQT - 1], F32)
        s_cinvl = sb("s_cinvl", [128, NQT - 1], F32)
        s_iota8 = sb("s_iota8", [128, 8 * (NQT - 1)], F32)
        s_idb = sb("s_idb", [128, 128], BF16)
        s_idf = sb("s_idf", [128, 128], F32)
        s_onesc = sb("s_onesc", [128, 1], BF16)
        s_onesrf = sb("s_onesrf", [1, 128], F32)
        s_shv = sb("s_shv", [128, 1], I32)
        s_maskv = sb("s_maskv", [128, 1], I32)
        s_pw2 = sb("s_pw2", [128, 1], F32)

        all_dma_sems = []
        qld = mksem("qld")
        xts = mksem("xts")
        wts = [mksem("wts0"), mksem("wts1")]
        sts = [mksem("sts0"), mksem("sts1")]
        vts = [mksem("vts0"), mksem("vts1")]
        et = mksem("et")
        ev = mksem("ev")
        ea = mksem("ea")
        eg = mksem("eg")
        qst = mksem("qst")
        all_dma_sems.extend([xts, wts[0], wts[1], sts[0], sts[1], vts[0], vts[1]])

        block = ctx.enter_context(nc.Block())

        def on(engname):
            def deco(f):
                getattr(block, engname)(f)
                return None
            return deco

        def phase_barrier():
            marks = [(sm, sm.v) for sm in all_dma_sems + [qld, et, ev, ea, eg, qst]]
            for engname in ("sync", "gpsimd", "tensor", "vector", "scalar"):
                @on(engname)
                def _(eng):
                    for sm, val in marks:
                        if val > 0:
                            eng.wait_ge(sm.h, val)

        @on("sync")
        def _(sy):
            for dst, src in [(s_diagm, diagm_i), (s_diagmT, diagmT_i),
                             (s_caus01, caus01_i),
                             (s_z0, z0_i), (s_sscale, sscale_i), (s_keep, keep_i),
                             (s_cinvl, cinvl_i), (s_iota8, iota8_i),
                             (s_idb, idb_i), (s_idf, idf_i), (s_onesc, onesc_i),
                             (s_onesrf, onesrf_i), (s_shv, shv_i),
                             (s_maskv, maskv_i), (s_pw2, pw2_i)]:
                qld.inc(sy.dma_start(out=dst[:], in_=src), 16)
            sy.wait_ge(qld.h, qld.v)
        CONST_LD = qld.v

        stg_st = [mksem("stgs0"), mksem("stgs1")]
        stg_in = [mksem("stgi0"), mksem("stgi1")]
        all_dma_sems.extend(stg_st + stg_in)

        with ExitStack() as stgctx:
            stg_buf = stgctx.enter_context(nc.sbuf_tensor("stg_buf", [128, 2 * 16384], BF16))
            jobs = []
            for cc in range(32):
                jobs.append((x_t.rearrange("(a p) n -> a p n", p=128)[cc],
                             x_bf.rearrange("(a p) n -> a p n", p=128)[cc]))
            for ic in range(48):
                jobs.append((wqkv_t[ic].rearrange("p a q -> p (a q)"),
                             wqkv_bf[ic].rearrange("p a q -> p (a q)")))
            for cc in range(8):
                jobs.append((wo_t[cc].rearrange("p a q -> p (a q)"),
                             wo_bf[cc].rearrange("p a q -> p (a q)")))
            jobs.append((g_sc, gsc_bf))
            jobs.append((u_sc, usc_bf))
            jobs.append((d_sc, dsc_bf))
            XW_JOBS = 32 + 48

            for ji, (srcap, dstap) in enumerate(jobs):
                par = ji % 2
                npart = srcap.shape[0]
                ncol = srcap.shape[1]

                @on("gpsimd")
                def _(gp, ji=ji, par=par, srcap=srcap, npart=npart, ncol=ncol):
                    if ji >= 2:
                        gp.wait_ge(stg_st[par].h, 16 * (ji // 2))
                    stg_in[par].inc(gp.dma_start(
                        out=stg_buf.ap()[0:npart, par * 16384: par * 16384 + ncol],
                        in_=srcap), 16)

                @on("sync")
                def _(sy, ji=ji, par=par, dstap=dstap, npart=npart, ncol=ncol):
                    sy.wait_ge(stg_in[par].h, stg_in[par].v)
                    stg_st[par].inc(sy.dma_start(
                        out=dstap,
                        in_=stg_buf.ap()[0:npart, par * 16384: par * 16384 + ncol]), 16)
                    if ji == XW_JOBS - 1:
                        sy.wait_ge(stg_st[0].h, 16 * ((XW_JOBS + 1) // 2))
                        sy.wait_ge(stg_st[1].h, 16 * (XW_JOBS // 2))
        XW_STAGED = (16 * ((XW_JOBS + 1) // 2), 16 * (XW_JOBS // 2))
        ALL_STAGED = None

        if "A" in phases:
            with ExitStack() as actx:
                s_cos = sb("s_cos", [128, S], F32, actx)
                s_sinF = sb("s_sinF", [128, S], F32, actx)
                xt_buf = sb("xt_buf", [128, NHC * 512], BF16, actx)
                wt_buf = sb("wt_buf", [128, 2 * D], BF16, actx)
                ta = sb("ta", [128, 512], F32, actx)
                tb = sb("tb", [128, 512], F32, actx)
                stg = sb("stg", [128, 2 * 512], BF16, actx)
                vstg = sb("vstg", [128, 2 * 512], BF16, actx)
                vout = sb("vout", [128, 2 * 512], BF16, actx)
                p_pr = [psm(f"p_pr{i}", [128, 512], F32, actx) for i in range(4)]
                p_vt = psm("p_vt", [128, 128], BF16, actx)

                NICA = H + 2 * HKV


                @on("sync")
                def _(sy):
                    for sm in (stg_st[0], stg_st[1], stg_in[0], stg_in[1]):
                        sy.wait_ge(sm.h, sm.v)
                    qld.inc(sy.dma_start(out=s_cos[:], in_=cosT_i), 16)
                    qld.inc(sy.dma_start(out=s_sinF[:], in_=sinF_i), 16)
                    sy.wait_ge(qld.h, qld.v)
                COS_LD = qld.v

                mm_done = {}
                evac_done = {}
                vstg_free = {0: None, 1: None}
                vt_tr = {}
                vt_evac = {}
                vt_prev = [None]

                for rs in range(a_rs):
                    @on("scalar")
                    def _(sc, rs=rs):
                        if rs == 0:
                            sc.wait_ge(qld.h, CONST_LD)
                            sc.wait_ge(stg_st[0].h, stg_st[0].v)
                            sc.wait_ge(stg_st[1].h, stg_st[1].v)
                            sc.wait_ge(stg_in[0].h, stg_in[0].v)
                            sc.wait_ge(stg_in[1].h, stg_in[1].v)
                        else:
                            sc.wait_ge(et.h, mm_done[rs * NICA - 1])
                        xts.inc(sc.dma_start(
                            out=xt_buf.ap().rearrange("p (c n) -> p c n", n=512),
                            in_=x_bf[:, rs * 512:(rs + 1) * 512]
                                .rearrange("(c p) n -> p c n", p=128)), 16)

                    for ic in range(NICA):
                        u = rs * NICA + ic

                        @on("scalar")
                        def _(sc, u=u, ic=ic):
                            if u >= 2:
                                sc.wait_ge(et.h, mm_done[u - 2])
                            wts[u % 2].inc(sc.dma_start(
                                out=wt_buf.ap()[:, (u % 2) * D:(u % 2 + 1) * D]
                                    .rearrange("p (c i) -> p c i", i=128),
                                in_=wqkv_bf[ic]), 16)

                        @on("tensor")
                        def _(te, rs=rs, u=u):
                            te.wait_ge(xts.h, 16 * (rs + 1))
                            te.wait_ge(wts[u % 2].h, wts[u % 2].v)
                            if u >= 4:
                                sm, val = evac_done[u - 4]
                                te.wait_ge(sm.h, val)
                            pb = p_pr[u % 4]
                            inst = None
                            for dc in range(NHC):
                                inst = te.matmul(
                                    pb[:], wt_buf.ap()[:, (u % 2) * D + dc * 128:(u % 2) * D + (dc + 1) * 128],
                                    xt_buf.ap()[:, dc * 512:(dc + 1) * 512],
                                    start=(dc == 0), stop=(dc == NHC - 1))
                            mm_done[u] = et.inc(inst)

                        if ic < H + HKV:
                            @on("vector")
                            def _(ve, rs=rs, u=u):
                                pb = p_pr[u % 4]
                                ve.wait_ge(et.h, mm_done[u])
                                ve.wait_ge(qld.h, COS_LD)
                                ve.wait_ge(sts[u % 2].h, sts[u % 2].v)
                                ve.wait_ge(ev.h, ev.v)
                                cs = s_cos.ap()[:, rs * 512:(rs + 1) * 512]
                                sn = s_sinF.ap()[:, rs * 512:(rs + 1) * 512]
                                ev.inc(ve.tensor_tensor(out=ta[:], in0=pb[:], in1=cs, op=ALU.mult))
                                ve.tensor_tensor(out=tb[0:64, :], in0=pb[64:128, :],
                                                 in1=sn[0:64, :], op=ALU.mult)
                                ev.inc(ve.tensor_tensor(out=tb[64:128, :], in0=pb[0:64, :],
                                                        in1=sn[64:128, :], op=ALU.mult))
                                ve.wait_ge(ev.h, ev.v)
                                i4 = ve.tensor_tensor(out=stg.ap()[:, (u % 2) * 512:(u % 2 + 1) * 512], in0=ta[:],
                                                      in1=tb[:], op=ALU.add)
                                evac_done[u] = (ev, ev.inc(i4))

                            @on("sync")
                            def _(sy, rs=rs, ic=ic, u=u):
                                sm, val = evac_done[u]
                                sy.wait_ge(sm.h, val)
                                if ic < H:
                                    dst = qr_dram[ic, :, rs * 512:(rs + 1) * 512]
                                else:
                                    dst = kr_dram[ic - H, :, rs * 512:(rs + 1) * 512]
                                sts[u % 2].inc(sy.dma_start(out=dst, in_=stg.ap()[:, (u % 2) * 512:(u % 2 + 1) * 512]), 16)
                        else:
                            kv = ic - H - HKV

                            @on("scalar")
                            def _(sc, u=u):
                                pb = p_pr[u % 4]
                                sc.wait_ge(et.h, mm_done[u])
                                if vstg_free[u % 2] is not None:
                                    sc.wait_ge(et.h, vstg_free[u % 2])
                                i1 = sc.activation(out=vstg.ap()[:, (u % 2) * 512:(u % 2 + 1) * 512], in_=pb[:], func=AF.Identity)
                                evac_done[u] = (ea, ea.inc(i1))

                            for j4 in range(4):
                                @on("tensor")
                                def _(te, u=u, j4=j4):
                                    if j4 == 0:
                                        sm, val = evac_done[u]
                                        te.wait_ge(sm.h, val)
                                    prev = vt_evac.get((u, j4 - 1), vt_prev[0])
                                    if prev is not None:
                                        te.wait_ge(ev.h, prev)
                                    inst = te.transpose(
                                        p_vt[:], vstg.ap()[:, (u % 2) * 512 + j4 * 128:(u % 2) * 512 + (j4 + 1) * 128],
                                        s_idb[:])
                                    vt_tr[(u, j4)] = et.inc(inst)

                                @on("vector")
                                def _(ve, u=u, j4=j4):
                                    if j4 == 0:
                                        ve.wait_ge(vts[u % 2].h, vts[u % 2].v)
                                    ve.wait_ge(et.h, vt_tr[(u, j4)])
                                    i1 = ve.tensor_copy(
                                        vout.ap()[:, (u % 2) * 512 + j4 * 128:(u % 2) * 512 + (j4 + 1) * 128], p_vt[:])
                                    vt_evac[(u, j4)] = ev.inc(i1)
                            vstg_free[u % 2] = vt_tr[(u, 3)]
                            vt_prev[0] = vt_evac[(u, 3)]

                            @on("sync")
                            def _(sy, u=u, kv=kv, rs=rs):
                                sy.wait_ge(ev.h, vt_evac[(u, 3)])
                                vts[u % 2].inc(sy.dma_start(
                                    out=v_dram[kv, rs * 512:(rs + 1) * 512, :]
                                        .rearrange("(a k) h -> k a h", k=128),
                                    in_=vout.ap()[:, (u % 2) * 512:(u % 2 + 1) * 512].rearrange("k (a h) -> k a h", h=128)), 16)


                if "dq" in dbg:
                    @on("sync")
                    def _(sy):
                        for sm in (sts[0], sts[1], vts[0], vts[1]):
                            sy.wait_ge(sm.h, sm.v)
                        qst.inc(sy.dma_start(out=dbg_out["dq"], in_=qr_dram), 16)
                        qst.inc(sy.dma_start(out=dbg_out["dk"], in_=kr_dram), 16)
                        qst.inc(sy.dma_start(out=dbg_out["dv"], in_=v_dram), 16)


        if "B" in phases:
            with ExitStack() as bctx:
                qr1 = sb("qr1", [128, 2 * S], BF16, bctx)
                kr1 = sb("kr1", [128, 2 * S], BF16, bctx)
                qr3 = sb("qr3", [128, 2 * S], BF16, bctx)
                kr3 = sb("kr3", [128, 2 * S], BF16, bctx)
                v3 = sb("v3", [128, 2 * S], BF16, bctx)
                tbrz = sb("tbrz", [128, S], F32, bctx)
                cjunk = sb("cjunk", [128, 2 * S], BF16, bctx)
                d2 = sb("d2", [128, 2 * S], BF16, bctx)
                exb = sb("exb", [128, S], BF16, bctx)
                aT_stg = sb("aT_stg", [128, S], BF16, bctx)
                zrow = sb("zrow", [1, S], F32, bctx)
                taub = sb("taub", [128, 16], F32, bctx)
                taubT = sb("taubT", [16, 128], F32, bctx)
                sel_names = ["sabs", "sig", "rstep", "tauc", "cntc", "ta_", "ca_",
                             "tb_", "cb_", "hava", "havb", "illa", "illb",
                             "blo_t", "blo_c", "bhi_t", "bhi_c",
                             "tmp0", "tmp1", "tmp2", "tmp3", "jl", "jh",
                             "taulo", "tauhi", "tstar", "ones15", "negs15"]
                SEL = {nm: sb("sel_" + nm, [128, NQT - 1], F32, bctx) for nm in sel_names}
                SELI = {nm: sb("seli_" + nm, [128, NQT - 1], I32, bctx)
                        for nm in ["mi", "mi2"]}
                m8lo = sb("m8lo", [128, 8 * (NQT - 1)], F32, bctx)
                m8hi = sb("m8hi", [128, 8 * (NQT - 1)], F32, bctx)
                ind120 = sb("ind120", [128, 8 * (NQT - 1)], F32, bctx)
                vsel = sb("vsel", [128, 8 * (NQT - 1)], F32, bctx)
                S_pack = sb("S_pack", [128, SPACK], BF16, bctx)
                st_pack = sb("st_pack", [128, PPACK], BF16, bctx)
                p_pack = sb("p_pack", [128, PPACK], BF16, bctx)

                p_s = [psm(f"p_s{i}", [128, 512], F32, bctx) for i in range(2)]
                p_st2 = [psm(f"p_st{i}", [128, 512], F32, bctx) for i in range(2)]
                p_z = psm("p_z", [1, 512], F32, bctx)
                p_av = psm("p_av", [128, 512], F32, bctx)
                p_mis = psm("p_mis", [128, 512], F32, bctx)
                p_tr2 = psm("p_tr2", [16, 128], F32, bctx)
                phase_barrier()

                q1l = [mksem("q1l0"), mksem("q1l1")]
                k1l = [mksem("k1l0"), mksem("k1l1")]
                q3l = [mksem("q3l0"), mksem("q3l1")]
                k3l = [mksem("k3l0"), mksem("k3l1")]
                v3l = [mksem("v3l0"), mksem("v3l1")]
                tbl = mksem("tbl")
                tbo = mksem("tbo")
                ats = mksem("ats")
                all_dma_sems.extend(q1l + k1l + q3l + k3l + v3l + [tbl, tbo, ats])

                GTGT = TGT
                E = SEL

                lm = {
                    "sel_done": None,
                    "b3dve_done": None,
                    "av_done": None,
                    "atstore": None,
                    "tau_tr": None,
                    "tb_used": None,
                    "q1_used": {0: None, 1: None},
                    "k1_used": {0: None, 1: None},
                    "q3_used": {0: None, 1: None},
                    "k3_used": {0: None, 1: None},
                    "v3_used": {0: None, 1: None},
                    "stev": {},
                    "sevac": {},
                    "psfree": {0: None, 1: None},
                    "pstfree": {0: None, 1: None},
                }

                def vsec(fns):
                    @on("vector")
                    def _(ve):
                        for fn in fns:
                            ve.wait_ge(ev.h, ev.v)
                            ev.inc(fn(ve))
                    return ev.v

                def emit_b1_loads(h):
                    par = h % 2
                    kv = h // 4

                    @on("sync")
                    def _(sy, h=h, par=par, kv=kv):
                        if lm["q1_used"][par] is not None:
                            sm, val = lm["q1_used"][par]
                            sy.wait_ge(sm.h, val)
                        q1l[par].inc(sy.dma_start(
                            out=qr1.ap()[:, par * S:(par + 1) * S], in_=qr_dram[h]), 16)
                        if lm["k1_used"][par] is not None:
                            sm, val = lm["k1_used"][par]
                            sy.wait_ge(sm.h, val)
                        k1l[par].inc(sy.dma_start(
                            out=kr1.ap()[:, par * S:(par + 1) * S], in_=kr_dram[kv]), 16)

                def emit_b1_scores(h):
                    par = h % 2
                    Q1 = q1l[par].v
                    K1 = k1l[par].v
                    sevac = {}
                    seq = [0]
                    last_mm = [None]
                    for t in range(1, NQT):
                        L = 128 * (t + 1)
                        nks = (L + 511) // 512
                        for ks in range(nks):
                            c0 = 512 * ks
                            cw = min(512, L - c0)
                            pidx = seq[0] % 2
                            seq[0] += 1

                            @on("tensor")
                            def _(te, t=t, ks=ks, c0=c0, cw=cw, pidx=pidx, par=par, Q1=Q1, K1=K1):
                                te.wait_ge(q1l[par].h, Q1)
                                te.wait_ge(k1l[par].h, K1)
                                if lm["psfree"][pidx] is not None:
                                    sm, val = lm["psfree"][pidx]
                                    te.wait_ge(sm.h, val)
                                inst = te.matmul(
                                    p_s[pidx][:, 0:cw],
                                    qr1.ap()[:, par * S + 128 * t: par * S + 128 * (t + 1)],
                                    kr1.ap()[:, par * S + c0: par * S + c0 + cw],
                                    start=True, stop=True)
                                lm[("mm1", t, ks)] = et.inc(inst)
                                last_mm[0] = lm[("mm1", t, ks)]

                            @on("scalar")
                            def _(sc, t=t, ks=ks, c0=c0, cw=cw, pidx=pidx):
                                sc.wait_ge(et.h, lm[("mm1", t, ks)])
                                if t == 1 and ks == 0 and lm["sel_done"] is not None:
                                    sc.wait_ge(ev.h, lm["sel_done"])
                                sc.wait_ge(ea.h, ea.v)
                                i1 = sc.activation(
                                    out=S_pack.ap()[:, SOFF[t] + c0: SOFF[t] + c0 + cw],
                                    in_=p_s[pidx][:, 0:cw], func=AF.Identity)
                                sevac[(t, ks)] = (ea, ea.inc(i1))
                                lm["psfree"][pidx] = sevac[(t, ks)]
                    lm["q1_used"][par] = (et, last_mm[0])
                    lm["k1_used"][par] = (et, last_mm[0])

                    @on("vector")
                    def _(ve):
                        ve.wait_ge(ev.h, ev.v)
                        for t in range(1, NQT):
                            L = 128 * (t + 1)
                            sm, val = sevac[(t, (L - 1) // 512)]
                            ve.wait_ge(sm.h, val)
                            sl = S_pack.ap()[:, SOFF[t] + L - 128: SOFF[t] + L]
                            ev.inc(ve.tensor_tensor(out=sl, in0=sl, in1=s_diagm[:], op=ALU.add))
                    DIAG_ALL = ev.v

                    @on("scalar")
                    def _(sc):
                        for t in range(1, NQT):
                            for ks in range((128 * t + 511) // 512):
                                sm, val = sevac[(t, ks)]
                                sc.wait_ge(sm.h, val)
                            sc.wait_ge(ea.h, ea.v)
                            ea.inc(sc.activation(
                                out=d2.ap()[:, S:S + 128 * t],
                                in_=S_pack.ap()[:, SOFF[t]: SOFF[t] + 128 * t],
                                func=AF.Abs, accum_out=SEL["sabs"][:, t - 1:t]))
                    SABS_DONE = ea.v
                    return sevac, DIAG_ALL, SABS_DONE

                def emit_b1_select(h, scores_ret):
                    sevac, DIAG_ALL, SABS_DONE = scores_ret

                    def counts(round_no):
                        @on("vector")
                        def _(ve):
                            if round_no == 0:
                                ve.wait_ge(ea.h, SABS_DONE)
                                ve.wait_ge(ev.h, DIAG_ALL)
                            ve.wait_ge(ev.h, ev.v)
                            for t in range(1, NQT):
                                L = 128 * (t + 1)
                                ev.inc(ve.tensor_scalar(
                                    out=d2.ap()[:, 0:L],
                                    in0=S_pack.ap()[:, SOFF[t]: SOFF[t] + L],
                                    scalar1=SEL["tauc"][:, t - 1:t], scalar2=0.0,
                                    op0=ALU.is_ge, op1=ALU.add,
                                    accum_out=SEL["cntc"][:, t - 1:t]))

                    def update_best():
                        vsec([
                            lambda ve: ve.tensor_scalar(out=E["tmp0"][:], in0=E["cntc"][:], scalar1=KSEL - 0.5, scalar2=None, op0=ALU.is_ge),
                            lambda ve: ve.tensor_tensor(out=E["tmp1"][:], in0=E["cntc"][:], in1=E["blo_c"][:], op=ALU.is_lt),
                            lambda ve: ve.tensor_tensor(out=E["tmp2"][:], in0=E["tmp0"][:], in1=E["tmp1"][:], op=ALU.mult),
                            lambda ve: ve.tensor_scalar(out=SELI["mi"][:], in0=E["tmp2"][:], scalar1=0.5, scalar2=None, op0=ALU.is_ge),
                            lambda ve: ve.copy_predicated(out=E["blo_c"][:], mask=SELI["mi"][:], data=E["cntc"][:]),
                            lambda ve: ve.copy_predicated(out=E["blo_t"][:], mask=SELI["mi"][:], data=E["tauc"][:]),
                            lambda ve: ve.tensor_scalar(out=E["tmp0"][:], in0=E["cntc"][:], scalar1=KSEL - 0.5, scalar2=None, op0=ALU.is_lt),
                            lambda ve: ve.tensor_tensor(out=E["tmp1"][:], in0=E["cntc"][:], in1=E["bhi_c"][:], op=ALU.is_gt),
                            lambda ve: ve.tensor_tensor(out=E["tmp2"][:], in0=E["tmp0"][:], in1=E["tmp1"][:], op=ALU.mult),
                            lambda ve: ve.tensor_scalar(out=SELI["mi"][:], in0=E["tmp2"][:], scalar1=0.5, scalar2=None, op0=ALU.is_ge),
                            lambda ve: ve.copy_predicated(out=E["bhi_c"][:], mask=SELI["mi"][:], data=E["cntc"][:]),
                            lambda ve: ve.copy_predicated(out=E["bhi_t"][:], mask=SELI["mi"][:], data=E["tauc"][:]),
                        ])

                    def update_bracket():
                        vsec([
                            lambda ve: ve.tensor_scalar(out=E["tmp0"][:], in0=E["cntc"][:], scalar1=GTGT, scalar2=None, op0=ALU.is_ge),
                            lambda ve: ve.tensor_scalar(out=E["tmp1"][:], in0=E["tmp0"][:], scalar1=-1.0, scalar2=1.0, op0=ALU.mult, op1=ALU.add),
                            lambda ve: ve.tensor_tensor(out=E["tmp2"][:], in0=E["tmp1"][:], in1=E["hava"][:], op=ALU.mult),
                            lambda ve: ve.tensor_scalar(out=SELI["mi"][:], in0=E["tmp2"][:], scalar1=0.5, scalar2=None, op0=ALU.is_ge),
                            lambda ve: ve.tensor_scalar(out=E["tmp3"][:], in0=E["illa"][:], scalar1=0.5, scalar2=None, op0=ALU.mult),
                            lambda ve: ve.tensor_copy(E["illa"][:], E["ones15"][:]),
                            lambda ve: ve.copy_predicated(out=E["illa"][:], mask=SELI["mi"][:], data=E["tmp3"][:]),
                            lambda ve: ve.tensor_tensor(out=E["tmp2"][:], in0=E["tmp0"][:], in1=E["havb"][:], op=ALU.mult),
                            lambda ve: ve.tensor_scalar(out=SELI["mi"][:], in0=E["tmp2"][:], scalar1=0.5, scalar2=None, op0=ALU.is_ge),
                            lambda ve: ve.tensor_scalar(out=E["tmp3"][:], in0=E["illb"][:], scalar1=0.5, scalar2=None, op0=ALU.mult),
                            lambda ve: ve.tensor_copy(E["illb"][:], E["ones15"][:]),
                            lambda ve: ve.copy_predicated(out=E["illb"][:], mask=SELI["mi"][:], data=E["tmp3"][:]),
                            lambda ve: ve.tensor_scalar(out=SELI["mi"][:], in0=E["tmp0"][:], scalar1=0.5, scalar2=None, op0=ALU.is_ge),
                            lambda ve: ve.copy_predicated(out=E["ta_"][:], mask=SELI["mi"][:], data=E["tauc"][:]),
                            lambda ve: ve.copy_predicated(out=E["ca_"][:], mask=SELI["mi"][:], data=E["cntc"][:]),
                            lambda ve: ve.tensor_scalar(out=SELI["mi2"][:], in0=E["tmp0"][:], scalar1=0.5, scalar2=None, op0=ALU.is_lt),
                            lambda ve: ve.copy_predicated(out=E["tb_"][:], mask=SELI["mi2"][:], data=E["tauc"][:]),
                            lambda ve: ve.copy_predicated(out=E["cb_"][:], mask=SELI["mi2"][:], data=E["cntc"][:]),
                            lambda ve: ve.tensor_tensor(out=E["hava"][:], in0=E["hava"][:], in1=E["tmp0"][:], op=ALU.max),
                            lambda ve: ve.tensor_tensor(out=E["havb"][:], in0=E["havb"][:], in1=E["tmp1"][:], op=ALU.max),
                        ])

                    vsec([
                        lambda ve: ve.memset(E["ones15"][:], 1.0),
                        lambda ve: ve.memset(E["negs15"][:], NEGSEL),
                        lambda ve: ve.memset(E["blo_c"][:], 1.0e9),
                        lambda ve: ve.memset(E["blo_t"][:], 0.0),
                        lambda ve: ve.memset(E["bhi_c"][:], -1.0e9),
                        lambda ve: ve.memset(E["bhi_t"][:], 0.0),
                    ])

                    @on("vector")
                    def _(ve):
                        ve.wait_ge(ea.h, SABS_DONE)
                        ve.wait_ge(ev.h, ev.v)
                        ev.inc(ve.tensor_tensor(out=E["sig"][:], in0=E["sabs"][:], in1=s_cinvl[:], op=ALU.mult))
                        ve.wait_ge(ev.h, ev.v)
                        ev.inc(ve.tensor_tensor(out=E["rstep"][:], in0=s_sscale[:], in1=E["sig"][:], op=ALU.mult))
                        ve.wait_ge(ev.h, ev.v)
                        ev.inc(ve.tensor_tensor(out=E["tauc"][:], in0=E["sig"][:], in1=s_z0[:], op=ALU.mult))
                    counts(0)
                    vsec([
                        lambda ve: ve.tensor_copy(E["ta_"][:], E["tauc"][:]),
                        lambda ve: ve.tensor_copy(E["ca_"][:], E["cntc"][:]),
                        lambda ve: ve.tensor_copy(E["tb_"][:], E["tauc"][:]),
                        lambda ve: ve.tensor_copy(E["cb_"][:], E["cntc"][:]),
                        lambda ve: ve.tensor_scalar(out=E["hava"][:], in0=E["cntc"][:], scalar1=GTGT, scalar2=None, op0=ALU.is_ge),
                        lambda ve: ve.tensor_scalar(out=E["havb"][:], in0=E["cntc"][:], scalar1=GTGT, scalar2=None, op0=ALU.is_lt),
                        lambda ve: ve.memset(E["illa"][:], 1.0),
                        lambda ve: ve.memset(E["illb"][:], 1.0),
                    ])
                    update_best()
                    vsec([
                        lambda ve: ve.tensor_scalar(out=E["tmp0"][:], in0=E["cntc"][:], scalar1=GTGT, scalar2=None, op0=ALU.subtract),
                        lambda ve: ve.tensor_tensor(out=E["tmp1"][:], in0=E["tmp0"][:], in1=E["rstep"][:], op=ALU.mult),
                        lambda ve: ve.tensor_tensor(out=E["tauc"][:], in0=E["tauc"][:], in1=E["tmp1"][:], op=ALU.add),
                    ])
                    counts(1)
                    update_best()
                    update_bracket()
                    for rnd in range(NFALSI):
                        vsec([
                            lambda ve: ve.tensor_scalar(out=E["tmp0"][:], in0=E["ca_"][:], scalar1=GTGT, scalar2=None, op0=ALU.subtract),
                            lambda ve: ve.tensor_tensor(out=E["tmp0"][:], in0=E["tmp0"][:], in1=E["illa"][:], op=ALU.mult),
                            lambda ve: ve.tensor_scalar(out=E["tmp1"][:], in0=E["cb_"][:], scalar1=GTGT, scalar2=None, op0=ALU.subtract),
                            lambda ve: ve.tensor_tensor(out=E["tmp1"][:], in0=E["tmp1"][:], in1=E["illb"][:], op=ALU.mult),
                            lambda ve: ve.tensor_tensor(out=E["tmp2"][:], in0=E["tmp0"][:], in1=E["tmp1"][:], op=ALU.subtract),
                            lambda ve: ve.tensor_scalar(out=E["tmp2"][:], in0=E["tmp2"][:], scalar1=1e-9, scalar2=None, op0=ALU.max),
                            lambda ve: ve.reciprocal(E["tmp2"][:], E["tmp2"][:]),
                            lambda ve: ve.tensor_tensor(out=E["tmp0"][:], in0=E["tmp0"][:], in1=E["tmp2"][:], op=ALU.mult),
                            lambda ve: ve.tensor_tensor(out=E["tmp1"][:], in0=E["tb_"][:], in1=E["ta_"][:], op=ALU.subtract),
                            lambda ve: ve.tensor_tensor(out=E["tmp1"][:], in0=E["tmp1"][:], in1=E["tmp0"][:], op=ALU.mult),
                            lambda ve: ve.tensor_tensor(out=E["tmp1"][:], in0=E["ta_"][:], in1=E["tmp1"][:], op=ALU.add),
                            lambda ve: ve.tensor_scalar(out=E["tmp2"][:], in0=E["cntc"][:], scalar1=GTGT, scalar2=None, op0=ALU.subtract),
                            lambda ve: ve.tensor_tensor(out=E["tmp2"][:], in0=E["tmp2"][:], in1=E["rstep"][:], op=ALU.mult),
                            lambda ve: ve.tensor_tensor(out=E["tmp2"][:], in0=E["tauc"][:], in1=E["tmp2"][:], op=ALU.add),
                            lambda ve: ve.tensor_tensor(out=E["tmp3"][:], in0=E["hava"][:], in1=E["havb"][:], op=ALU.mult),
                            lambda ve: ve.tensor_scalar(out=SELI["mi"][:], in0=E["tmp3"][:], scalar1=0.5, scalar2=None, op0=ALU.is_ge),
                            lambda ve: ve.tensor_copy(E["tauc"][:], E["tmp2"][:]),
                            lambda ve: ve.copy_predicated(out=E["tauc"][:], mask=SELI["mi"][:], data=E["tmp1"][:]),
                        ])
                        counts(2 + rnd)
                        update_best()
                        update_bracket()

                    lo_done = {}
                    m8lo_ev = {}
                    for t in range(1, NQT):
                        L = 128 * (t + 1)
                        sl_ = (t % 2) * S

                        @on("vector")
                        def _(ve, t=t, L=L, sl_=sl_):
                            ve.wait_ge(ev.h, ev.v)
                            lo_done[("cmp", t)] = ev.inc(ve.tensor_scalar(
                                out=d2.ap()[:, sl_:sl_ + L],
                                in0=S_pack.ap()[:, SOFF[t]:SOFF[t] + L],
                                scalar1=E["blo_t"][:, t - 1:t], scalar2=-BIGM,
                                op0=ALU.is_lt, op1=ALU.mult))
                            ve.wait_ge(ev.h, ev.v)
                            ev.inc(ve.tensor_scalar(
                                out=exb.ap()[:, 0:L],
                                in0=S_pack.ap()[:, SOFF[t]:SOFF[t] + L],
                                scalar1=E["bhi_t"][:, t - 1:t], scalar2=-BIGM,
                                op0=ALU.is_ge, op1=ALU.mult))
                            ve.wait_ge(ev.h, ev.v)
                            ev.inc(ve.tensor_tensor(
                                out=exb.ap()[:, 0:L], in0=exb.ap()[:, 0:L],
                                in1=S_pack.ap()[:, SOFF[t]:SOFF[t] + L], op=ALU.add))
                            ve.wait_ge(ev.h, ev.v)
                            ev.inc(ve.max(out=m8hi.ap()[:, 8 * (t - 1):8 * t], in_=exb.ap()[:, 0:L]))
                            if t >= 2:
                                ve.wait_ge(eg.h, lo_done[t - 1])
                                ve.wait_ge(ev.h, ev.v)
                                Lp = 128 * t
                                m8lo_ev[t - 1] = ev.inc(ve.max(
                                    out=m8lo.ap()[:, 8 * (t - 2):8 * (t - 1)],
                                    in_=cjunk.ap()[:, ((t - 1) % 2) * S:((t - 1) % 2) * S + Lp]))

                        @on("gpsimd")
                        def _(gp, t=t, L=L, sl_=sl_):
                            gp.wait_ge(ev.h, lo_done[("cmp", t)])
                            if t >= 3:
                                gp.wait_ge(ev.h, m8lo_ev[t - 2])
                            gp.wait_ge(eg.h, eg.v)
                            lo_done[t] = eg.inc(gp.tensor_tensor(
                                out=cjunk.ap()[:, sl_:sl_ + L],
                                in0=d2.ap()[:, sl_:sl_ + L],
                                in1=S_pack.ap()[:, SOFF[t]:SOFF[t] + L], op=ALU.subtract))

                    @on("vector")
                    def _(ve):
                        t = NQT - 1
                        L = 128 * (t + 1)
                        ve.wait_ge(eg.h, lo_done[t])
                        ve.wait_ge(ev.h, ev.v)
                        m8lo_ev[t] = ev.inc(ve.max(
                            out=m8lo.ap()[:, 8 * (t - 1):8 * t],
                            in_=cjunk.ap()[:, (t % 2) * S:(t % 2) * S + L]))

                    vsec([
                        lambda ve: ve.tensor_scalar(out=E["jl"][:], in0=E["blo_c"][:], scalar1=-(KSEL - 1.0), scalar2=None, op0=ALU.add),
                        lambda ve: ve.tensor_scalar(out=E["jl"][:], in0=E["jl"][:], scalar1=1.0, scalar2=8.0, op0=ALU.max, op1=ALU.min),
                        lambda ve: ve.tensor_scalar(out=E["jh"][:], in0=E["bhi_c"][:], scalar1=-1.0, scalar2=float(KSEL), op0=ALU.mult, op1=ALU.add),
                        lambda ve: ve.tensor_scalar(out=E["jh"][:], in0=E["jh"][:], scalar1=1.0, scalar2=8.0, op0=ALU.max, op1=ALU.min),
                        lambda ve: ve.tensor_tensor(
                            out=ind120.ap().rearrange("p (t j) -> p t j", j=8),
                            in0=s_iota8.ap().rearrange("p (t j) -> p t j", j=8),
                            in1=E["jl"].ap().unsqueeze(2).broadcast_to([128, NQT - 1, 8]),
                            op=ALU.is_equal),
                        lambda ve: ve.tensor_tensor(out=vsel[:], in0=m8lo[:], in1=ind120[:], op=ALU.mult),
                        lambda ve: ve.reduce_sum(E["taulo"][:], vsel.ap().rearrange("p (t j) -> p t j", j=8), axis=AX.X),
                        lambda ve: ve.tensor_scalar(out=E["taulo"][:], in0=E["taulo"][:], scalar1=-1.0, scalar2=None, op0=ALU.mult),
                        lambda ve: ve.tensor_tensor(
                            out=ind120.ap().rearrange("p (t j) -> p t j", j=8),
                            in0=s_iota8.ap().rearrange("p (t j) -> p t j", j=8),
                            in1=E["jh"].ap().unsqueeze(2).broadcast_to([128, NQT - 1, 8]),
                            op=ALU.is_equal),
                        lambda ve: ve.tensor_tensor(out=vsel[:], in0=m8hi[:], in1=ind120[:], op=ALU.mult),
                        lambda ve: ve.reduce_sum(E["tauhi"][:], vsel.ap().rearrange("p (t j) -> p t j", j=8), axis=AX.X),
                        lambda ve: ve.tensor_scalar(out=SELI["mi"][:], in0=E["blo_c"][:], scalar1=1.0e8, scalar2=None, op0=ALU.is_lt),
                        lambda ve: ve.tensor_copy(E["tstar"][:], E["tauhi"][:]),
                        lambda ve: ve.copy_predicated(out=E["tstar"][:], mask=SELI["mi"][:], data=E["taulo"][:]),
                        lambda ve: ve.tensor_scalar(out=SELI["mi"][:], in0=E["bhi_c"][:], scalar1=KSEL - 8.5, scalar2=None, op0=ALU.is_ge),
                        lambda ve: ve.copy_predicated(out=E["tstar"][:], mask=SELI["mi"][:], data=E["tauhi"][:]),
                        lambda ve: ve.tensor_scalar(out=SELI["mi"][:], in0=E["blo_c"][:], scalar1=KSEL + 7.5, scalar2=None, op0=ALU.is_le),
                        lambda ve: ve.copy_predicated(out=E["tstar"][:], mask=SELI["mi"][:], data=E["taulo"][:]),
                        lambda ve: ve.tensor_scalar(out=SELI["mi"][:], in0=s_keep[:], scalar1=0.5, scalar2=None, op0=ALU.is_ge),
                        lambda ve: ve.copy_predicated(out=E["tstar"][:], mask=SELI["mi"][:], data=E["negs15"][:]),
                    ])

                    @on("vector")
                    def _(ve):
                        if lm["tau_tr"] is not None:
                            ve.wait_ge(et.h, lm["tau_tr"])
                        ve.wait_ge(ev.h, ev.v)
                        ev.inc(ve.memset(taub.ap()[:, 0:1], NEGSEL))
                        ve.wait_ge(ev.h, ev.v)
                        ev.inc(ve.tensor_copy(taub.ap()[:, 1:16], E["tstar"][:]))
                    lm["sel_done"] = ev.v
                    TAUB = ev.v

                    @on("tensor")
                    def _(te, h=h):
                        te.wait_ge(ev.h, TAUB)
                        lm["tau_tr"] = et.inc(te.transpose(p_tr2[:], taub[:], s_idf[:]))

                    @on("vector")
                    def _(ve):
                        ve.wait_ge(et.h, lm["tau_tr"])
                        ve.wait_ge(ev.h, ev.v)
                        ev.inc(ve.tensor_copy(taubT[:], p_tr2[:]))
                    TAUTT = ev.v

                    @on("sync")
                    def _(sy, h=h):
                        sy.wait_ge(ev.h, TAUTT)
                        tbo.inc(sy.dma_start(
                            out=tau_dram[h].rearrange("(t q) -> t q", q=128),
                            in_=taubT[:]), 16)
                        sy.wait_ge(tbo.h, tbo.v)
                        if lm["tb_used"] is not None:
                            sy.wait_ge(eg.h, lm["tb_used"])
                        tbl.inc(sy.dma_start(
                            out=tbrz[:],
                            in_=tau_dram[h:h + 1, :].broadcast_to([128, S])), 16)
                    if "dtau" in dbg:
                        @on("sync")
                        def _(sy, h=h):
                            qst.inc(sy.dma_start(out=dbg_out["dtau"][h:h + 1], in_=tau_dram[h:h + 1]), 16)

                def emit_b3_loads(h):
                    par = h % 2
                    kv = h // 4

                    @on("sync")
                    def _(sy, h=h, par=par, kv=kv):
                        for key, sem_l, buf, src in [
                            ("q3_used", q3l, qr3, qr_dram[h]),
                            ("k3_used", k3l, kr3, kr_dram[kv]),
                        ]:
                            if lm[key][par] is not None:
                                sm, val = lm[key][par]
                                sy.wait_ge(sm.h, val)
                            sem_l[par].inc(sy.dma_start(
                                out=buf.ap()[:, par * S:(par + 1) * S], in_=src), 16)
                        if lm["v3_used"][par] is not None:
                            sm, val = lm["v3_used"][par]
                            sy.wait_ge(sm.h, val)
                        v3l[par].inc(sy.dma_start(
                            out=v3.ap()[:, par * S:(par + 1) * S]
                                .rearrange("p (t u) -> p t u", u=128),
                            in_=v_dram[kv].rearrange("(t p) u -> p t u", p=128)), 16)

                def emit_b3_compute(h):
                    par = h % 2
                    Q3 = q3l[par].v
                    K3 = k3l[par].v
                    V3 = v3l[par].v
                    TBLD = tbl.v
                    st_ev = {}
                    seq2 = [0]
                    last_mm = [None]
                    for kt in range(NQT):
                        q0 = 128 * kt
                        w = S - q0
                        ncs = (w + 511) // 512
                        for cs_ in range(ncs):
                            c0 = q0 + 512 * cs_
                            cw = min(512, S - c0)
                            pidx = seq2[0] % 2
                            seq2[0] += 1

                            @on("tensor")
                            def _(te, kt=kt, cs_=cs_, c0=c0, cw=cw, pidx=pidx, par=par, Q3=Q3, K3=K3):
                                te.wait_ge(q3l[par].h, Q3)
                                te.wait_ge(k3l[par].h, K3)
                                if lm["pstfree"][pidx] is not None:
                                    sm, val = lm["pstfree"][pidx]
                                    te.wait_ge(sm.h, val)
                                inst = te.matmul(
                                    p_st2[pidx][:, 0:cw],
                                    kr3.ap()[:, par * S + 128 * kt: par * S + 128 * (kt + 1)],
                                    qr3.ap()[:, par * S + c0: par * S + c0 + cw],
                                    start=True, stop=True)
                                lm[("mm3", kt, cs_)] = et.inc(inst)
                                last_mm[0] = lm[("mm3", kt, cs_)]

                            @on("scalar")
                            def _(sc, kt=kt, cs_=cs_, c0=c0, cw=cw, pidx=pidx, q0=q0):
                                sc.wait_ge(et.h, lm[("mm3", kt, cs_)])
                                if kt == 0 and cs_ == 0 and lm["av_done"] is not None:
                                    sc.wait_ge(et.h, lm["av_done"])
                                sc.wait_ge(ea.h, ea.v)
                                i1 = sc.activation(
                                    out=st_pack.ap()[:, POFF[kt] + (c0 - q0): POFF[kt] + (c0 - q0) + cw],
                                    in_=p_st2[pidx][:, 0:cw], func=AF.Identity)
                                st_ev[(kt, cs_)] = (ea, ea.inc(i1))
                                lm["pstfree"][pidx] = st_ev[(kt, cs_)]
                    lm["q3_used"][par] = (et, last_mm[0])
                    lm["k3_used"][par] = (et, last_mm[0])

                    pm = {}

                    for kt in range(NQT):
                        q0 = 128 * kt
                        w = S - q0
                        csl = (kt % 2) * S

                        @on("vector")
                        def _(ve, kt=kt, q0=q0, w=w, csl=csl):
                            if kt == 0:
                                ve.wait_ge(tbl.h, TBLD)
                            for cs_ in range((w + 511) // 512):
                                sm, val = st_ev[(kt, cs_)]
                                ve.wait_ge(sm.h, val)
                            if kt >= 2 and pm.get((kt - 2, "pm")) is not None:
                                ve.wait_ge(eg.h, pm[(kt - 2, "pm")])
                            ve.wait_ge(ev.h, ev.v)
                            ev.inc(ve.tensor_tensor(
                                out=cjunk.ap()[:, csl:csl + w],
                                in0=st_pack.ap()[:, POFF[kt]: POFF[kt] + w],
                                in1=tbrz.ap()[:, q0:S], op=ALU.is_ge))
                            ve.wait_ge(ev.h, ev.v)
                            pm[(kt, "msk")] = ev.inc(ve.tensor_tensor(
                                out=cjunk.ap()[:, csl:csl + 128],
                                in0=cjunk.ap()[:, csl:csl + 128],
                                in1=s_caus01[:], op=ALU.mult))

                        @on("scalar")
                        def _(sc, kt=kt, w=w):
                            sc.wait_ge(ea.h, ea.v)
                            pm[(kt, "exp")] = (ea, ea.inc(sc.activation(
                                out=p_pack.ap()[:, POFF[kt]: POFF[kt] + w],
                                in_=st_pack.ap()[:, POFF[kt]: POFF[kt] + w],
                                func=AF.Exp, scale=ISQ)))

                        @on("gpsimd")
                        def _(gp, kt=kt, w=w, csl=csl):
                            gp.wait_ge(ev.h, pm[(kt, "msk")])
                            sm, val = pm[(kt, "exp")]
                            gp.wait_ge(sm.h, val)
                            gp.wait_ge(eg.h, eg.v)
                            pm[(kt, "pm")] = eg.inc(gp.tensor_tensor(
                                out=st_pack.ap()[:, POFF[kt]: POFF[kt] + w],
                                in0=p_pack.ap()[:, POFF[kt]: POFF[kt] + w],
                                in1=cjunk.ap()[:, csl:csl + w], op=ALU.mult))

                    z_ev = {}
                    z_done = {}
                    for s4 in range(4):
                        @on("tensor")
                        def _(te, s4=s4):
                            for kt in range(4 * s4 + 4):
                                te.wait_ge(eg.h, pm[(kt, "pm")])
                            if s4 > 0:
                                sm, val = z_ev[s4 - 1]
                                te.wait_ge(sm.h, val)
                            nkt = 4 * s4 + 4
                            inst = None
                            for kt in range(nkt):
                                q0 = 128 * kt
                                off = 512 * s4 - q0
                                last = (kt == nkt - 1)
                                if off >= 0:
                                    inst = te.matmul(
                                        p_z[:, 0:512], s_onesc[:],
                                        st_pack.ap()[:, POFF[kt] + off: POFF[kt] + off + 512],
                                        start=(kt == 0), stop=last)
                                else:
                                    wv = 512 * (s4 + 1) - q0
                                    inst = te.matmul(
                                        p_z[:, 512 - wv:512], s_onesc[:],
                                        st_pack.ap()[:, POFF[kt]: POFF[kt] + wv],
                                        start=False, stop=last)
                            z_done[s4] = et.inc(inst)

                        @on("scalar")
                        def _(sc, s4=s4):
                            sc.wait_ge(et.h, z_done[s4])
                            sc.wait_ge(ea.h, ea.v)
                            z_ev[s4] = (ea, ea.inc(sc.activation(
                                out=zrow.ap()[:, 512 * s4:512 * s4 + 512],
                                in_=p_z[:, 0:512], func=AF.Ln)))

                    @on("scalar")
                    def _(sc):
                        sm, val = z_ev[3]
                        sc.wait_ge(sm.h, val)
                        sc.wait_ge(ea.h, ea.v)
                        ea.inc(sc.activation(out=zrow[:], in_=zrow[:],
                                             func=AF.Exp, scale=-1.0))
                    RECIP = ea.v
                    if "dz" in dbg:
                        @on("sync")
                        def _(sy, h=h):
                            sy.wait_ge(ea.h, RECIP)
                            qst.inc(sy.dma_start(out=dbg_out["dz"][h:h + 1], in_=zrow[:]), 16)

                    rz_ev = {}
                    rz_tr = {}
                    for s4 in range(4):
                        @on("tensor")
                        def _(te, s4=s4):
                            te.wait_ge(ea.h, RECIP)
                            if "dz" in dbg:
                                te.wait_ge(qst.h, qst.v)
                            if s4 > 0:
                                sm, val = rz_ev[s4 - 1]
                                te.wait_ge(sm.h, val)
                            rz_tr[s4] = et.inc(te.matmul(
                                p_mis[:], s_onesrf[:],
                                zrow.ap()[:, 512 * s4: 512 * s4 + 512],
                                start=True, stop=True))

                        @on("scalar")
                        def _(sc, s4=s4):
                            sc.wait_ge(et.h, rz_tr[s4])
                            sc.wait_ge(ea.h, ea.v)
                            rz_ev[s4] = (ea, ea.inc(sc.activation(
                                out=tbrz.ap()[:, 512 * s4: 512 * s4 + 512],
                                in_=p_mis[:], func=AF.Identity)))

                    norm_mark = {}

                    @on("gpsimd")
                    def _(gp):
                        sm, val = rz_ev[3]
                        gp.wait_ge(sm.h, val)
                        for kt in range(NQT):
                            q0 = 128 * kt
                            w = S - q0
                            gp.wait_ge(eg.h, eg.v)
                            norm_mark[kt] = eg.inc(gp.tensor_tensor(
                                out=p_pack.ap()[:, POFF[kt]: POFF[kt] + w],
                                in0=st_pack.ap()[:, POFF[kt]: POFF[kt] + w],
                                in1=tbrz.ap()[:, q0:S], op=ALU.mult))
                    P_NORM = eg.v
                    lm["tb_used"] = P_NORM

                    av_ev = {}
                    av_tr = {}
                    for s4 in range(4):
                        @on("tensor")
                        def _(te, s4=s4, par=par, V3=V3):
                            te.wait_ge(eg.h, norm_mark[4 * s4 + 3])
                            te.wait_ge(v3l[par].h, V3)
                            if s4 > 0:
                                sm, val = av_ev[s4 - 1]
                                te.wait_ge(sm.h, val)
                            nkt = 4 * s4 + 4
                            inst = None
                            for kt in range(nkt):
                                q0 = 128 * kt
                                off = 512 * s4 - q0
                                last = (kt == nkt - 1)
                                vsl = v3.ap()[:, par * S + kt * 128: par * S + (kt + 1) * 128]
                                if off >= 0:
                                    inst = te.matmul(
                                        p_av[:, 0:512], vsl,
                                        p_pack.ap()[:, POFF[kt] + off: POFF[kt] + off + 512],
                                        start=(kt == 0), stop=last)
                                else:
                                    wv = 512 * (s4 + 1) - q0
                                    inst = te.matmul(
                                        p_av[:, 512 - wv:512], vsl,
                                        p_pack.ap()[:, POFF[kt]: POFF[kt] + wv],
                                        start=False, stop=last)
                            av_tr[s4] = et.inc(inst)

                        @on("scalar")
                        def _(sc, s4=s4):
                            sc.wait_ge(et.h, av_tr[s4])
                            if s4 == 0:
                                sc.wait_ge(ats.h, ats.v)
                            sc.wait_ge(ea.h, ea.v)
                            av_ev[s4] = (ea, ea.inc(sc.activation(
                                out=aT_stg.ap()[:, 512 * s4: 512 * s4 + 512],
                                in_=p_av[:], func=AF.Identity)))
                    lm["av_done"] = av_tr[3]
                    lm["v3_used"][par] = (et, av_tr[3])

                    @on("sync")
                    def _(sy, h=h):
                        sm, val = av_ev[3]
                        sy.wait_ge(sm.h, val)
                        ats.inc(sy.dma_start(
                            out=aT_dram[:, :, h, :].rearrange("t p q -> p t q"),
                            in_=aT_stg.ap().rearrange("p (t q) -> p t q", q=128)), 16)

                emit_b1_loads(0)
                sr = emit_b1_scores(0)
                emit_b1_select(0, sr)
                emit_b3_loads(0)
                for h in range(H):
                    if h + 1 < H:
                        emit_b1_loads(h + 1)
                        sr = emit_b1_scores(h + 1)
                        emit_b3_loads(h + 1)
                    emit_b3_compute(h)
                    if h + 1 < H:
                        emit_b1_select(h + 1, sr)

                if "daT" in dbg:
                    @on("sync")
                    def _(sy):
                        sy.wait_ge(ats.h, ats.v)
                        qst.inc(sy.dma_start(out=dbg_out["daT"], in_=aT_dram), 16)

        if "W" in phases:
            with ExitStack() as wctx:
                aTq = sb("aTq", [128, 4 * NHC * 128], BF16, wctx)
                wo_buf = sb("wo_buf", [128, NHC * 512], BF16, wctx)
                a1g = sb("a1g", [128, 4 * D], F32, wctx)
                xw_buf = sb("xw_buf", [128, 2 * 512], F32, wctx)
                y1_stg = sb("y1_stg", [128, 2 * D], BF16, wctx)
                lnv = sb("lnv", [128, 8], F32, wctx)
                ajunk2 = sb("ajunk2", [128, D], BF16, wctx)
                p_w = [psm(f"p_w{i}", [128, 512], F32, wctx) for i in range(2)]
                phase_barrier()

                wol = mksem("wol")
                aql = mksem("aql")
                xwl = [mksem("xwl0"), mksem("xwl1")]
                y1s = [mksem("y1s0"), mksem("y1s1")]
                all_dma_sems.extend([wol, aql] + xwl + y1s)


                wo_mm = {}
                wo_mm_last = [None]
                xw_free = {}
                pw_free = {0: None, 1: None}
                a1_ev = {}
                seqw = [0]
                for qtg in range(4):
                    qts = [4 * qtg + i for i in range(4)]

                    @on("sync")
                    def _(sy, qtg=qtg, qts=qts):
                        if qtg > 0 and wo_mm_last[0] is not None:
                            sy.wait_ge(et.h, wo_mm_last[0])
                        for i, qt in enumerate(qts):
                            aql.inc(sy.dma_start(
                                out=aTq.ap()[:, i * NHC * 128:(i + 1) * NHC * 128]
                                    .rearrange("p (c q) -> p c q", q=128),
                                in_=aT_dram[qt]), 16)
                        sy.wait_ge(aql.h, aql.v)
                    AQ_LD = aql.v

                    for js in range(8):
                        @on("sync")
                        def _(sy, qtg=qtg, js=js):
                            if wo_mm_last[0] is not None and (qtg > 0 or js > 0):
                                sy.wait_ge(et.h, wo_mm_last[0])
                            sy.wait_ge(stg_st[0].h, stg_st[0].v)
                            sy.wait_ge(stg_st[1].h, stg_st[1].v)
                            wol.inc(sy.dma_start(
                                out=wo_buf.ap().rearrange("p (c j) -> p c j", j=512),
                                in_=wo_bf[js]), 16)
                            sy.wait_ge(wol.h, wol.v)
                        WO_LD = wol.v

                        for i, qt in enumerate(qts):
                            pidx = seqw[0] % 2
                            seqw[0] += 1

                            @on("sync")
                            def _(sy, qt=qt, js=js, pidx=pidx):
                                if xw_free.get(pidx) is not None:
                                    sm, val = xw_free[pidx]
                                    sy.wait_ge(sm.h, val)
                                xwl[pidx].inc(sy.dma_start(
                                    out=xw_buf.ap()[:, pidx * 512:(pidx + 1) * 512],
                                    in_=x_nat[qt * 128:(qt + 1) * 128, js * 512:(js + 1) * 512]), 16)

                            @on("tensor")
                            def _(te, i=i, js=js, pidx=pidx):
                                te.wait_ge(wol.h, WO_LD)
                                te.wait_ge(aql.h, AQ_LD)
                                if pw_free[pidx] is not None:
                                    sm, val = pw_free[pidx]
                                    te.wait_ge(sm.h, val)
                                inst = None
                                for dc in range(NHC):
                                    inst = te.matmul(
                                        p_w[pidx][:],
                                        aTq.ap()[:, i * NHC * 128 + dc * 128: i * NHC * 128 + (dc + 1) * 128],
                                        wo_buf.ap()[:, dc * 512:(dc + 1) * 512],
                                        start=(dc == 0), stop=(dc == NHC - 1))
                                wo_mm[(i, js)] = et.inc(inst)
                                wo_mm_last[0] = wo_mm[(i, js)]

                            @on("vector")
                            def _(ve, i=i, js=js, pidx=pidx):
                                ve.wait_ge(et.h, wo_mm[(i, js)])
                                ve.wait_ge(xwl[pidx].h, xwl[pidx].v)
                                ve.wait_ge(ev.h, ev.v)
                                i1 = ve.tensor_tensor(
                                    out=a1g.ap()[:, i * D + js * 512: i * D + (js + 1) * 512],
                                    in0=p_w[pidx][:],
                                    in1=xw_buf.ap()[:, pidx * 512:(pidx + 1) * 512],
                                    op=ALU.add)
                                lm = ev.inc(i1)
                                pw_free[pidx] = (ev, lm)
                                xw_free[pidx] = (ev, lm)
                                a1_ev[(i, js)] = lm

                    for i, qt in enumerate(qts):
                        @on("scalar")
                        def _(sc, i=i):
                            for js in range(8):
                                sc.wait_ge(ev.h, a1_ev[(i, js)])
                            sc.wait_ge(ea.h, ea.v)
                            a1sl = a1g.ap()[:, i * D:(i + 1) * D]
                            ea.inc(sc.activation(out=ajunk2[:], in_=a1sl,
                                                 func=AF.Identity, accum_out=lnv[:, 0:1]))
                            sc.wait_ge(ea.h, ea.v)
                            ea.inc(sc.activation(out=ajunk2[:], in_=a1sl,
                                                 func=AF.Square, accum_out=lnv[:, 1:2]))
                        STATS = ea.v

                        @on("vector")
                        def _(ve):
                            ve.wait_ge(ea.h, STATS)
                            ve.wait_ge(ev.h, ev.v)
                            ev.inc(ve.tensor_scalar(out=lnv[:, 2:3], in0=lnv[:, 0:1],
                                                    scalar1=1.0 / D, scalar2=None, op0=ALU.mult))
                            ve.wait_ge(ev.h, ev.v)
                            ev.inc(ve.tensor_scalar(out=lnv[:, 3:4], in0=lnv[:, 1:2],
                                                    scalar1=1.0 / D, scalar2=None, op0=ALU.mult))
                            ve.wait_ge(ev.h, ev.v)
                            ev.inc(ve.tensor_tensor(out=lnv[:, 4:5], in0=lnv[:, 2:3],
                                                    in1=lnv[:, 2:3], op=ALU.mult))
                            ve.wait_ge(ev.h, ev.v)
                            ev.inc(ve.tensor_tensor(out=lnv[:, 5:6], in0=lnv[:, 3:4],
                                                    in1=lnv[:, 4:5], op=ALU.subtract))
                            ve.wait_ge(ev.h, ev.v)
                            ev.inc(ve.tensor_scalar(out=lnv[:, 5:6], in0=lnv[:, 5:6],
                                                    scalar1=EPS, scalar2=None, op0=ALU.add))
                            ve.wait_ge(ev.h, ev.v)
                            ev.inc(ve.reciprocal(lnv[:, 6:7], lnv[:, 5:6]))
                        RECV = ev.v

                        @on("scalar")
                        def _(sc):
                            sc.wait_ge(ev.h, RECV)
                            sc.wait_ge(ea.h, ea.v)
                            ea.inc(sc.activation(out=lnv[:, 6:7], in_=lnv[:, 6:7], func=AF.Sqrt))
                        RSTD = ea.v

                        @on("vector")
                        def _(ve):
                            ve.wait_ge(ea.h, RSTD)
                            ve.wait_ge(ev.h, ev.v)
                            ev.inc(ve.tensor_tensor(out=lnv[:, 7:8], in0=lnv[:, 2:3],
                                                    in1=lnv[:, 6:7], op=ALU.mult))
                            ve.wait_ge(ev.h, ev.v)
                            ev.inc(ve.tensor_scalar(out=lnv[:, 7:8], in0=lnv[:, 7:8],
                                                    scalar1=-1.0, scalar2=None, op0=ALU.mult))
                        NBIAS = ev.v

                        @on("scalar")
                        def _(sc, i=i, qt=qt):
                            sc.wait_ge(ev.h, NBIAS)
                            pidx = qt % 2
                            sc.wait_ge(y1s[pidx].h, y1s[pidx].v)
                            sc.wait_ge(ea.h, ea.v)
                            ea.inc(sc.activation(
                                out=y1_stg.ap()[:, pidx * D:(pidx + 1) * D],
                                in_=a1g.ap()[:, i * D:(i + 1) * D],
                                func=AF.Identity, bias=lnv[:, 7:8], scale=lnv[:, 6:7]))
                        Y1A = ea.v

                        @on("sync")
                        def _(sy, qt=qt):
                            sy.wait_ge(ea.h, Y1A)
                            pidx = qt % 2
                            y1s[pidx].inc(sy.dma_start(
                                out=y1_dram[qt * 128:(qt + 1) * 128, :],
                                in_=y1_stg.ap()[:, pidx * D:(pidx + 1) * D]), 16)

                if "dy1" in dbg:
                    @on("sync")
                    def _(sy):
                        for sm in (y1s[0], y1s[1]):
                            sy.wait_ge(sm.h, sm.v)
                        qst.inc(sy.dma_start(out=dbg_out["dy1"], in_=y1_dram), 16)

        if "C" in phases:
            zsg_dram = dri("zsg_dram", [NG_HID, INTER], BF16)
            zsu_dram = dri("zsu_dram", [NG_HID, INTER], BF16)
            zsd_dram = dri("zsd_dram", [NG_INT, D], BF16)
            phase_barrier()
            qwl = [mksem(f"qwl{i}") for i in range(4)]
            srl = mksem("srl")
            w4s_ = [mksem("w4s0"), mksem("w4s1")]
            zql = mksem("zql")
            zst = mksem("zst")
            all_dma_sems.extend(qwl + [srl] + w4s_ + [zql, zst])

            with ExitStack() as c0z:
                z4b = sb("z4b", [NG_INT, INTER], I32, c0z)
                qzb = sb("qzb", [NG_INT, INTER // 8], I32, c0z)
                scb = sb("scb", [NG_INT, INTER], F32, c0z)
                zout = sb("zout", [NG_INT, INTER], BF16, c0z)
                for qz_i, sc_i, zdst, ng, nn in [
                    (g_qz, g_sc, zsg_dram, NG_HID, INTER),
                    (u_qz, u_sc, zsu_dram, NG_HID, INTER),
                    (d_qz, d_sc, zsd_dram, NG_INT, D),
                ]:
                    @on("sync")
                    def _(sy, qz_i=qz_i, sc_i=sc_i, ng=ng, nn=nn):
                        sy.wait_ge(zst.h, zst.v)
                        sy.wait_ge(ev.h, ev.v)
                        zql.inc(sy.dma_start(out=qzb[0:ng, 0:nn // 8], in_=qz_i), 16)
                        zql.inc(sy.dma_start(out=scb[0:ng, 0:nn], in_=sc_i), 16)
                        sy.wait_ge(zql.h, zql.v)
                    ZQ_LD = zql.v

                    @on("vector")
                    def _(ve, ng=ng, nn=nn):
                        ve.wait_ge(zql.h, ZQ_LD)
                        for j in range(8):
                            ve.wait_ge(ev.h, ev.v)
                            ev.inc(ve.tensor_scalar(
                                out=z4b.ap()[0:ng, 0:nn].rearrange("p (n e) -> p n e", e=8)[:, :, j],
                                in0=qzb[0:ng, 0:nn // 8], scalar1=4 * j, scalar2=15,
                                op0=ALU.logical_shift_right, op1=ALU.bitwise_and))
                        ve.wait_ge(ev.h, ev.v)
                        ev.inc(ve.tensor_scalar(
                            out=scb[0:ng, 0:nn], in0=scb[0:ng, 0:nn],
                            scalar1=-1.0, scalar2=None, op0=ALU.mult))
                        ve.wait_ge(ev.h, ev.v)
                        ev.inc(ve.tensor_scalar(
                            out=z4b[0:ng, 0:nn], in0=z4b[0:ng, 0:nn],
                            scalar1=1, scalar2=None, op0=ALU.add))
                        ve.wait_ge(ev.h, ev.v)
                        ev.inc(ve.tensor_tensor(
                            out=zout[0:ng, 0:nn], in0=z4b[0:ng, 0:nn],
                            in1=scb[0:ng, 0:nn], op=ALU.mult))
                    Z_EV = ev.v

                    @on("sync")
                    def _(sy, zdst=zdst, ng=ng, nn=nn):
                        sy.wait_ge(ev.h, Z_EV)
                        zst.inc(sy.dma_start(out=zdst, in_=zout[0:ng, 0:nn]), 16)

            phase_barrier()
            with ExitStack() as c0w:
                qwr = sb("qwr", [128, 2 * 5504], I32, c0w)
                nibb = sb("nibb", [128, 5504], I32, c0w)
                srep = sb("srep", [128, 2 * INTER], BF16, c0w)
                srow = sb("srow", [1, 2 * INTER], BF16, c0w)
                onesr_b = sb("onesr_b", [1, 128], BF16, c0w)
                w4o = sb("w4o", [128, 2 * 5504], BF16, c0w)
                p_sr = [psm(f"p_sr{i}", [128, 512], F32, c0w) for i in range(2)]
                qw_free = {}
                srow_free = {}
                srep_free = {}
                psr_free = {}
                sr_bcast = {}
                sr_done = {}
                seqc = [0]
                seqk = [0]

                @on("vector")
                def _(ve):
                    ve.wait_ge(ev.h, ev.v)
                    ev.inc(ve.memset(onesr_b[:], 1.0))
                ONES_B = ev.v

                def dq_matrix(qw_i, sc_i, wdst, nkc, ncols, gmap, tiled):
                    halves = [(0, 5504), (5504, 5504)] if ncols == INTER else [(0, ncols)]
                    for kc in range(nkc):
                        kpar = seqk[0] % 2
                        seqk[0] += 1

                        @on("sync")
                        def _(sy, kc=kc, sc_i=sc_i, ncols=ncols, gmap=gmap, kpar=kpar):
                            if kc == 0:
                                sy.wait_ge(stg_st[0].h, stg_st[0].v)
                                sy.wait_ge(stg_st[1].h, stg_st[1].v)
                            if srow_free.get(kpar) is not None:
                                sm, val = srow_free[kpar]
                                sy.wait_ge(sm.h, val)
                            srl.inc(sy.dma_start(
                                out=srow[0:1, kpar * INTER: kpar * INTER + ncols],
                                in_=sc_i[gmap[kc]:gmap[kc] + 1, :]), 16)
                        SR_LD = srl.v

                        nch = (ncols + 511) // 512
                        for ch in range(nch):
                            c0_ = 512 * ch
                            cw_ = min(512, ncols - c0_)

                            @on("tensor")
                            def _(te, kpar=kpar, c0_=c0_, cw_=cw_, ch=ch, SR_LD=SR_LD):
                                te.wait_ge(srl.h, SR_LD)
                                if ch == 0:
                                    te.wait_ge(qld.h, CONST_LD)
                                    te.wait_ge(ev.h, ONES_B)
                                pp = ch % 2
                                if psr_free.get(pp) is not None:
                                    sm, val = psr_free[pp]
                                    te.wait_ge(sm.h, val)
                                inst = te.matmul(
                                    p_sr[pp][:, 0:cw_], onesr_b[:],
                                    srow[0:1, kpar * INTER + c0_: kpar * INTER + c0_ + cw_],
                                    start=True, stop=True)
                                sr_bcast[(kpar, ch)] = et.inc(inst)

                            @on("vector")
                            def _(ve, kpar=kpar, c0_=c0_, cw_=cw_, ch=ch, nch=nch):
                                ve.wait_ge(et.h, sr_bcast[(kpar, ch)])
                                if ch == 0 and srep_free.get(kpar) is not None:
                                    sm, val = srep_free[kpar]
                                    ve.wait_ge(sm.h, val)
                                ve.wait_ge(ev.h, ev.v)
                                i1 = ve.tensor_scalar(
                                    out=srep[:, kpar * INTER + c0_: kpar * INTER + c0_ + cw_],
                                    in0=p_sr[ch % 2][:, 0:cw_],
                                    scalar1=s_pw2[:], scalar2=None, op0=ALU.mult)
                                v = ev.inc(i1)
                                psr_free[ch % 2] = (ev, v)
                                if ch == nch - 1:
                                    sr_done[kpar] = v
                                    srow_free[kpar] = (ev, v)
                        SR_ADJ = sr_done[kpar]

                        for off, hw in halves:
                            pidx = seqc[0] % 2
                            vp = seqc[0] % 2
                            seqc[0] += 1

                            @on("sync" if pidx % 2 == 0 else "scalar")
                            def _(eng, kc=kc, off=off, hw=hw, pidx=pidx, qw_i=qw_i):
                                if qw_free.get(pidx) is not None:
                                    sm, val = qw_free[pidx]
                                    eng.wait_ge(sm.h, val)
                                qwl[pidx].inc(eng.dma_start(
                                    out=qwr.ap()[:, pidx * 5504: pidx * 5504 + hw],
                                    in_=qw_i[16 * kc:16 * kc + 16, off:off + hw]
                                        .unsqueeze(1).broadcast_to([16, 8, hw])), 16)

                            @on("vector")
                            def _(ve, off=off, hw=hw, pidx=pidx, vp=vp, kpar=kpar, SR_ADJ=SR_ADJ):
                                ve.wait_ge(qwl[pidx].h, qwl[pidx].v)
                                ve.wait_ge(ev.h, SR_ADJ)
                                ve.wait_ge(w4s_[vp].h, w4s_[vp].v)
                                ve.wait_ge(ev.h, ev.v)
                                ev.inc(ve.tensor_scalar(
                                    out=nibb.ap()[:, 0:hw],
                                    in0=qwr.ap()[:, pidx * 5504: pidx * 5504 + hw],
                                    scalar1=s_maskv[:], scalar2=None,
                                    op0=ALU.bitwise_and))
                                qw_free[pidx] = (ev, ev.v)
                                ve.wait_ge(ev.h, ev.v)
                                i2 = ve.tensor_tensor(
                                    out=w4o.ap()[:, vp * 5504: vp * 5504 + hw],
                                    in0=nibb.ap()[:, 0:hw],
                                    in1=srep[:, kpar * INTER + off: kpar * INTER + off + hw],
                                    op=ALU.mult)
                                srep_free[kpar] = (ev, ev.inc(i2))

                            @on("sync")
                            def _(sy, kc=kc, off=off, hw=hw, vp=vp, kpar=kpar, wdst=wdst, tiled=tiled):
                                sm, val = srep_free[kpar]
                                sy.wait_ge(sm.h, val)
                                if tiled:
                                    ic0 = off // 128
                                    nic = hw // 128
                                    w4s_[vp].inc(sy.dma_start(
                                        out=wdst[ic0:ic0 + nic, :, kc, :]
                                            .rearrange("a p q -> p a q"),
                                        in_=w4o.ap()[:, vp * 5504: vp * 5504 + hw]
                                            .rearrange("p (a q) -> p a q", q=128)), 16)
                                else:
                                    w4s_[vp].inc(sy.dma_start(
                                        out=wdst[128 * kc:128 * kc + 128, :],
                                        in_=w4o.ap()[:, vp * 5504: vp * 5504 + hw]), 16)

                dq_matrix(g_qw, gsc_bf, w4g_dram, NHC, INTER, gmap_g, True)
                dq_matrix(u_qw, usc_bf, w4u_dram, NHC, INTER, gmap_u, True)
                dq_matrix(d_qw, dsc_bf, w4d_dram, NG_INT, D, gmap_d, False)

            if "dw4g" in dbg:
                @on("sync")
                def _(sy):
                    for sm in w4s_:
                        sy.wait_ge(sm.h, sm.v)
                    sy.wait_ge(zst.h, zst.v)
                    qst.inc(sy.dma_start(out=dbg_out["dw4g"], in_=w4g_dram), 16)
                    qst.inc(sy.dma_start(out=dbg_out["dw4d"], in_=w4d_dram), 16)
                    qst.inc(sy.dma_start(out=dbg_out["dzsg"], in_=zsg_dram), 16)
                    qst.inc(sy.dma_start(out=dbg_out["dzsd"], in_=zsd_dram), 16)

            with ExitStack() as cctx:
                phase_barrier()
                y1tl = mksem("y1tl")
                wgl = [mksem("wgl0"), mksem("wgl1")]
                wul = [mksem("wul0"), mksem("wul1")]
                wdl = [mksem(f"wdl{i}") for i in range(4)]
                y1l = [mksem("y1l0"), mksem("y1l1")]
                zgl = [mksem("zgl0"), mksem("zgl1")]
                zdl = mksem("zdl")
                indl = mksem("indl")
                y2s = [mksem("y2s0"), mksem("y2s1")]
                all_dma_sems.extend([y1tl] + wgl + wul + wdl + y1l + zgl + [zdl, indl] + y2s)

                s_indg = sb("s_indg", [128, NHC * NG_HID], BF16, cctx)
                s_indd = sb("s_indd", [128, NIC * NG_INT], BF16, cctx)
                wg_buf = sb("wg_buf", [128, 2 * D], BF16, cctx)
                wu_buf = sb("wu_buf", [128, 2 * D], BF16, cctx)
                wd_buf = sb("wd_buf", [128, 4 * 512], BF16, cctx)
                h_buf = sb("h_buf", [128, NIC * 512], BF16, cctx)
                sg_buf = sb("sg_buf", [128, 512], BF16, cctx)
                gt_s = sb("gt_s", [NG_HID, 512], BF16, cctx)
                gd_s = sb("gd_s", [NG_INT, 512], BF16, cctx)
                zs_buf = sb("zs_buf", [NG_HID, 2 * 2 * 128], BF16, cctx)
                zsd_buf = sb("zsd_buf", [NG_INT, 2 * 512], BF16, cctx)
                y1r_buf = sb("y1r_buf", [128, 2 * 512], BF16, cctx)
                lnv2 = sb("lnv2", [128, 8], F32, cctx)
                p_cg = psm("p_cg", [128, 512], F32, cctx)
                p_cu = psm("p_cu", [128, 512], F32, cctx)
                p_gt = psm("p_gt", [NG_HID, 512], F32, cctx)
                p_gd = psm("p_gd", [NG_INT, 512], F32, cctx)
                p_d4 = [psm(f"p_d{i}", [128, 512], F32, cctx) for i in range(4)]

                @on("sync")
                def _(sy):
                    indl.inc(sy.dma_start(
                        out=s_indg.ap().rearrange("p (c g) -> p c g", g=NG_HID),
                        in_=indg_i.rearrange("(c p) g -> p c g", p=128)), 16)
                    indl.inc(sy.dma_start(
                        out=s_indd.ap().rearrange("p (c g) -> p c g", g=NG_INT),
                        in_=indd_i.rearrange("(c p) g -> p c g", p=128)), 16)
                    sy.wait_ge(indl.h, indl.v)
                IND_LD = indl.v

                c3_done = [None]
                gt_mm = [None]; gt_ev = [None]; gt_free = [None]
                cg_mm = [None]; cu_mm = [None]; sil_ea = [None]
                pcg_free = [None]; pcu_free = [None]
                gd_mm = [None]; gd_ev = [None]
                wd_mm = [None]; dcor_mm = [None]
                wfree = {}; h_done = {}; wdfree = {}; pd_free = {}; y1rfree = {}
                zs_free = {}; zsd_free = {}

                for rs in range(4):
                    a2_ev = {}
                    with ExitStack() as rsctx:
                        c2ctx = ExitStack()
                        y1t_buf = c2ctx.enter_context(nc.sbuf_tensor(f"y1t_{rs}", [128, NHC * 512], BF16))

                        @on("sync")
                        def _(sy, rs=rs):
                            if c3_done[0] is not None:
                                sy.wait_ge(et.h, c3_done[0])
                            for dc in range(NHC):
                                y1tl.inc(sy.dma_start(
                                    out=y1t_buf.ap()[:, dc * 512:(dc + 1) * 512],
                                    in_=y1_dram[rs * 512:(rs + 1) * 512, dc * 128:(dc + 1) * 128],
                                    transpose=True), 16)
                            sy.wait_ge(y1tl.h, y1tl.v)
                        Y1T_LD = y1tl.v

                        @on("tensor")
                        def _(te, rs=rs):
                            te.wait_ge(y1tl.h, Y1T_LD)
                            te.wait_ge(indl.h, IND_LD)
                            if gt_free[0] is not None:
                                sm, val = gt_free[0]
                                te.wait_ge(sm.h, val)
                            inst = None
                            for dc in range(NHC):
                                inst = te.matmul(
                                    p_gt[:], s_indg.ap()[:, dc * NG_HID:(dc + 1) * NG_HID],
                                    y1t_buf.ap()[:, dc * 512:(dc + 1) * 512],
                                    start=(dc == 0), stop=(dc == NHC - 1))
                            gt_mm[0] = et.inc(inst)

                        @on("vector")
                        def _(ve):
                            ve.wait_ge(et.h, gt_mm[0])
                            ve.wait_ge(ev.h, ev.v)
                            gt_ev[0] = ev.inc(ve.tensor_copy(gt_s[:], p_gt[:]))
                        if "dgt" in dbg and rs == 0:
                            @on("sync")
                            def _(sy):
                                sy.wait_ge(ev.h, gt_ev[0])
                                qst.inc(sy.dma_start(out=dbg_out["dgt"], in_=gt_s[:]), 16)

                        for ic in range(NIC):
                            pidx = ic % 2
                            for wl, wbuf, wdr, which in [(wgl, wg_buf, w4g_dram, "g"),
                                                         (wul, wu_buf, w4u_dram, "u")]:
                                @on("sync")
                                def _(sy, ic=ic, pidx=pidx, wl=wl, wbuf=wbuf, wdr=wdr, which=which):
                                    if wfree.get((which, pidx)) is not None:
                                        sm, val = wfree[(which, pidx)]
                                        sy.wait_ge(sm.h, val)
                                    wl[pidx].inc(sy.dma_start(
                                        out=wbuf.ap()[:, pidx * D:(pidx + 1) * D]
                                            .rearrange("p (c q) -> p c q", q=128),
                                        in_=wdr[ic]), 16)

                            @on("sync")
                            def _(sy, ic=ic, pidx=pidx):
                                if zs_free.get(pidx) is not None:
                                    sm, val = zs_free[pidx]
                                    sy.wait_ge(sm.h, val)
                                zgl[pidx].inc(sy.dma_start(
                                    out=zs_buf[0:NG_HID, pidx * 256: pidx * 256 + 128],
                                    in_=zsg_dram[:, ic * 128:(ic + 1) * 128]), 16)
                                zgl[pidx].inc(sy.dma_start(
                                    out=zs_buf[0:NG_HID, pidx * 256 + 128: pidx * 256 + 256],
                                    in_=zsu_dram[:, ic * 128:(ic + 1) * 128]), 16)

                            @on("tensor")
                            def _(te, rs=rs, ic=ic, pidx=pidx):
                                te.wait_ge(wgl[pidx].h, wgl[pidx].v)
                                te.wait_ge(wul[pidx].h, wul[pidx].v)
                                te.wait_ge(zgl[pidx].h, zgl[pidx].v)
                                te.wait_ge(ev.h, gt_ev[0])
                                if pcg_free[0] is not None:
                                    sm, val = pcg_free[0]
                                    te.wait_ge(sm.h, val)
                                inst = None
                                for dc in range(NHC):
                                    inst = te.matmul(
                                        p_cg[:], wg_buf.ap()[:, pidx * D + dc * 128: pidx * D + (dc + 1) * 128],
                                        y1t_buf.ap()[:, dc * 512:(dc + 1) * 512],
                                        start=(dc == 0), stop=False)
                                inst = te.matmul(
                                    p_cg[:], zs_buf[0:NG_HID, pidx * 256: pidx * 256 + 128],
                                    gt_s[:], start=False, stop=True)
                                cg_mm[0] = et.inc(inst)
                                if pcu_free[0] is not None:
                                    sm, val = pcu_free[0]
                                    te.wait_ge(sm.h, val)
                                inst = None
                                for dc in range(NHC):
                                    inst = te.matmul(
                                        p_cu[:], wu_buf.ap()[:, pidx * D + dc * 128: pidx * D + (dc + 1) * 128],
                                        y1t_buf.ap()[:, dc * 512:(dc + 1) * 512],
                                        start=(dc == 0), stop=False)
                                inst = te.matmul(
                                    p_cu[:], zs_buf[0:NG_HID, pidx * 256 + 128: pidx * 256 + 256],
                                    gt_s[:], start=False, stop=True)
                                cu_mm[0] = et.inc(inst)
                                wfree[("g", pidx)] = (et, cg_mm[0])
                                wfree[("u", pidx)] = (et, cu_mm[0])
                                zs_free[pidx] = (et, cu_mm[0])

                            @on("scalar")
                            def _(sc, ic=ic):
                                sc.wait_ge(et.h, cg_mm[0])
                                sc.wait_ge(ea.h, ea.v)
                                sil_ea[0] = ea.inc(sc.activation(out=sg_buf[:], in_=p_cg[:], func=AF.Silu))
                                pcg_free[0] = (ea, sil_ea[0])

                            @on("vector")
                            def _(ve, ic=ic):
                                ve.wait_ge(ea.h, sil_ea[0])
                                ve.wait_ge(et.h, cu_mm[0])
                                ve.wait_ge(ev.h, ev.v)
                                i1 = ve.tensor_tensor(
                                    out=h_buf.ap()[:, ic * 512:(ic + 1) * 512],
                                    in0=sg_buf[:], in1=p_cu[:], op=ALU.mult)
                                hev = ev.inc(i1)
                                pcu_free[0] = (ev, hev)
                                h_done[ic] = hev

                            @on("tensor")
                            def _(te, ic=ic):
                                te.wait_ge(ev.h, h_done[ic])
                                inst = te.matmul(
                                    p_gd[:], s_indd.ap()[:, ic * NG_INT:(ic + 1) * NG_INT],
                                    h_buf.ap()[:, ic * 512:(ic + 1) * 512],
                                    start=(ic == 0), stop=(ic == NIC - 1))
                                gd_mm[0] = et.inc(inst)

                        @on("vector")
                        def _(ve):
                            ve.wait_ge(et.h, gd_mm[0])
                            ve.wait_ge(ev.h, ev.v)
                            gd_ev[0] = ev.inc(ve.tensor_copy(gd_s[:], p_gd[:]))
                            gt_free[0] = (ev, gd_ev[0])

                        if "dh" in dbg and rs == 0:
                            @on("sync")
                            def _(sy):
                                sy.wait_ge(ev.h, ev.v)
                                qst.inc(sy.dma_start(
                                    out=dbg_out["dh"],
                                    in_=h_buf.ap().rearrange("p (a q) -> p a q", q=512)), 16)

                        c2ctx.close()
                        a2g = sb(f"a2g_{rs}", [128, 4 * D], BF16, rsctx)
                        y2_stg = sb(f"y2stg_{rs}", [128, D], F32, rsctx)
                        for js in range(8):
                            @on("sync")
                            def _(sy, js=js):
                                pz = js % 2
                                if zsd_free.get(pz) is not None:
                                    sm, val = zsd_free[pz]
                                    sy.wait_ge(sm.h, val)
                                zdl.inc(sy.dma_start(
                                    out=zsd_buf[0:NG_INT, (js % 2) * 512:(js % 2) * 512 + 512],
                                    in_=zsd_dram[:, js * 512:(js + 1) * 512]), 16)
                            ZD_LD = zdl.v

                            for ic in range(NIC):
                                pidx = ic % 4

                                @on("sync")
                                def _(sy, rs=rs, js=js, ic=ic, pidx=pidx):
                                    if wdfree.get(pidx) is not None:
                                        sm, val = wdfree[pidx]
                                        sy.wait_ge(sm.h, val)
                                    wdl[pidx].inc(sy.dma_start(
                                        out=wd_buf.ap()[:, pidx * 512:(pidx + 1) * 512],
                                        in_=w4d_dram[128 * ic:128 * ic + 128, 512 * js:512 * js + 512]), 16)

                                @on("tensor")
                                def _(te, js=js, ic=ic, pidx=pidx):
                                    te.wait_ge(wdl[pidx].h, wdl[pidx].v)
                                    if ic == 0:
                                        te.wait_ge(ev.h, gd_ev[0])
                                        for rt in range(4):
                                            if pd_free.get(rt) is not None:
                                                sm, val = pd_free[rt]
                                                te.wait_ge(sm.h, val)
                                    inst = None
                                    for rt in range(4):
                                        inst = te.matmul(
                                            p_d4[rt][:],
                                            h_buf.ap()[:, ic * 512 + rt * 128: ic * 512 + (rt + 1) * 128],
                                            wd_buf.ap()[:, pidx * 512:(pidx + 1) * 512],
                                            start=(ic == 0), stop=False)
                                    wd_mm[0] = et.inc(inst)
                                    wdfree[pidx] = (et, wd_mm[0])

                            @on("tensor")
                            def _(te, js=js, ZD_LD=ZD_LD):
                                te.wait_ge(zdl.h, ZD_LD)
                                inst = None
                                for rt in range(4):
                                    inst = te.matmul(
                                        p_d4[rt][:], gd_s[0:NG_INT, rt * 128:(rt + 1) * 128],
                                        zsd_buf[0:NG_INT, (js % 2) * 512:(js % 2) * 512 + 512],
                                        start=False, stop=True)
                                dcor_mm[0] = et.inc(inst)
                                c3_done[0] = dcor_mm[0]
                                zsd_free[js % 2] = (et, dcor_mm[0])

                            for rt in range(4):
                                pidx = (js * 4 + rt) % 2

                                @on("sync")
                                def _(sy, rs=rs, js=js, rt=rt, pidx=pidx):
                                    if y1rfree.get(pidx) is not None:
                                        sm, val = y1rfree[pidx]
                                        sy.wait_ge(sm.h, val)
                                    y1l[pidx].inc(sy.dma_start(
                                        out=y1r_buf.ap()[:, pidx * 512:(pidx + 1) * 512],
                                        in_=y1_dram[rs * 512 + rt * 128: rs * 512 + (rt + 1) * 128,
                                                    512 * js:512 * js + 512]), 16)

                                @on("vector")
                                def _(ve, js=js, rt=rt, pidx=pidx):
                                    ve.wait_ge(et.h, dcor_mm[0])
                                    ve.wait_ge(y1l[pidx].h, y1l[pidx].v)
                                    ve.wait_ge(ev.h, ev.v)
                                    i1 = ve.tensor_tensor(
                                        out=a2g.ap()[:, rt * D + 512 * js: rt * D + 512 * js + 512],
                                        in0=p_d4[rt][:],
                                        in1=y1r_buf.ap()[:, pidx * 512:(pidx + 1) * 512],
                                        op=ALU.add)
                                    lm = ev.inc(i1)
                                    pd_free[rt] = (ev, lm)
                                    y1rfree[pidx] = (ev, lm)
                                    a2_ev[(js, rt)] = lm

                        for rt in range(4):
                            @on("scalar")
                            def _(sc, rt=rt):
                                for js in range(8):
                                    sc.wait_ge(ev.h, a2_ev[(js, rt)])
                                sc.wait_ge(y2s[0].h, y2s[0].v)
                                sc.wait_ge(ea.h, ea.v)
                                a2sl = a2g.ap()[:, rt * D:(rt + 1) * D]
                                ea.inc(sc.activation(out=y2_stg[:], in_=a2sl,
                                                     func=AF.Identity, accum_out=lnv2[:, 0:1]))
                                sc.wait_ge(ea.h, ea.v)
                                ea.inc(sc.activation(out=y2_stg[:], in_=a2sl,
                                                     func=AF.Square, accum_out=lnv2[:, 1:2]))
                            STATS2 = ea.v

                            @on("vector")
                            def _(ve):
                                ve.wait_ge(ea.h, STATS2)
                                ve.wait_ge(ev.h, ev.v)
                                ev.inc(ve.tensor_scalar(out=lnv2[:, 2:3], in0=lnv2[:, 0:1], scalar1=1.0 / D, scalar2=None, op0=ALU.mult))
                                ve.wait_ge(ev.h, ev.v)
                                ev.inc(ve.tensor_scalar(out=lnv2[:, 3:4], in0=lnv2[:, 1:2], scalar1=1.0 / D, scalar2=None, op0=ALU.mult))
                                ve.wait_ge(ev.h, ev.v)
                                ev.inc(ve.tensor_tensor(out=lnv2[:, 4:5], in0=lnv2[:, 2:3], in1=lnv2[:, 2:3], op=ALU.mult))
                                ve.wait_ge(ev.h, ev.v)
                                ev.inc(ve.tensor_tensor(out=lnv2[:, 5:6], in0=lnv2[:, 3:4], in1=lnv2[:, 4:5], op=ALU.subtract))
                                ve.wait_ge(ev.h, ev.v)
                                ev.inc(ve.tensor_scalar(out=lnv2[:, 5:6], in0=lnv2[:, 5:6], scalar1=EPS, scalar2=None, op0=ALU.add))
                                ve.wait_ge(ev.h, ev.v)
                                ev.inc(ve.reciprocal(lnv2[:, 6:7], lnv2[:, 5:6]))
                            RECV2 = ev.v

                            @on("scalar")
                            def _(sc):
                                sc.wait_ge(ev.h, RECV2)
                                sc.wait_ge(ea.h, ea.v)
                                ea.inc(sc.activation(out=lnv2[:, 6:7], in_=lnv2[:, 6:7], func=AF.Sqrt))
                            RSTD2 = ea.v

                            @on("vector")
                            def _(ve):
                                ve.wait_ge(ea.h, RSTD2)
                                ve.wait_ge(ev.h, ev.v)
                                ev.inc(ve.tensor_tensor(out=lnv2[:, 7:8], in0=lnv2[:, 2:3], in1=lnv2[:, 6:7], op=ALU.mult))
                                ve.wait_ge(ev.h, ev.v)
                                ev.inc(ve.tensor_scalar(out=lnv2[:, 7:8], in0=lnv2[:, 7:8], scalar1=-1.0, scalar2=None, op0=ALU.mult))
                            NB2 = ev.v

                            @on("scalar")
                            def _(sc, rt=rt):
                                sc.wait_ge(ev.h, NB2)
                                sc.wait_ge(y2s[0].h, y2s[0].v)
                                sc.wait_ge(ea.h, ea.v)
                                ea.inc(sc.activation(
                                    out=y2_stg[:],
                                    in_=a2g.ap()[:, rt * D:(rt + 1) * D],
                                    func=AF.Identity, bias=lnv2[:, 7:8], scale=lnv2[:, 6:7]))
                            Y2A = ea.v

                            @on("sync")
                            def _(sy, rs=rs, rt=rt):
                                sy.wait_ge(ea.h, Y2A)
                                y2s[0].inc(sy.dma_start(
                                    out=out[rs * 512 + rt * 128: rs * 512 + (rt + 1) * 128, :],
                                    in_=y2_stg[:]), 16)
                        @on("sync")
                        def _(sy):
                            sy.wait_ge(y2s[0].h, y2s[0].v)

        @on("sync")
        def _(sy):
            for sm in [qld] + all_dma_sems:
                if sm.v > 0:
                    sy.wait_ge(sm.h, sm.v)
            if qst.v > 0:
                sy.wait_ge(qst.h, qst.v)

    return nc


def kernel(**inputs):
    hs = np.ascontiguousarray(inputs["L0_attn_hidden_states"])
    Wq = inputs["L0_attn_Wq"]; Wk = inputs["L0_attn_Wk"]
    Wv = inputs["L0_attn_Wv"]; Wo = inputs["L0_attn_Wo"]
    gi_g = np.asarray(inputs["L0_ffn_gate_g_idx"])
    gi_u = np.asarray(inputs["L0_ffn_up_g_idx"])
    gi_d = np.asarray(inputs["L0_ffn_down_g_idx"])

    def tile_groups(gi):
        g2 = gi.reshape(-1, 128)
        assert (g2 == g2[:, :1]).all(), "g_idx not tile-constant"
        return g2[:, 0].astype(int)

    gmap_g, gmap_u, gmap_d = tile_groups(gi_g), tile_groups(gi_u), tile_groups(gi_d)
    t = host_tables()
    nc = build_program(gmap_g, gmap_u, gmap_d)

    base = make_base_inputs(inputs, t, gi_g, gi_d)
    in_maps = []
    for b in range(B):
        xb = np.ascontiguousarray(hs[b])
        m = dict(base)
        m["x_nat"] = xb
        m["x_t"] = np.ascontiguousarray(xb.T)
        in_maps.append(m)

    res = run_bass_kernel_spmd(nc, in_maps, core_ids=list(range(B)))
    return np.stack([np.asarray(r["out"], dtype=np.float32) for r in res.results])


def make_base_inputs(inputs, t, gi_g, gi_d):
    Wq = inputs["L0_attn_Wq"]; Wk = inputs["L0_attn_Wk"]
    Wv = inputs["L0_attn_Wv"]; Wo = inputs["L0_attn_Wo"]
    wqkvT = np.concatenate([Wq.T, Wk.T, Wv.T], axis=1)
    wqkv_t = np.ascontiguousarray(
        wqkvT.reshape(NHC, 128, 48, 128).transpose(2, 1, 0, 3))
    wo_t = np.ascontiguousarray(
        Wo.T.reshape(NHC, 128, 8, 512).transpose(2, 1, 0, 3))
    return {
        "wqkv_t": wqkv_t, "wo_t": wo_t,
        "g_qw": inputs["L0_ffn_gate_qweight"], "g_sc": inputs["L0_ffn_gate_scales"],
        "g_qz": inputs["L0_ffn_gate_qzeros"],
        "u_qw": inputs["L0_ffn_up_qweight"], "u_sc": inputs["L0_ffn_up_scales"],
        "u_qz": inputs["L0_ffn_up_qzeros"],
        "d_qw": inputs["L0_ffn_down_qweight"], "d_sc": inputs["L0_ffn_down_scales"],
        "d_qz": inputs["L0_ffn_down_qzeros"],
        "cosT": t["cosT"], "sinF": t["sinF"], "diagm": t["diagm"], "diagmT": t["diagmT"],
        "caus01": t["caus01"],
        "z0c": t["z0"], "sscale": t["sscale"], "keepc": t["keep"], "cinvl": t["cinvl"],
        "iota8": t["iota8"], "idb": t["idb"], "idf": t["idf"],
        "onesc": t["onesc"], "onesrf": t["onesrf"], "shv": t["shv"],
        "maskv": t["maskv"], "pw2": t["pw2"],
        "ind_g": one_hot_ind(gi_g, NG_HID), "ind_d": one_hot_ind(gi_d, NG_INT),
    }


# version 16
# speedup vs baseline: 1.2320x; 1.0714x over previous
import math
import numpy as np
import ml_dtypes
from contextlib import ExitStack

import concourse.bass as bass
import concourse.mybir as mybir
from concourse.bass_utils import run_bass_kernel_spmd

F32 = mybir.dt.float32
BF16 = mybir.dt.bfloat16
I32 = mybir.dt.int32
AF = mybir.ActivationFunctionType
ALU = mybir.AluOpType
AX = mybir.AxisListType

B, S, D = 8, 2048, 4096
H, HKV, HD = 32, 8, 128
INTER = 11008
KSEL = 205
NEG = -1.0e9
NEGSEL = -2.0e38
ISQ = 1.0 / math.sqrt(HD)
TGT = 208.5
NQT = S // 128
NHC = D // 128
NIC = INTER // 128
NG_HID = D // 128
NG_INT = INTER // 128
EPS = 1.0e-5
BIGM = 2.0e9
NFALSI = 5

SOFF = [0] * (NQT + 1)
for _t in range(1, NQT):
    SOFF[_t + 1] = SOFF[_t] + 128 * (_t + 1)
SPACK = SOFF[NQT]
POFF = [0] * (NQT + 1)
for _k in range(NQT):
    POFF[_k + 1] = POFF[_k] + (S - 128 * _k)
PPACK = POFF[NQT]


class SemC:
    def __init__(self, h):
        self.h = h
        self.v = 0

    def inc(self, inst, k=1):
        inst.then_inc(self.h, k)
        self.v += k
        return self.v


def host_tables():
    t = {}
    inv = 1.0 / 10000.0 ** (np.arange(0, HD, 2, dtype=np.float64) / HD)
    fr = np.outer(np.arange(S, dtype=np.float64), inv)
    emb = np.concatenate([fr, fr], axis=-1)
    t["cosT"] = np.ascontiguousarray(np.cos(emb).T.astype(np.float32))
    sinF = np.sin(emb).T.astype(np.float32)
    sinF[:64] *= -1.0
    t["sinF"] = np.ascontiguousarray(sinF)

    ql = np.arange(128)[:, None]
    kl = np.arange(128)[None, :]
    t["diagm"] = np.where(kl <= ql, 0.0, NEG).astype(np.float32)
    t["diagmT"] = np.ascontiguousarray(t["diagm"].T)
    t["caus01"] = np.ascontiguousarray(
        (t["diagmT"] == 0.0).astype(np.float32)).astype(ml_dtypes.bfloat16)

    from scipy.special import ndtri
    z0 = np.zeros((128, NQT - 1), np.float32)
    sscale = np.zeros((128, NQT - 1), np.float32)
    keep = np.zeros((128, NQT - 1), np.float32)
    for tt in range(1, NQT):
        valid = 128 * tt + np.arange(128) + 1.0
        f = np.clip(KSEL / valid, 1e-4, 0.9999)
        zz = ndtri(1.0 - f)
        z0[:, tt - 1] = zz
        phi = np.exp(-zz * zz / 2.0) / math.sqrt(2 * math.pi)
        sscale[:, tt - 1] = 0.75 / np.maximum(valid * phi, 1e-6)
        keep[:, tt - 1] = (valid <= KSEL).astype(np.float32)
    t["z0"] = z0
    t["sscale"] = sscale
    t["keep"] = keep
    t["cinvl"] = np.repeat((1.2533141 / (128.0 * np.arange(1, NQT))
                            ).astype(np.float32)[None, :], 128, axis=0)
    t["iota8"] = np.ascontiguousarray(
        np.tile(np.arange(1, 9, dtype=np.float32)[None, :], (128, NQT - 1)))
    t["idb"] = np.eye(128, dtype=np.float32).astype(ml_dtypes.bfloat16)
    t["idf"] = np.eye(128, dtype=np.float32)
    t["onesc"] = np.ones((128, 1), np.float32).astype(ml_dtypes.bfloat16)
    t["onesrf"] = np.ones((1, 128), np.float32)
    shv = (4 * (np.arange(128, dtype=np.int32) % 8))[:, None].astype(np.int32)
    t["shv"] = shv
    t["maskv"] = (np.int32(15) << shv).astype(np.int32)
    t["pw2"] = (2.0 ** (-shv.astype(np.float64))).astype(np.float32)
    return t


def one_hot_ind(g_idx, ngroups):
    n = g_idx.shape[0]
    ind = np.zeros((n, ngroups), np.float32)
    ind[np.arange(n), np.asarray(g_idx, dtype=int)] = 1.0
    return ind.astype(ml_dtypes.bfloat16)


def build_program(gmap_g, gmap_u, gmap_d, phases=("A", "B", "W", "C"), dbg=(), a_rs=4, b_grps=4):
    nc = bass.Bass("TRN2", target_bir_lowering=False, debug=False)

    ein = lambda name, shape, dt: nc.dram_tensor(name, shape, dt, kind="ExternalInput").ap()
    x_nat = ein("x_nat", [S, D], F32)
    x_t = ein("x_t", [D, S], F32)
    wqkv_t = ein("wqkv_t", [48, 128, NHC, 128], F32)
    wo_t = ein("wo_t", [8, 128, NHC, 512], F32)
    g_qw = ein("g_qw", [D // 8, INTER], I32)
    g_sc = ein("g_sc", [NG_HID, INTER], F32)
    g_qz = ein("g_qz", [NG_HID, INTER // 8], I32)
    u_qw = ein("u_qw", [D // 8, INTER], I32)
    u_sc = ein("u_sc", [NG_HID, INTER], F32)
    u_qz = ein("u_qz", [NG_HID, INTER // 8], I32)
    d_qw = ein("d_qw", [INTER // 8, D], I32)
    d_sc = ein("d_sc", [NG_INT, D], F32)
    d_qz = ein("d_qz", [NG_INT, D // 8], I32)
    cosT_i = ein("cosT", [128, S], F32)
    sinF_i = ein("sinF", [128, S], F32)
    diagm_i = ein("diagm", [128, 128], F32)
    diagmT_i = ein("diagmT", [128, 128], F32)
    caus01_i = ein("caus01", [128, 128], BF16)
    z0_i = ein("z0c", [128, NQT - 1], F32)
    sscale_i = ein("sscale", [128, NQT - 1], F32)
    keep_i = ein("keepc", [128, NQT - 1], F32)
    cinvl_i = ein("cinvl", [128, NQT - 1], F32)
    iota8_i = ein("iota8", [128, 8 * (NQT - 1)], F32)
    idb_i = ein("idb", [128, 128], BF16)
    idf_i = ein("idf", [128, 128], F32)
    onesc_i = ein("onesc", [128, 1], BF16)
    onesrf_i = ein("onesrf", [1, 128], F32)
    shv_i = ein("shv", [128, 1], I32)
    maskv_i = ein("maskv", [128, 1], I32)
    pw2_i = ein("pw2", [128, 1], F32)
    indg_i = ein("ind_g", [D, NG_HID], BF16)
    indd_i = ein("ind_d", [INTER, NG_INT], BF16)

    out = nc.dram_tensor("out", [S, D], F32, kind="ExternalOutput").ap()
    dbg_out = {}
    def dtap(nm, shape, dtt):
        if nm in dbg:
            dbg_out[nm] = nc.dram_tensor(nm, shape, dtt, kind="ExternalOutput").ap()
        return dbg_out.get(nm)
    dtap("dq", [H, 128, S], BF16)
    dtap("dk", [HKV, 128, S], BF16)
    dtap("dv", [HKV, S, 128], BF16)
    dtap("dtau", [H, S], F32)
    dtap("dz", [H, S], F32)
    dtap("daT", [NQT, D, 128], BF16)
    dtap("dy1", [S, D], BF16)
    dtap("dw4g", [NIC, D, 128], BF16)
    dtap("dw4d", [INTER, D], BF16)
    dtap("dzsg", [NG_HID, INTER], F32)
    dtap("dzsd", [NG_INT, D], F32)
    dtap("dgt", [NG_HID, 512], F32)
    dtap("dh", [NIC, 128, 512], BF16)

    dri = lambda name, shape, dt: nc.dram_tensor(name, shape, dt, kind="Internal").ap()
    x_bf = dri("x_bf", [D, S], BF16)
    wqkv_bf = dri("wqkv_bf", [48, 128, NHC, 128], BF16)
    wo_bf = dri("wo_bf", [8, 128, NHC, 512], BF16)
    gsc_bf = dri("gsc_bf", [NG_HID, INTER], BF16)
    usc_bf = dri("usc_bf", [NG_HID, INTER], BF16)
    dsc_bf = dri("dsc_bf", [NG_INT, D], BF16)
    qr_dram = dri("qr_dram", [H, 128, S], BF16)
    kr_dram = dri("kr_dram", [HKV, 128, S], BF16)
    v_dram = dri("v_dram", [HKV, S, 128], BF16)
    tau_dram = dri("tau_dram", [H, S], F32)
    aT_dram = dri("aT_dram", [NQT, 128, H, 128], BF16)
    y1_dram = dri("y1_dram", [S, D], BF16)
    a1_dram = dri("a1_dram", [S, D], F32)
    w4g_dram = dri("w4g_dram", [NIC, 128, NHC, 128], BF16)
    w4u_dram = dri("w4u_dram", [NIC, 128, NHC, 128], BF16)
    w4d_dram = dri("w4d_dram", [INTER, D], BF16)

    with ExitStack() as ctx:
        def sb(name, shape, dt, c=None):
            return (c or ctx).enter_context(nc.sbuf_tensor(name, shape, dt))

        def psm(name, shape, dt=F32, c=None):
            return (c or ctx).enter_context(nc.psum_tensor(name, shape, dt))

        def mksem(name):
            return SemC(ctx.enter_context(nc.semaphore(name)))

        s_diagm = sb("s_diagm", [128, 128], F32)
        s_diagmT = sb("s_diagmT", [128, 128], F32)
        s_caus01 = sb("s_caus01", [128, 128], BF16)
        s_z0 = sb("s_z0", [128, NQT - 1], F32)
        s_sscale = sb("s_sscale", [128, NQT - 1], F32)
        s_keep = sb("s_keep", [128, NQT - 1], F32)
        s_cinvl = sb("s_cinvl", [128, NQT - 1], F32)
        s_iota8 = sb("s_iota8", [128, 8 * (NQT - 1)], F32)
        s_idb = sb("s_idb", [128, 128], BF16)
        s_idf = sb("s_idf", [128, 128], F32)
        s_onesc = sb("s_onesc", [128, 1], BF16)
        s_onesrf = sb("s_onesrf", [1, 128], F32)
        s_shv = sb("s_shv", [128, 1], I32)
        s_maskv = sb("s_maskv", [128, 1], I32)
        s_pw2 = sb("s_pw2", [128, 1], F32)

        all_dma_sems = []
        qld = mksem("qld")
        xts = mksem("xts")
        wts = [mksem("wts0"), mksem("wts1")]
        sts = [mksem("sts0"), mksem("sts1")]
        vts = [mksem("vts0"), mksem("vts1")]
        et = mksem("et")
        ev = mksem("ev")
        ea = mksem("ea")
        eg = mksem("eg")
        qst = mksem("qst")
        all_dma_sems.extend([xts, wts[0], wts[1], sts[0], sts[1], vts[0], vts[1]])

        block = ctx.enter_context(nc.Block())

        def on(engname):
            def deco(f):
                getattr(block, engname)(f)
                return None
            return deco

        def phase_barrier():
            marks = [(sm, sm.v) for sm in all_dma_sems + [qld, et, ev, ea, eg, qst]]
            for engname in ("sync", "gpsimd", "tensor", "vector", "scalar"):
                @on(engname)
                def _(eng):
                    for sm, val in marks:
                        if val > 0:
                            eng.wait_ge(sm.h, val)

        @on("sync")
        def _(sy):
            for dst, src in [(s_diagm, diagm_i), (s_diagmT, diagmT_i),
                             (s_caus01, caus01_i),
                             (s_z0, z0_i), (s_sscale, sscale_i), (s_keep, keep_i),
                             (s_cinvl, cinvl_i), (s_iota8, iota8_i),
                             (s_idb, idb_i), (s_idf, idf_i), (s_onesc, onesc_i),
                             (s_onesrf, onesrf_i), (s_shv, shv_i),
                             (s_maskv, maskv_i), (s_pw2, pw2_i)]:
                qld.inc(sy.dma_start(out=dst[:], in_=src), 16)
            sy.wait_ge(qld.h, qld.v)
        CONST_LD = qld.v

        stg_st = [mksem("stgs0"), mksem("stgs1")]
        stg_in = [mksem("stgi0"), mksem("stgi1")]
        all_dma_sems.extend(stg_st + stg_in)

        with ExitStack() as stgctx:
            stg_buf = stgctx.enter_context(nc.sbuf_tensor("stg_buf", [128, 2 * 16384], BF16))
            jobs = []
            for cc in range(32):
                jobs.append((x_t.rearrange("(a p) n -> a p n", p=128)[cc],
                             x_bf.rearrange("(a p) n -> a p n", p=128)[cc]))
            for ic in range(48):
                jobs.append((wqkv_t[ic].rearrange("p a q -> p (a q)"),
                             wqkv_bf[ic].rearrange("p a q -> p (a q)")))
            for cc in range(8):
                jobs.append((wo_t[cc].rearrange("p a q -> p (a q)"),
                             wo_bf[cc].rearrange("p a q -> p (a q)")))
            jobs.append((g_sc, gsc_bf))
            jobs.append((u_sc, usc_bf))
            jobs.append((d_sc, dsc_bf))
            XW_JOBS = 32 + 48

            for ji, (srcap, dstap) in enumerate(jobs):
                par = ji % 2
                npart = srcap.shape[0]
                ncol = srcap.shape[1]

                @on("gpsimd")
                def _(gp, ji=ji, par=par, srcap=srcap, npart=npart, ncol=ncol):
                    if ji >= 2:
                        gp.wait_ge(stg_st[par].h, 16 * (ji // 2))
                    stg_in[par].inc(gp.dma_start(
                        out=stg_buf.ap()[0:npart, par * 16384: par * 16384 + ncol],
                        in_=srcap), 16)

                @on("sync")
                def _(sy, ji=ji, par=par, dstap=dstap, npart=npart, ncol=ncol):
                    sy.wait_ge(stg_in[par].h, stg_in[par].v)
                    stg_st[par].inc(sy.dma_start(
                        out=dstap,
                        in_=stg_buf.ap()[0:npart, par * 16384: par * 16384 + ncol]), 16)
                    if ji == XW_JOBS - 1:
                        sy.wait_ge(stg_st[0].h, 16 * ((XW_JOBS + 1) // 2))
                        sy.wait_ge(stg_st[1].h, 16 * (XW_JOBS // 2))
        XW_STAGED = (16 * ((XW_JOBS + 1) // 2), 16 * (XW_JOBS // 2))
        ALL_STAGED = None

        if "A" in phases:
            with ExitStack() as actx:
                s_cos = sb("s_cos", [128, S], F32, actx)
                s_sinF = sb("s_sinF", [128, S], F32, actx)
                xt_buf = sb("xt_buf", [128, NHC * 512], BF16, actx)
                wt_buf = sb("wt_buf", [128, 2 * D], BF16, actx)
                ta = sb("ta", [128, 512], F32, actx)
                tb = sb("tb", [128, 512], F32, actx)
                stg = sb("stg", [128, 2 * 512], BF16, actx)
                vstg = sb("vstg", [128, 2 * 512], BF16, actx)
                vout = sb("vout", [128, 2 * 512], BF16, actx)
                p_pr = [psm(f"p_pr{i}", [128, 512], F32, actx) for i in range(4)]
                p_vt = psm("p_vt", [128, 128], BF16, actx)

                NICA = H + 2 * HKV


                @on("sync")
                def _(sy):
                    for sm in (stg_st[0], stg_st[1], stg_in[0], stg_in[1]):
                        sy.wait_ge(sm.h, sm.v)
                    qld.inc(sy.dma_start(out=s_cos[:], in_=cosT_i), 16)
                    qld.inc(sy.dma_start(out=s_sinF[:], in_=sinF_i), 16)
                    sy.wait_ge(qld.h, qld.v)
                COS_LD = qld.v

                mm_done = {}
                evac_done = {}
                vstg_free = {0: None, 1: None}
                vt_tr = {}
                vt_evac = {}
                vt_prev = [None]

                for rs in range(a_rs):
                    @on("scalar")
                    def _(sc, rs=rs):
                        if rs == 0:
                            sc.wait_ge(qld.h, CONST_LD)
                            sc.wait_ge(stg_st[0].h, stg_st[0].v)
                            sc.wait_ge(stg_st[1].h, stg_st[1].v)
                            sc.wait_ge(stg_in[0].h, stg_in[0].v)
                            sc.wait_ge(stg_in[1].h, stg_in[1].v)
                        else:
                            sc.wait_ge(et.h, mm_done[rs * NICA - 1])
                        xts.inc(sc.dma_start(
                            out=xt_buf.ap().rearrange("p (c n) -> p c n", n=512),
                            in_=x_bf[:, rs * 512:(rs + 1) * 512]
                                .rearrange("(c p) n -> p c n", p=128)), 16)

                    for ic in range(NICA):
                        u = rs * NICA + ic

                        @on("scalar")
                        def _(sc, u=u, ic=ic):
                            if u >= 2:
                                sc.wait_ge(et.h, mm_done[u - 2])
                            wts[u % 2].inc(sc.dma_start(
                                out=wt_buf.ap()[:, (u % 2) * D:(u % 2 + 1) * D]
                                    .rearrange("p (c i) -> p c i", i=128),
                                in_=wqkv_bf[ic]), 16)

                        @on("tensor")
                        def _(te, rs=rs, u=u):
                            te.wait_ge(xts.h, 16 * (rs + 1))
                            te.wait_ge(wts[u % 2].h, wts[u % 2].v)
                            if u >= 4:
                                sm, val = evac_done[u - 4]
                                te.wait_ge(sm.h, val)
                            pb = p_pr[u % 4]
                            inst = None
                            for dc in range(NHC):
                                inst = te.matmul(
                                    pb[:], wt_buf.ap()[:, (u % 2) * D + dc * 128:(u % 2) * D + (dc + 1) * 128],
                                    xt_buf.ap()[:, dc * 512:(dc + 1) * 512],
                                    start=(dc == 0), stop=(dc == NHC - 1))
                            mm_done[u] = et.inc(inst)

                        if ic < H + HKV:
                            @on("vector")
                            def _(ve, rs=rs, u=u):
                                pb = p_pr[u % 4]
                                ve.wait_ge(et.h, mm_done[u])
                                ve.wait_ge(qld.h, COS_LD)
                                ve.wait_ge(sts[u % 2].h, sts[u % 2].v)
                                ve.wait_ge(ev.h, ev.v)
                                cs = s_cos.ap()[:, rs * 512:(rs + 1) * 512]
                                sn = s_sinF.ap()[:, rs * 512:(rs + 1) * 512]
                                ev.inc(ve.tensor_tensor(out=ta[:], in0=pb[:], in1=cs, op=ALU.mult))
                                ve.tensor_tensor(out=tb[0:64, :], in0=pb[64:128, :],
                                                 in1=sn[0:64, :], op=ALU.mult)
                                ev.inc(ve.tensor_tensor(out=tb[64:128, :], in0=pb[0:64, :],
                                                        in1=sn[64:128, :], op=ALU.mult))
                                ve.wait_ge(ev.h, ev.v)
                                i4 = ve.tensor_tensor(out=stg.ap()[:, (u % 2) * 512:(u % 2 + 1) * 512], in0=ta[:],
                                                      in1=tb[:], op=ALU.add)
                                evac_done[u] = (ev, ev.inc(i4))

                            @on("sync")
                            def _(sy, rs=rs, ic=ic, u=u):
                                sm, val = evac_done[u]
                                sy.wait_ge(sm.h, val)
                                if ic < H:
                                    dst = qr_dram[ic, :, rs * 512:(rs + 1) * 512]
                                else:
                                    dst = kr_dram[ic - H, :, rs * 512:(rs + 1) * 512]
                                sts[u % 2].inc(sy.dma_start(out=dst, in_=stg.ap()[:, (u % 2) * 512:(u % 2 + 1) * 512]), 16)
                        else:
                            kv = ic - H - HKV

                            @on("scalar")
                            def _(sc, u=u):
                                pb = p_pr[u % 4]
                                sc.wait_ge(et.h, mm_done[u])
                                if vstg_free[u % 2] is not None:
                                    sc.wait_ge(et.h, vstg_free[u % 2])
                                i1 = sc.copy(vstg.ap()[:, (u % 2) * 512:(u % 2 + 1) * 512], pb[:])
                                evac_done[u] = (ea, ea.inc(i1))

                            for j4 in range(4):
                                @on("tensor")
                                def _(te, u=u, j4=j4):
                                    if j4 == 0:
                                        sm, val = evac_done[u]
                                        te.wait_ge(sm.h, val)
                                    prev = vt_evac.get((u, j4 - 1), vt_prev[0])
                                    if prev is not None:
                                        te.wait_ge(ev.h, prev)
                                    inst = te.transpose(
                                        p_vt[:], vstg.ap()[:, (u % 2) * 512 + j4 * 128:(u % 2) * 512 + (j4 + 1) * 128],
                                        s_idb[:])
                                    vt_tr[(u, j4)] = et.inc(inst)

                                @on("vector")
                                def _(ve, u=u, j4=j4):
                                    if j4 == 0:
                                        ve.wait_ge(vts[u % 2].h, vts[u % 2].v)
                                    ve.wait_ge(et.h, vt_tr[(u, j4)])
                                    i1 = ve.tensor_copy(
                                        vout.ap()[:, (u % 2) * 512 + j4 * 128:(u % 2) * 512 + (j4 + 1) * 128], p_vt[:])
                                    vt_evac[(u, j4)] = ev.inc(i1)
                            vstg_free[u % 2] = vt_tr[(u, 3)]
                            vt_prev[0] = vt_evac[(u, 3)]

                            @on("sync")
                            def _(sy, u=u, kv=kv, rs=rs):
                                sy.wait_ge(ev.h, vt_evac[(u, 3)])
                                vts[u % 2].inc(sy.dma_start(
                                    out=v_dram[kv, rs * 512:(rs + 1) * 512, :]
                                        .rearrange("(a k) h -> k a h", k=128),
                                    in_=vout.ap()[:, (u % 2) * 512:(u % 2 + 1) * 512].rearrange("k (a h) -> k a h", h=128)), 16)


                if "dq" in dbg:
                    @on("sync")
                    def _(sy):
                        for sm in (sts[0], sts[1], vts[0], vts[1]):
                            sy.wait_ge(sm.h, sm.v)
                        qst.inc(sy.dma_start(out=dbg_out["dq"], in_=qr_dram), 16)
                        qst.inc(sy.dma_start(out=dbg_out["dk"], in_=kr_dram), 16)
                        qst.inc(sy.dma_start(out=dbg_out["dv"], in_=v_dram), 16)


        if "B" in phases:
            with ExitStack() as bctx:
                qr1 = sb("qr1", [128, 2 * S], BF16, bctx)
                kr1 = sb("kr1", [128, 2 * S], BF16, bctx)
                qr3 = sb("qr3", [128, 2 * S], BF16, bctx)
                kr3 = sb("kr3", [128, 2 * S], BF16, bctx)
                v3 = sb("v3", [128, 2 * S], BF16, bctx)
                tbrz = sb("tbrz", [128, S], F32, bctx)
                cjunk = sb("cjunk", [128, 2 * S], BF16, bctx)
                d2 = sb("d2", [128, 2 * S], BF16, bctx)
                exb = sb("exb", [128, S], BF16, bctx)
                aT_stg = sb("aT_stg", [128, S], BF16, bctx)
                zrow = sb("zrow", [1, S], F32, bctx)
                taub = sb("taub", [128, 16], F32, bctx)
                taubT = sb("taubT", [16, 128], F32, bctx)
                sel_names = ["sabs", "sig", "rstep", "tauc", "cntc", "ta_", "ca_",
                             "tb_", "cb_", "hava", "havb", "illa", "illb",
                             "blo_t", "blo_c", "bhi_t", "bhi_c",
                             "tmp0", "tmp1", "tmp2", "tmp3", "jl", "jh",
                             "taulo", "tauhi", "tstar", "ones15", "negs15"]
                SEL = {nm: sb("sel_" + nm, [128, NQT - 1], F32, bctx) for nm in sel_names}
                SELI = {nm: sb("seli_" + nm, [128, NQT - 1], I32, bctx)
                        for nm in ["mi", "mi2"]}
                m8lo = sb("m8lo", [128, 8 * (NQT - 1)], F32, bctx)
                m8hi = sb("m8hi", [128, 8 * (NQT - 1)], F32, bctx)
                ind120 = sb("ind120", [128, 8 * (NQT - 1)], F32, bctx)
                vsel = sb("vsel", [128, 8 * (NQT - 1)], F32, bctx)
                S_pack = sb("S_pack", [128, SPACK], BF16, bctx)
                st_pack = sb("st_pack", [128, PPACK], BF16, bctx)
                p_pack = sb("p_pack", [128, PPACK], BF16, bctx)

                p_s = [psm(f"p_s{i}", [128, 512], F32, bctx) for i in range(2)]
                p_st2 = [psm(f"p_st{i}", [128, 512], F32, bctx) for i in range(2)]
                p_z = psm("p_z", [1, 512], F32, bctx)
                p_av = psm("p_av", [128, 512], F32, bctx)
                p_mis = psm("p_mis", [128, 512], F32, bctx)
                p_tr2 = psm("p_tr2", [16, 128], F32, bctx)
                phase_barrier()

                q1l = [mksem("q1l0"), mksem("q1l1")]
                k1l = [mksem("k1l0"), mksem("k1l1")]
                q3l = [mksem("q3l0"), mksem("q3l1")]
                k3l = [mksem("k3l0"), mksem("k3l1")]
                v3l = [mksem("v3l0"), mksem("v3l1")]
                tbl = mksem("tbl")
                tbo = mksem("tbo")
                ats = mksem("ats")
                all_dma_sems.extend(q1l + k1l + q3l + k3l + v3l + [tbl, tbo, ats])

                GTGT = TGT
                E = SEL

                lm = {
                    "sel_done": None,
                    "b3dve_done": None,
                    "av_done": None,
                    "atstore": None,
                    "tau_tr": None,
                    "tb_used": None,
                    "q1_used": {0: None, 1: None},
                    "k1_used": {0: None, 1: None},
                    "q3_used": {0: None, 1: None},
                    "k3_used": {0: None, 1: None},
                    "v3_used": {0: None, 1: None},
                    "stev": {},
                    "sevac": {},
                    "psfree": {0: None, 1: None},
                    "pstfree": {0: None, 1: None},
                }

                def vsec(fns):
                    @on("vector")
                    def _(ve):
                        for fn in fns:
                            ve.wait_ge(ev.h, ev.v)
                            ev.inc(fn(ve))
                    return ev.v

                def emit_b1_loads(h):
                    par = h % 2
                    kv = h // 4

                    @on("sync")
                    def _(sy, h=h, par=par, kv=kv):
                        if lm["q1_used"][par] is not None:
                            sm, val = lm["q1_used"][par]
                            sy.wait_ge(sm.h, val)
                        q1l[par].inc(sy.dma_start(
                            out=qr1.ap()[:, par * S:(par + 1) * S], in_=qr_dram[h]), 16)
                        if lm["k1_used"][par] is not None:
                            sm, val = lm["k1_used"][par]
                            sy.wait_ge(sm.h, val)
                        k1l[par].inc(sy.dma_start(
                            out=kr1.ap()[:, par * S:(par + 1) * S], in_=kr_dram[kv]), 16)

                def emit_b1_scores(h):
                    par = h % 2
                    Q1 = q1l[par].v
                    K1 = k1l[par].v
                    sevac = {}
                    seq = [0]
                    last_mm = [None]
                    for t in range(1, NQT):
                        L = 128 * (t + 1)
                        nks = (L + 511) // 512
                        for ks in range(nks):
                            c0 = 512 * ks
                            cw = min(512, L - c0)
                            pidx = seq[0] % 2
                            seq[0] += 1

                            @on("tensor")
                            def _(te, t=t, ks=ks, c0=c0, cw=cw, pidx=pidx, par=par, Q1=Q1, K1=K1):
                                te.wait_ge(q1l[par].h, Q1)
                                te.wait_ge(k1l[par].h, K1)
                                if lm["psfree"][pidx] is not None:
                                    sm, val = lm["psfree"][pidx]
                                    te.wait_ge(sm.h, val)
                                inst = te.matmul(
                                    p_s[pidx][:, 0:cw],
                                    qr1.ap()[:, par * S + 128 * t: par * S + 128 * (t + 1)],
                                    kr1.ap()[:, par * S + c0: par * S + c0 + cw],
                                    start=True, stop=True)
                                lm[("mm1", t, ks)] = et.inc(inst)
                                last_mm[0] = lm[("mm1", t, ks)]

                            @on("scalar")
                            def _(sc, t=t, ks=ks, c0=c0, cw=cw, pidx=pidx):
                                sc.wait_ge(et.h, lm[("mm1", t, ks)])
                                if t == 1 and ks == 0 and lm["sel_done"] is not None:
                                    sc.wait_ge(ev.h, lm["sel_done"])
                                sc.wait_ge(ea.h, ea.v)
                                i1 = sc.copy(
                                    S_pack.ap()[:, SOFF[t] + c0: SOFF[t] + c0 + cw],
                                    p_s[pidx][:, 0:cw])
                                sevac[(t, ks)] = (ea, ea.inc(i1))
                                lm["psfree"][pidx] = sevac[(t, ks)]
                    lm["q1_used"][par] = (et, last_mm[0])
                    lm["k1_used"][par] = (et, last_mm[0])

                    @on("vector")
                    def _(ve):
                        ve.wait_ge(ev.h, ev.v)
                        for t in range(1, NQT):
                            L = 128 * (t + 1)
                            sm, val = sevac[(t, (L - 1) // 512)]
                            ve.wait_ge(sm.h, val)
                            sl = S_pack.ap()[:, SOFF[t] + L - 128: SOFF[t] + L]
                            ev.inc(ve.tensor_tensor(out=sl, in0=sl, in1=s_diagm[:], op=ALU.add))
                    DIAG_ALL = ev.v

                    @on("scalar")
                    def _(sc):
                        for t in range(1, NQT):
                            for ks in range((128 * t + 511) // 512):
                                sm, val = sevac[(t, ks)]
                                sc.wait_ge(sm.h, val)
                            sc.wait_ge(ea.h, ea.v)
                            ea.inc(sc.activation(
                                out=d2.ap()[:, S:S + 128 * t],
                                in_=S_pack.ap()[:, SOFF[t]: SOFF[t] + 128 * t],
                                func=AF.Abs, accum_out=SEL["sabs"][:, t - 1:t]))
                    SABS_DONE = ea.v
                    return sevac, DIAG_ALL, SABS_DONE

                def emit_b1_select(h, scores_ret):
                    sevac, DIAG_ALL, SABS_DONE = scores_ret

                    def counts(round_no):
                        @on("vector")
                        def _(ve):
                            if round_no == 0:
                                ve.wait_ge(ea.h, SABS_DONE)
                                ve.wait_ge(ev.h, DIAG_ALL)
                            ve.wait_ge(ev.h, ev.v)
                            for t in range(1, NQT):
                                L = 128 * (t + 1)
                                ev.inc(ve.tensor_scalar(
                                    out=d2.ap()[:, 0:L],
                                    in0=S_pack.ap()[:, SOFF[t]: SOFF[t] + L],
                                    scalar1=SEL["tauc"][:, t - 1:t], scalar2=0.0,
                                    op0=ALU.is_ge, op1=ALU.add,
                                    accum_out=SEL["cntc"][:, t - 1:t]))

                    def update_best():
                        vsec([
                            lambda ve: ve.tensor_scalar(out=E["tmp0"][:], in0=E["cntc"][:], scalar1=KSEL - 0.5, scalar2=None, op0=ALU.is_ge),
                            lambda ve: ve.tensor_tensor(out=E["tmp1"][:], in0=E["cntc"][:], in1=E["blo_c"][:], op=ALU.is_lt),
                            lambda ve: ve.tensor_tensor(out=E["tmp2"][:], in0=E["tmp0"][:], in1=E["tmp1"][:], op=ALU.mult),
                            lambda ve: ve.tensor_scalar(out=SELI["mi"][:], in0=E["tmp2"][:], scalar1=0.5, scalar2=None, op0=ALU.is_ge),
                            lambda ve: ve.copy_predicated(out=E["blo_c"][:], mask=SELI["mi"][:], data=E["cntc"][:]),
                            lambda ve: ve.copy_predicated(out=E["blo_t"][:], mask=SELI["mi"][:], data=E["tauc"][:]),
                            lambda ve: ve.tensor_scalar(out=E["tmp0"][:], in0=E["cntc"][:], scalar1=KSEL - 0.5, scalar2=None, op0=ALU.is_lt),
                            lambda ve: ve.tensor_tensor(out=E["tmp1"][:], in0=E["cntc"][:], in1=E["bhi_c"][:], op=ALU.is_gt),
                            lambda ve: ve.tensor_tensor(out=E["tmp2"][:], in0=E["tmp0"][:], in1=E["tmp1"][:], op=ALU.mult),
                            lambda ve: ve.tensor_scalar(out=SELI["mi"][:], in0=E["tmp2"][:], scalar1=0.5, scalar2=None, op0=ALU.is_ge),
                            lambda ve: ve.copy_predicated(out=E["bhi_c"][:], mask=SELI["mi"][:], data=E["cntc"][:]),
                            lambda ve: ve.copy_predicated(out=E["bhi_t"][:], mask=SELI["mi"][:], data=E["tauc"][:]),
                        ])

                    def update_bracket():
                        vsec([
                            lambda ve: ve.tensor_scalar(out=E["tmp0"][:], in0=E["cntc"][:], scalar1=GTGT, scalar2=None, op0=ALU.is_ge),
                            lambda ve: ve.tensor_scalar(out=E["tmp1"][:], in0=E["tmp0"][:], scalar1=-1.0, scalar2=1.0, op0=ALU.mult, op1=ALU.add),
                            lambda ve: ve.tensor_tensor(out=E["tmp2"][:], in0=E["tmp1"][:], in1=E["hava"][:], op=ALU.mult),
                            lambda ve: ve.tensor_scalar(out=SELI["mi"][:], in0=E["tmp2"][:], scalar1=0.5, scalar2=None, op0=ALU.is_ge),
                            lambda ve: ve.tensor_scalar(out=E["tmp3"][:], in0=E["illa"][:], scalar1=0.5, scalar2=None, op0=ALU.mult),
                            lambda ve: ve.tensor_copy(E["illa"][:], E["ones15"][:]),
                            lambda ve: ve.copy_predicated(out=E["illa"][:], mask=SELI["mi"][:], data=E["tmp3"][:]),
                            lambda ve: ve.tensor_tensor(out=E["tmp2"][:], in0=E["tmp0"][:], in1=E["havb"][:], op=ALU.mult),
                            lambda ve: ve.tensor_scalar(out=SELI["mi"][:], in0=E["tmp2"][:], scalar1=0.5, scalar2=None, op0=ALU.is_ge),
                            lambda ve: ve.tensor_scalar(out=E["tmp3"][:], in0=E["illb"][:], scalar1=0.5, scalar2=None, op0=ALU.mult),
                            lambda ve: ve.tensor_copy(E["illb"][:], E["ones15"][:]),
                            lambda ve: ve.copy_predicated(out=E["illb"][:], mask=SELI["mi"][:], data=E["tmp3"][:]),
                            lambda ve: ve.tensor_scalar(out=SELI["mi"][:], in0=E["tmp0"][:], scalar1=0.5, scalar2=None, op0=ALU.is_ge),
                            lambda ve: ve.copy_predicated(out=E["ta_"][:], mask=SELI["mi"][:], data=E["tauc"][:]),
                            lambda ve: ve.copy_predicated(out=E["ca_"][:], mask=SELI["mi"][:], data=E["cntc"][:]),
                            lambda ve: ve.tensor_scalar(out=SELI["mi2"][:], in0=E["tmp0"][:], scalar1=0.5, scalar2=None, op0=ALU.is_lt),
                            lambda ve: ve.copy_predicated(out=E["tb_"][:], mask=SELI["mi2"][:], data=E["tauc"][:]),
                            lambda ve: ve.copy_predicated(out=E["cb_"][:], mask=SELI["mi2"][:], data=E["cntc"][:]),
                            lambda ve: ve.tensor_tensor(out=E["hava"][:], in0=E["hava"][:], in1=E["tmp0"][:], op=ALU.max),
                            lambda ve: ve.tensor_tensor(out=E["havb"][:], in0=E["havb"][:], in1=E["tmp1"][:], op=ALU.max),
                        ])

                    vsec([
                        lambda ve: ve.memset(E["ones15"][:], 1.0),
                        lambda ve: ve.memset(E["negs15"][:], NEGSEL),
                        lambda ve: ve.memset(E["blo_c"][:], 1.0e9),
                        lambda ve: ve.memset(E["blo_t"][:], 0.0),
                        lambda ve: ve.memset(E["bhi_c"][:], -1.0e9),
                        lambda ve: ve.memset(E["bhi_t"][:], 0.0),
                    ])

                    @on("vector")
                    def _(ve):
                        ve.wait_ge(ea.h, SABS_DONE)
                        ve.wait_ge(ev.h, ev.v)
                        ev.inc(ve.tensor_tensor(out=E["sig"][:], in0=E["sabs"][:], in1=s_cinvl[:], op=ALU.mult))
                        ve.wait_ge(ev.h, ev.v)
                        ev.inc(ve.tensor_tensor(out=E["rstep"][:], in0=s_sscale[:], in1=E["sig"][:], op=ALU.mult))
                        ve.wait_ge(ev.h, ev.v)
                        ev.inc(ve.tensor_tensor(out=E["tauc"][:], in0=E["sig"][:], in1=s_z0[:], op=ALU.mult))
                    counts(0)
                    vsec([
                        lambda ve: ve.tensor_copy(E["ta_"][:], E["tauc"][:]),
                        lambda ve: ve.tensor_copy(E["ca_"][:], E["cntc"][:]),
                        lambda ve: ve.tensor_copy(E["tb_"][:], E["tauc"][:]),
                        lambda ve: ve.tensor_copy(E["cb_"][:], E["cntc"][:]),
                        lambda ve: ve.tensor_scalar(out=E["hava"][:], in0=E["cntc"][:], scalar1=GTGT, scalar2=None, op0=ALU.is_ge),
                        lambda ve: ve.tensor_scalar(out=E["havb"][:], in0=E["cntc"][:], scalar1=GTGT, scalar2=None, op0=ALU.is_lt),
                        lambda ve: ve.memset(E["illa"][:], 1.0),
                        lambda ve: ve.memset(E["illb"][:], 1.0),
                    ])
                    update_best()
                    vsec([
                        lambda ve: ve.tensor_scalar(out=E["tmp0"][:], in0=E["cntc"][:], scalar1=GTGT, scalar2=None, op0=ALU.subtract),
                        lambda ve: ve.tensor_tensor(out=E["tmp1"][:], in0=E["tmp0"][:], in1=E["rstep"][:], op=ALU.mult),
                        lambda ve: ve.tensor_tensor(out=E["tauc"][:], in0=E["tauc"][:], in1=E["tmp1"][:], op=ALU.add),
                    ])
                    counts(1)
                    update_best()
                    update_bracket()
                    for rnd in range(NFALSI):
                        vsec([
                            lambda ve: ve.tensor_scalar(out=E["tmp0"][:], in0=E["ca_"][:], scalar1=GTGT, scalar2=None, op0=ALU.subtract),
                            lambda ve: ve.tensor_tensor(out=E["tmp0"][:], in0=E["tmp0"][:], in1=E["illa"][:], op=ALU.mult),
                            lambda ve: ve.tensor_scalar(out=E["tmp1"][:], in0=E["cb_"][:], scalar1=GTGT, scalar2=None, op0=ALU.subtract),
                            lambda ve: ve.tensor_tensor(out=E["tmp1"][:], in0=E["tmp1"][:], in1=E["illb"][:], op=ALU.mult),
                            lambda ve: ve.tensor_tensor(out=E["tmp2"][:], in0=E["tmp0"][:], in1=E["tmp1"][:], op=ALU.subtract),
                            lambda ve: ve.tensor_scalar(out=E["tmp2"][:], in0=E["tmp2"][:], scalar1=1e-9, scalar2=None, op0=ALU.max),
                            lambda ve: ve.reciprocal(E["tmp2"][:], E["tmp2"][:]),
                            lambda ve: ve.tensor_tensor(out=E["tmp0"][:], in0=E["tmp0"][:], in1=E["tmp2"][:], op=ALU.mult),
                            lambda ve: ve.tensor_tensor(out=E["tmp1"][:], in0=E["tb_"][:], in1=E["ta_"][:], op=ALU.subtract),
                            lambda ve: ve.tensor_tensor(out=E["tmp1"][:], in0=E["tmp1"][:], in1=E["tmp0"][:], op=ALU.mult),
                            lambda ve: ve.tensor_tensor(out=E["tmp1"][:], in0=E["ta_"][:], in1=E["tmp1"][:], op=ALU.add),
                            lambda ve: ve.tensor_scalar(out=E["tmp2"][:], in0=E["cntc"][:], scalar1=GTGT, scalar2=None, op0=ALU.subtract),
                            lambda ve: ve.tensor_tensor(out=E["tmp2"][:], in0=E["tmp2"][:], in1=E["rstep"][:], op=ALU.mult),
                            lambda ve: ve.tensor_tensor(out=E["tmp2"][:], in0=E["tauc"][:], in1=E["tmp2"][:], op=ALU.add),
                            lambda ve: ve.tensor_tensor(out=E["tmp3"][:], in0=E["hava"][:], in1=E["havb"][:], op=ALU.mult),
                            lambda ve: ve.tensor_scalar(out=SELI["mi"][:], in0=E["tmp3"][:], scalar1=0.5, scalar2=None, op0=ALU.is_ge),
                            lambda ve: ve.tensor_copy(E["tauc"][:], E["tmp2"][:]),
                            lambda ve: ve.copy_predicated(out=E["tauc"][:], mask=SELI["mi"][:], data=E["tmp1"][:]),
                        ])
                        counts(2 + rnd)
                        update_best()
                        update_bracket()

                    lo_done = {}
                    m8lo_ev = {}
                    for t in range(1, NQT):
                        L = 128 * (t + 1)
                        sl_ = (t % 2) * S

                        @on("vector")
                        def _(ve, t=t, L=L, sl_=sl_):
                            ve.wait_ge(ev.h, ev.v)
                            lo_done[("cmp", t)] = ev.inc(ve.tensor_scalar(
                                out=d2.ap()[:, sl_:sl_ + L],
                                in0=S_pack.ap()[:, SOFF[t]:SOFF[t] + L],
                                scalar1=E["blo_t"][:, t - 1:t], scalar2=-BIGM,
                                op0=ALU.is_lt, op1=ALU.mult))
                            ve.wait_ge(ev.h, ev.v)
                            ev.inc(ve.tensor_scalar(
                                out=exb.ap()[:, 0:L],
                                in0=S_pack.ap()[:, SOFF[t]:SOFF[t] + L],
                                scalar1=E["bhi_t"][:, t - 1:t], scalar2=-BIGM,
                                op0=ALU.is_ge, op1=ALU.mult))
                            ve.wait_ge(ev.h, ev.v)
                            ev.inc(ve.tensor_tensor(
                                out=exb.ap()[:, 0:L], in0=exb.ap()[:, 0:L],
                                in1=S_pack.ap()[:, SOFF[t]:SOFF[t] + L], op=ALU.add))
                            ve.wait_ge(ev.h, ev.v)
                            ev.inc(ve.max(out=m8hi.ap()[:, 8 * (t - 1):8 * t], in_=exb.ap()[:, 0:L]))
                            if t >= 2:
                                ve.wait_ge(eg.h, lo_done[t - 1])
                                ve.wait_ge(ev.h, ev.v)
                                Lp = 128 * t
                                m8lo_ev[t - 1] = ev.inc(ve.max(
                                    out=m8lo.ap()[:, 8 * (t - 2):8 * (t - 1)],
                                    in_=cjunk.ap()[:, ((t - 1) % 2) * S:((t - 1) % 2) * S + Lp]))

                        @on("gpsimd")
                        def _(gp, t=t, L=L, sl_=sl_):
                            gp.wait_ge(ev.h, lo_done[("cmp", t)])
                            if t >= 3:
                                gp.wait_ge(ev.h, m8lo_ev[t - 2])
                            gp.wait_ge(eg.h, eg.v)
                            lo_done[t] = eg.inc(gp.tensor_tensor(
                                out=cjunk.ap()[:, sl_:sl_ + L],
                                in0=d2.ap()[:, sl_:sl_ + L],
                                in1=S_pack.ap()[:, SOFF[t]:SOFF[t] + L], op=ALU.subtract))

                    @on("vector")
                    def _(ve):
                        t = NQT - 1
                        L = 128 * (t + 1)
                        ve.wait_ge(eg.h, lo_done[t])
                        ve.wait_ge(ev.h, ev.v)
                        m8lo_ev[t] = ev.inc(ve.max(
                            out=m8lo.ap()[:, 8 * (t - 1):8 * t],
                            in_=cjunk.ap()[:, (t % 2) * S:(t % 2) * S + L]))

                    vsec([
                        lambda ve: ve.tensor_scalar(out=E["jl"][:], in0=E["blo_c"][:], scalar1=-(KSEL - 1.0), scalar2=None, op0=ALU.add),
                        lambda ve: ve.tensor_scalar(out=E["jl"][:], in0=E["jl"][:], scalar1=1.0, scalar2=8.0, op0=ALU.max, op1=ALU.min),
                        lambda ve: ve.tensor_scalar(out=E["jh"][:], in0=E["bhi_c"][:], scalar1=-1.0, scalar2=float(KSEL), op0=ALU.mult, op1=ALU.add),
                        lambda ve: ve.tensor_scalar(out=E["jh"][:], in0=E["jh"][:], scalar1=1.0, scalar2=8.0, op0=ALU.max, op1=ALU.min),
                        lambda ve: ve.tensor_tensor(
                            out=ind120.ap().rearrange("p (t j) -> p t j", j=8),
                            in0=s_iota8.ap().rearrange("p (t j) -> p t j", j=8),
                            in1=E["jl"].ap().unsqueeze(2).broadcast_to([128, NQT - 1, 8]),
                            op=ALU.is_equal),
                        lambda ve: ve.tensor_tensor(out=vsel[:], in0=m8lo[:], in1=ind120[:], op=ALU.mult),
                        lambda ve: ve.reduce_sum(E["taulo"][:], vsel.ap().rearrange("p (t j) -> p t j", j=8), axis=AX.X),
                        lambda ve: ve.tensor_scalar(out=E["taulo"][:], in0=E["taulo"][:], scalar1=-1.0, scalar2=None, op0=ALU.mult),
                        lambda ve: ve.tensor_tensor(
                            out=ind120.ap().rearrange("p (t j) -> p t j", j=8),
                            in0=s_iota8.ap().rearrange("p (t j) -> p t j", j=8),
                            in1=E["jh"].ap().unsqueeze(2).broadcast_to([128, NQT - 1, 8]),
                            op=ALU.is_equal),
                        lambda ve: ve.tensor_tensor(out=vsel[:], in0=m8hi[:], in1=ind120[:], op=ALU.mult),
                        lambda ve: ve.reduce_sum(E["tauhi"][:], vsel.ap().rearrange("p (t j) -> p t j", j=8), axis=AX.X),
                        lambda ve: ve.tensor_scalar(out=SELI["mi"][:], in0=E["blo_c"][:], scalar1=1.0e8, scalar2=None, op0=ALU.is_lt),
                        lambda ve: ve.tensor_copy(E["tstar"][:], E["tauhi"][:]),
                        lambda ve: ve.copy_predicated(out=E["tstar"][:], mask=SELI["mi"][:], data=E["taulo"][:]),
                        lambda ve: ve.tensor_scalar(out=SELI["mi"][:], in0=E["bhi_c"][:], scalar1=KSEL - 8.5, scalar2=None, op0=ALU.is_ge),
                        lambda ve: ve.copy_predicated(out=E["tstar"][:], mask=SELI["mi"][:], data=E["tauhi"][:]),
                        lambda ve: ve.tensor_scalar(out=SELI["mi"][:], in0=E["blo_c"][:], scalar1=KSEL + 7.5, scalar2=None, op0=ALU.is_le),
                        lambda ve: ve.copy_predicated(out=E["tstar"][:], mask=SELI["mi"][:], data=E["taulo"][:]),
                        lambda ve: ve.tensor_scalar(out=SELI["mi"][:], in0=s_keep[:], scalar1=0.5, scalar2=None, op0=ALU.is_ge),
                        lambda ve: ve.copy_predicated(out=E["tstar"][:], mask=SELI["mi"][:], data=E["negs15"][:]),
                    ])

                    @on("vector")
                    def _(ve):
                        if lm["tau_tr"] is not None:
                            ve.wait_ge(et.h, lm["tau_tr"])
                        ve.wait_ge(ev.h, ev.v)
                        ev.inc(ve.memset(taub.ap()[:, 0:1], NEGSEL))
                        ve.wait_ge(ev.h, ev.v)
                        ev.inc(ve.tensor_copy(taub.ap()[:, 1:16], E["tstar"][:]))
                    lm["sel_done"] = ev.v
                    TAUB = ev.v

                    @on("tensor")
                    def _(te, h=h):
                        te.wait_ge(ev.h, TAUB)
                        lm["tau_tr"] = et.inc(te.transpose(p_tr2[:], taub[:], s_idf[:]))

                    @on("vector")
                    def _(ve):
                        ve.wait_ge(et.h, lm["tau_tr"])
                        ve.wait_ge(ev.h, ev.v)
                        ev.inc(ve.tensor_copy(taubT[:], p_tr2[:]))
                    TAUTT = ev.v

                    @on("sync")
                    def _(sy, h=h):
                        sy.wait_ge(ev.h, TAUTT)
                        tbo.inc(sy.dma_start(
                            out=tau_dram[h].rearrange("(t q) -> t q", q=128),
                            in_=taubT[:]), 16)
                        sy.wait_ge(tbo.h, tbo.v)
                        if lm["tb_used"] is not None:
                            sy.wait_ge(eg.h, lm["tb_used"])
                        tbl.inc(sy.dma_start(
                            out=tbrz[:],
                            in_=tau_dram[h:h + 1, :].broadcast_to([128, S])), 16)
                    if "dtau" in dbg:
                        @on("sync")
                        def _(sy, h=h):
                            qst.inc(sy.dma_start(out=dbg_out["dtau"][h:h + 1], in_=tau_dram[h:h + 1]), 16)

                def emit_b3_loads(h):
                    par = h % 2
                    kv = h // 4

                    @on("sync")
                    def _(sy, h=h, par=par, kv=kv):
                        for key, sem_l, buf, src in [
                            ("q3_used", q3l, qr3, qr_dram[h]),
                            ("k3_used", k3l, kr3, kr_dram[kv]),
                        ]:
                            if lm[key][par] is not None:
                                sm, val = lm[key][par]
                                sy.wait_ge(sm.h, val)
                            sem_l[par].inc(sy.dma_start(
                                out=buf.ap()[:, par * S:(par + 1) * S], in_=src), 16)
                        if lm["v3_used"][par] is not None:
                            sm, val = lm["v3_used"][par]
                            sy.wait_ge(sm.h, val)
                        v3l[par].inc(sy.dma_start(
                            out=v3.ap()[:, par * S:(par + 1) * S]
                                .rearrange("p (t u) -> p t u", u=128),
                            in_=v_dram[kv].rearrange("(t p) u -> p t u", p=128)), 16)

                def emit_b3_compute(h):
                    par = h % 2
                    Q3 = q3l[par].v
                    K3 = k3l[par].v
                    V3 = v3l[par].v
                    TBLD = tbl.v
                    st_ev = {}
                    seq2 = [0]
                    last_mm = [None]
                    for kt in range(NQT):
                        q0 = 128 * kt
                        w = S - q0
                        ncs = (w + 511) // 512
                        for cs_ in range(ncs):
                            c0 = q0 + 512 * cs_
                            cw = min(512, S - c0)
                            pidx = seq2[0] % 2
                            seq2[0] += 1

                            @on("tensor")
                            def _(te, kt=kt, cs_=cs_, c0=c0, cw=cw, pidx=pidx, par=par, Q3=Q3, K3=K3):
                                te.wait_ge(q3l[par].h, Q3)
                                te.wait_ge(k3l[par].h, K3)
                                if lm["pstfree"][pidx] is not None:
                                    sm, val = lm["pstfree"][pidx]
                                    te.wait_ge(sm.h, val)
                                inst = te.matmul(
                                    p_st2[pidx][:, 0:cw],
                                    kr3.ap()[:, par * S + 128 * kt: par * S + 128 * (kt + 1)],
                                    qr3.ap()[:, par * S + c0: par * S + c0 + cw],
                                    start=True, stop=True)
                                lm[("mm3", kt, cs_)] = et.inc(inst)
                                last_mm[0] = lm[("mm3", kt, cs_)]

                            @on("vector")
                            def _(ve, kt=kt, cs_=cs_, c0=c0, cw=cw, pidx=pidx, q0=q0):
                                ve.wait_ge(et.h, lm[("mm3", kt, cs_)])
                                if kt == 0 and cs_ == 0 and lm["av_done"] is not None:
                                    ve.wait_ge(et.h, lm["av_done"])
                                ve.wait_ge(ev.h, ev.v)
                                i1 = ve.tensor_copy(
                                    st_pack.ap()[:, POFF[kt] + (c0 - q0): POFF[kt] + (c0 - q0) + cw],
                                    p_st2[pidx][:, 0:cw])
                                st_ev[(kt, cs_)] = (ev, ev.inc(i1))
                                lm["pstfree"][pidx] = st_ev[(kt, cs_)]
                    lm["q3_used"][par] = (et, last_mm[0])
                    lm["k3_used"][par] = (et, last_mm[0])

                    pm = {}

                    for kt in range(NQT):
                        q0 = 128 * kt
                        w = S - q0
                        csl = (kt % 2) * S

                        @on("vector")
                        def _(ve, kt=kt, q0=q0, w=w, csl=csl):
                            if kt == 0:
                                ve.wait_ge(tbl.h, TBLD)
                            for cs_ in range((w + 511) // 512):
                                sm, val = st_ev[(kt, cs_)]
                                ve.wait_ge(sm.h, val)
                            if kt >= 2 and pm.get((kt - 2, "pm")) is not None:
                                ve.wait_ge(eg.h, pm[(kt - 2, "pm")])
                            ve.wait_ge(ev.h, ev.v)
                            ev.inc(ve.tensor_tensor(
                                out=cjunk.ap()[:, csl:csl + w],
                                in0=st_pack.ap()[:, POFF[kt]: POFF[kt] + w],
                                in1=tbrz.ap()[:, q0:S], op=ALU.is_ge))
                            ve.wait_ge(ev.h, ev.v)
                            pm[(kt, "msk")] = ev.inc(ve.tensor_tensor(
                                out=cjunk.ap()[:, csl:csl + 128],
                                in0=cjunk.ap()[:, csl:csl + 128],
                                in1=s_caus01[:], op=ALU.mult))

                        @on("scalar")
                        def _(sc, kt=kt, w=w):
                            for cs_ in range((w + 511) // 512):
                                sm, val = st_ev[(kt, cs_)]
                                sc.wait_ge(sm.h, val)
                            sc.wait_ge(ea.h, ea.v)
                            pm[(kt, "exp")] = (ea, ea.inc(sc.activation(
                                out=p_pack.ap()[:, POFF[kt]: POFF[kt] + w],
                                in_=st_pack.ap()[:, POFF[kt]: POFF[kt] + w],
                                func=AF.Exp, scale=ISQ)))

                        @on("gpsimd")
                        def _(gp, kt=kt, w=w, csl=csl):
                            gp.wait_ge(ev.h, pm[(kt, "msk")])
                            sm, val = pm[(kt, "exp")]
                            gp.wait_ge(sm.h, val)
                            gp.wait_ge(eg.h, eg.v)
                            pm[(kt, "pm")] = eg.inc(gp.tensor_tensor(
                                out=st_pack.ap()[:, POFF[kt]: POFF[kt] + w],
                                in0=p_pack.ap()[:, POFF[kt]: POFF[kt] + w],
                                in1=cjunk.ap()[:, csl:csl + w], op=ALU.mult))

                    z_ev = {}
                    z_done = {}
                    for s4 in range(4):
                        @on("tensor")
                        def _(te, s4=s4):
                            for kt in range(4 * s4 + 4):
                                te.wait_ge(eg.h, pm[(kt, "pm")])
                            if s4 > 0:
                                sm, val = z_ev[s4 - 1]
                                te.wait_ge(sm.h, val)
                            nkt = 4 * s4 + 4
                            inst = None
                            for kt in range(nkt):
                                q0 = 128 * kt
                                off = 512 * s4 - q0
                                last = (kt == nkt - 1)
                                if off >= 0:
                                    inst = te.matmul(
                                        p_z[:, 0:512], s_onesc[:],
                                        st_pack.ap()[:, POFF[kt] + off: POFF[kt] + off + 512],
                                        start=(kt == 0), stop=last)
                                else:
                                    wv = 512 * (s4 + 1) - q0
                                    inst = te.matmul(
                                        p_z[:, 512 - wv:512], s_onesc[:],
                                        st_pack.ap()[:, POFF[kt]: POFF[kt] + wv],
                                        start=False, stop=last)
                            z_done[s4] = et.inc(inst)

                        @on("scalar")
                        def _(sc, s4=s4):
                            sc.wait_ge(et.h, z_done[s4])
                            sc.wait_ge(ea.h, ea.v)
                            z_ev[s4] = (ea, ea.inc(sc.activation(
                                out=zrow.ap()[:, 512 * s4:512 * s4 + 512],
                                in_=p_z[:, 0:512], func=AF.Ln)))

                    @on("scalar")
                    def _(sc):
                        sm, val = z_ev[3]
                        sc.wait_ge(sm.h, val)
                        sc.wait_ge(ea.h, ea.v)
                        ea.inc(sc.activation(out=zrow[:], in_=zrow[:],
                                             func=AF.Exp, scale=-1.0))
                    RECIP = ea.v
                    if "dz" in dbg:
                        @on("sync")
                        def _(sy, h=h):
                            sy.wait_ge(ea.h, RECIP)
                            qst.inc(sy.dma_start(out=dbg_out["dz"][h:h + 1], in_=zrow[:]), 16)

                    rz_ev = {}
                    rz_tr = {}
                    for s4 in range(4):
                        @on("tensor")
                        def _(te, s4=s4):
                            te.wait_ge(ea.h, RECIP)
                            if "dz" in dbg:
                                te.wait_ge(qst.h, qst.v)
                            if s4 > 0:
                                sm, val = rz_ev[s4 - 1]
                                te.wait_ge(sm.h, val)
                            rz_tr[s4] = et.inc(te.matmul(
                                p_mis[:], s_onesrf[:],
                                zrow.ap()[:, 512 * s4: 512 * s4 + 512],
                                start=True, stop=True))

                        @on("scalar")
                        def _(sc, s4=s4):
                            sc.wait_ge(et.h, rz_tr[s4])
                            sc.wait_ge(ea.h, ea.v)
                            rz_ev[s4] = (ea, ea.inc(sc.copy(
                                tbrz.ap()[:, 512 * s4: 512 * s4 + 512], p_mis[:])))

                    norm_mark = {}

                    @on("gpsimd")
                    def _(gp):
                        sm, val = rz_ev[3]
                        gp.wait_ge(sm.h, val)
                        for kt in range(NQT):
                            q0 = 128 * kt
                            w = S - q0
                            gp.wait_ge(eg.h, eg.v)
                            norm_mark[kt] = eg.inc(gp.tensor_tensor(
                                out=p_pack.ap()[:, POFF[kt]: POFF[kt] + w],
                                in0=st_pack.ap()[:, POFF[kt]: POFF[kt] + w],
                                in1=tbrz.ap()[:, q0:S], op=ALU.mult))
                    P_NORM = eg.v
                    lm["tb_used"] = P_NORM

                    av_ev = {}
                    av_tr = {}
                    for s4 in range(4):
                        @on("tensor")
                        def _(te, s4=s4, par=par, V3=V3):
                            te.wait_ge(eg.h, norm_mark[4 * s4 + 3])
                            te.wait_ge(v3l[par].h, V3)
                            if s4 > 0:
                                sm, val = av_ev[s4 - 1]
                                te.wait_ge(sm.h, val)
                            nkt = 4 * s4 + 4
                            inst = None
                            for kt in range(nkt):
                                q0 = 128 * kt
                                off = 512 * s4 - q0
                                last = (kt == nkt - 1)
                                vsl = v3.ap()[:, par * S + kt * 128: par * S + (kt + 1) * 128]
                                if off >= 0:
                                    inst = te.matmul(
                                        p_av[:, 0:512], vsl,
                                        p_pack.ap()[:, POFF[kt] + off: POFF[kt] + off + 512],
                                        start=(kt == 0), stop=last)
                                else:
                                    wv = 512 * (s4 + 1) - q0
                                    inst = te.matmul(
                                        p_av[:, 512 - wv:512], vsl,
                                        p_pack.ap()[:, POFF[kt]: POFF[kt] + wv],
                                        start=False, stop=last)
                            av_tr[s4] = et.inc(inst)

                        @on("scalar")
                        def _(sc, s4=s4):
                            sc.wait_ge(et.h, av_tr[s4])
                            if s4 == 0:
                                sc.wait_ge(ats.h, ats.v)
                            sc.wait_ge(ea.h, ea.v)
                            av_ev[s4] = (ea, ea.inc(sc.copy(
                                aT_stg.ap()[:, 512 * s4: 512 * s4 + 512], p_av[:])))
                    lm["av_done"] = av_tr[3]
                    lm["v3_used"][par] = (et, av_tr[3])

                    @on("sync")
                    def _(sy, h=h):
                        sm, val = av_ev[3]
                        sy.wait_ge(sm.h, val)
                        ats.inc(sy.dma_start(
                            out=aT_dram[:, :, h, :].rearrange("t p q -> p t q"),
                            in_=aT_stg.ap().rearrange("p (t q) -> p t q", q=128)), 16)

                emit_b1_loads(0)
                sr = emit_b1_scores(0)
                emit_b1_select(0, sr)
                emit_b3_loads(0)
                for h in range(H):
                    if h + 1 < H:
                        emit_b1_loads(h + 1)
                        sr = emit_b1_scores(h + 1)
                        emit_b3_loads(h + 1)
                    emit_b3_compute(h)
                    if h + 1 < H:
                        emit_b1_select(h + 1, sr)

                if "daT" in dbg:
                    @on("sync")
                    def _(sy):
                        sy.wait_ge(ats.h, ats.v)
                        qst.inc(sy.dma_start(out=dbg_out["daT"], in_=aT_dram), 16)

        if "W" in phases:
            with ExitStack() as wctx:
                aTq = sb("aTq", [128, 4 * NHC * 128], BF16, wctx)
                wo_buf = sb("wo_buf", [128, NHC * 512], BF16, wctx)
                a1g = sb("a1g", [128, 4 * D], F32, wctx)
                xw_buf = sb("xw_buf", [128, 2 * 512], F32, wctx)
                y1_stg = sb("y1_stg", [128, 2 * D], BF16, wctx)
                lnv = sb("lnv", [128, 8], F32, wctx)
                ajunk2 = sb("ajunk2", [128, D], BF16, wctx)
                p_w = [psm(f"p_w{i}", [128, 512], F32, wctx) for i in range(2)]
                phase_barrier()

                wol = mksem("wol")
                aql = mksem("aql")
                xwl = [mksem("xwl0"), mksem("xwl1")]
                y1s = [mksem("y1s0"), mksem("y1s1")]
                all_dma_sems.extend([wol, aql] + xwl + y1s)


                wo_mm = {}
                wo_mm_last = [None]
                xw_free = {}
                pw_free = {0: None, 1: None}
                a1_ev = {}
                seqw = [0]
                for qtg in range(4):
                    qts = [4 * qtg + i for i in range(4)]

                    @on("sync")
                    def _(sy, qtg=qtg, qts=qts):
                        if qtg > 0 and wo_mm_last[0] is not None:
                            sy.wait_ge(et.h, wo_mm_last[0])
                        for i, qt in enumerate(qts):
                            aql.inc(sy.dma_start(
                                out=aTq.ap()[:, i * NHC * 128:(i + 1) * NHC * 128]
                                    .rearrange("p (c q) -> p c q", q=128),
                                in_=aT_dram[qt]), 16)
                        sy.wait_ge(aql.h, aql.v)
                    AQ_LD = aql.v

                    for js in range(8):
                        @on("sync")
                        def _(sy, qtg=qtg, js=js):
                            if wo_mm_last[0] is not None and (qtg > 0 or js > 0):
                                sy.wait_ge(et.h, wo_mm_last[0])
                            sy.wait_ge(stg_st[0].h, stg_st[0].v)
                            sy.wait_ge(stg_st[1].h, stg_st[1].v)
                            wol.inc(sy.dma_start(
                                out=wo_buf.ap().rearrange("p (c j) -> p c j", j=512),
                                in_=wo_bf[js]), 16)
                            sy.wait_ge(wol.h, wol.v)
                        WO_LD = wol.v

                        for i, qt in enumerate(qts):
                            pidx = seqw[0] % 2
                            seqw[0] += 1

                            @on("sync")
                            def _(sy, qt=qt, js=js, pidx=pidx):
                                if xw_free.get(pidx) is not None:
                                    sm, val = xw_free[pidx]
                                    sy.wait_ge(sm.h, val)
                                xwl[pidx].inc(sy.dma_start(
                                    out=xw_buf.ap()[:, pidx * 512:(pidx + 1) * 512],
                                    in_=x_nat[qt * 128:(qt + 1) * 128, js * 512:(js + 1) * 512]), 16)

                            @on("tensor")
                            def _(te, i=i, js=js, pidx=pidx):
                                te.wait_ge(wol.h, WO_LD)
                                te.wait_ge(aql.h, AQ_LD)
                                if pw_free[pidx] is not None:
                                    sm, val = pw_free[pidx]
                                    te.wait_ge(sm.h, val)
                                inst = None
                                for dc in range(NHC):
                                    inst = te.matmul(
                                        p_w[pidx][:],
                                        aTq.ap()[:, i * NHC * 128 + dc * 128: i * NHC * 128 + (dc + 1) * 128],
                                        wo_buf.ap()[:, dc * 512:(dc + 1) * 512],
                                        start=(dc == 0), stop=(dc == NHC - 1))
                                wo_mm[(i, js)] = et.inc(inst)
                                wo_mm_last[0] = wo_mm[(i, js)]

                            @on("vector")
                            def _(ve, i=i, js=js, pidx=pidx):
                                ve.wait_ge(et.h, wo_mm[(i, js)])
                                ve.wait_ge(xwl[pidx].h, xwl[pidx].v)
                                ve.wait_ge(ev.h, ev.v)
                                i1 = ve.tensor_tensor(
                                    out=a1g.ap()[:, i * D + js * 512: i * D + (js + 1) * 512],
                                    in0=p_w[pidx][:],
                                    in1=xw_buf.ap()[:, pidx * 512:(pidx + 1) * 512],
                                    op=ALU.add)
                                lm = ev.inc(i1)
                                pw_free[pidx] = (ev, lm)
                                xw_free[pidx] = (ev, lm)
                                a1_ev[(i, js)] = lm

                    for i, qt in enumerate(qts):
                        @on("scalar")
                        def _(sc, i=i):
                            for js in range(8):
                                sc.wait_ge(ev.h, a1_ev[(i, js)])
                            sc.wait_ge(ea.h, ea.v)
                            a1sl = a1g.ap()[:, i * D:(i + 1) * D]
                            ea.inc(sc.activation(out=ajunk2[:], in_=a1sl,
                                                 func=AF.Identity, accum_out=lnv[:, 0:1]))
                            sc.wait_ge(ea.h, ea.v)
                            ea.inc(sc.activation(out=ajunk2[:], in_=a1sl,
                                                 func=AF.Square, accum_out=lnv[:, 1:2]))
                        STATS = ea.v

                        @on("vector")
                        def _(ve):
                            ve.wait_ge(ea.h, STATS)
                            ve.wait_ge(ev.h, ev.v)
                            ev.inc(ve.tensor_scalar(out=lnv[:, 2:3], in0=lnv[:, 0:1],
                                                    scalar1=1.0 / D, scalar2=None, op0=ALU.mult))
                            ve.wait_ge(ev.h, ev.v)
                            ev.inc(ve.tensor_scalar(out=lnv[:, 3:4], in0=lnv[:, 1:2],
                                                    scalar1=1.0 / D, scalar2=None, op0=ALU.mult))
                            ve.wait_ge(ev.h, ev.v)
                            ev.inc(ve.tensor_tensor(out=lnv[:, 4:5], in0=lnv[:, 2:3],
                                                    in1=lnv[:, 2:3], op=ALU.mult))
                            ve.wait_ge(ev.h, ev.v)
                            ev.inc(ve.tensor_tensor(out=lnv[:, 5:6], in0=lnv[:, 3:4],
                                                    in1=lnv[:, 4:5], op=ALU.subtract))
                            ve.wait_ge(ev.h, ev.v)
                            ev.inc(ve.tensor_scalar(out=lnv[:, 5:6], in0=lnv[:, 5:6],
                                                    scalar1=EPS, scalar2=None, op0=ALU.add))
                            ve.wait_ge(ev.h, ev.v)
                            ev.inc(ve.reciprocal(lnv[:, 6:7], lnv[:, 5:6]))
                        RECV = ev.v

                        @on("scalar")
                        def _(sc):
                            sc.wait_ge(ev.h, RECV)
                            sc.wait_ge(ea.h, ea.v)
                            ea.inc(sc.activation(out=lnv[:, 6:7], in_=lnv[:, 6:7], func=AF.Sqrt))
                        RSTD = ea.v

                        @on("vector")
                        def _(ve):
                            ve.wait_ge(ea.h, RSTD)
                            ve.wait_ge(ev.h, ev.v)
                            ev.inc(ve.tensor_tensor(out=lnv[:, 7:8], in0=lnv[:, 2:3],
                                                    in1=lnv[:, 6:7], op=ALU.mult))
                            ve.wait_ge(ev.h, ev.v)
                            ev.inc(ve.tensor_scalar(out=lnv[:, 7:8], in0=lnv[:, 7:8],
                                                    scalar1=-1.0, scalar2=None, op0=ALU.mult))
                        NBIAS = ev.v

                        @on("scalar")
                        def _(sc, i=i, qt=qt):
                            sc.wait_ge(ev.h, NBIAS)
                            pidx = qt % 2
                            sc.wait_ge(y1s[pidx].h, y1s[pidx].v)
                            sc.wait_ge(ea.h, ea.v)
                            ea.inc(sc.activation(
                                out=y1_stg.ap()[:, pidx * D:(pidx + 1) * D],
                                in_=a1g.ap()[:, i * D:(i + 1) * D],
                                func=AF.Identity, bias=lnv[:, 7:8], scale=lnv[:, 6:7]))
                        Y1A = ea.v

                        @on("sync")
                        def _(sy, qt=qt):
                            sy.wait_ge(ea.h, Y1A)
                            pidx = qt % 2
                            y1s[pidx].inc(sy.dma_start(
                                out=y1_dram[qt * 128:(qt + 1) * 128, :],
                                in_=y1_stg.ap()[:, pidx * D:(pidx + 1) * D]), 16)

                if "dy1" in dbg:
                    @on("sync")
                    def _(sy):
                        for sm in (y1s[0], y1s[1]):
                            sy.wait_ge(sm.h, sm.v)
                        qst.inc(sy.dma_start(out=dbg_out["dy1"], in_=y1_dram), 16)

        if "C" in phases:
            zsg_dram = dri("zsg_dram", [NG_HID, INTER], BF16)
            zsu_dram = dri("zsu_dram", [NG_HID, INTER], BF16)
            zsd_dram = dri("zsd_dram", [NG_INT, D], BF16)
            phase_barrier()
            qwl = [mksem(f"qwl{i}") for i in range(4)]
            srl = mksem("srl")
            w4s_ = [mksem("w4s0"), mksem("w4s1")]
            zql = mksem("zql")
            zst = mksem("zst")
            all_dma_sems.extend(qwl + [srl] + w4s_ + [zql, zst])

            with ExitStack() as c0z:
                z4b = sb("z4b", [NG_INT, INTER], I32, c0z)
                qzb = sb("qzb", [NG_INT, INTER // 8], I32, c0z)
                scb = sb("scb", [NG_INT, INTER], F32, c0z)
                zout = sb("zout", [NG_INT, INTER], BF16, c0z)
                for qz_i, sc_i, zdst, ng, nn in [
                    (g_qz, g_sc, zsg_dram, NG_HID, INTER),
                    (u_qz, u_sc, zsu_dram, NG_HID, INTER),
                    (d_qz, d_sc, zsd_dram, NG_INT, D),
                ]:
                    @on("sync")
                    def _(sy, qz_i=qz_i, sc_i=sc_i, ng=ng, nn=nn):
                        sy.wait_ge(zst.h, zst.v)
                        sy.wait_ge(ev.h, ev.v)
                        zql.inc(sy.dma_start(out=qzb[0:ng, 0:nn // 8], in_=qz_i), 16)
                        zql.inc(sy.dma_start(out=scb[0:ng, 0:nn], in_=sc_i), 16)
                        sy.wait_ge(zql.h, zql.v)
                    ZQ_LD = zql.v

                    @on("vector")
                    def _(ve, ng=ng, nn=nn):
                        ve.wait_ge(zql.h, ZQ_LD)
                        for j in range(8):
                            ve.wait_ge(ev.h, ev.v)
                            ev.inc(ve.tensor_scalar(
                                out=z4b.ap()[0:ng, 0:nn].rearrange("p (n e) -> p n e", e=8)[:, :, j],
                                in0=qzb[0:ng, 0:nn // 8], scalar1=4 * j, scalar2=15,
                                op0=ALU.logical_shift_right, op1=ALU.bitwise_and))
                        ve.wait_ge(ev.h, ev.v)
                        ev.inc(ve.tensor_scalar(
                            out=scb[0:ng, 0:nn], in0=scb[0:ng, 0:nn],
                            scalar1=-1.0, scalar2=None, op0=ALU.mult))
                        ve.wait_ge(ev.h, ev.v)
                        ev.inc(ve.tensor_scalar(
                            out=z4b[0:ng, 0:nn], in0=z4b[0:ng, 0:nn],
                            scalar1=1, scalar2=None, op0=ALU.add))
                        ve.wait_ge(ev.h, ev.v)
                        ev.inc(ve.tensor_tensor(
                            out=zout[0:ng, 0:nn], in0=z4b[0:ng, 0:nn],
                            in1=scb[0:ng, 0:nn], op=ALU.mult))
                    Z_EV = ev.v

                    @on("sync")
                    def _(sy, zdst=zdst, ng=ng, nn=nn):
                        sy.wait_ge(ev.h, Z_EV)
                        zst.inc(sy.dma_start(out=zdst, in_=zout[0:ng, 0:nn]), 16)

            phase_barrier()
            with ExitStack() as c0w:
                qwr = sb("qwr", [128, 2 * 5504], I32, c0w)
                nibb = sb("nibb", [128, 5504], I32, c0w)
                srep = sb("srep", [128, 2 * INTER], BF16, c0w)
                srow = sb("srow", [1, 2 * INTER], BF16, c0w)
                onesr_b = sb("onesr_b", [1, 128], BF16, c0w)
                w4o = sb("w4o", [128, 2 * 5504], BF16, c0w)
                p_sr = [psm(f"p_sr{i}", [128, 512], F32, c0w) for i in range(2)]
                qw_free = {}
                srow_free = {}
                srep_free = {}
                psr_free = {}
                sr_bcast = {}
                sr_done = {}
                seqc = [0]
                seqk = [0]

                @on("vector")
                def _(ve):
                    ve.wait_ge(ev.h, ev.v)
                    ev.inc(ve.memset(onesr_b[:], 1.0))
                ONES_B = ev.v

                def dq_matrix(qw_i, sc_i, wdst, nkc, ncols, gmap, tiled):
                    halves = [(0, 5504), (5504, 5504)] if ncols == INTER else [(0, ncols)]
                    for kc in range(nkc):
                        kpar = seqk[0] % 2
                        seqk[0] += 1

                        @on("sync")
                        def _(sy, kc=kc, sc_i=sc_i, ncols=ncols, gmap=gmap, kpar=kpar):
                            if kc == 0:
                                sy.wait_ge(stg_st[0].h, stg_st[0].v)
                                sy.wait_ge(stg_st[1].h, stg_st[1].v)
                            if srow_free.get(kpar) is not None:
                                sm, val = srow_free[kpar]
                                sy.wait_ge(sm.h, val)
                            srl.inc(sy.dma_start(
                                out=srow[0:1, kpar * INTER: kpar * INTER + ncols],
                                in_=sc_i[gmap[kc]:gmap[kc] + 1, :]), 16)
                        SR_LD = srl.v

                        nch = (ncols + 511) // 512
                        for ch in range(nch):
                            c0_ = 512 * ch
                            cw_ = min(512, ncols - c0_)

                            @on("tensor")
                            def _(te, kpar=kpar, c0_=c0_, cw_=cw_, ch=ch, SR_LD=SR_LD):
                                te.wait_ge(srl.h, SR_LD)
                                if ch == 0:
                                    te.wait_ge(qld.h, CONST_LD)
                                    te.wait_ge(ev.h, ONES_B)
                                pp = ch % 2
                                if psr_free.get(pp) is not None:
                                    sm, val = psr_free[pp]
                                    te.wait_ge(sm.h, val)
                                inst = te.matmul(
                                    p_sr[pp][:, 0:cw_], onesr_b[:],
                                    srow[0:1, kpar * INTER + c0_: kpar * INTER + c0_ + cw_],
                                    start=True, stop=True)
                                sr_bcast[(kpar, ch)] = et.inc(inst)

                            @on("vector")
                            def _(ve, kpar=kpar, c0_=c0_, cw_=cw_, ch=ch, nch=nch):
                                ve.wait_ge(et.h, sr_bcast[(kpar, ch)])
                                if ch == 0 and srep_free.get(kpar) is not None:
                                    sm, val = srep_free[kpar]
                                    ve.wait_ge(sm.h, val)
                                ve.wait_ge(ev.h, ev.v)
                                i1 = ve.tensor_scalar(
                                    out=srep[:, kpar * INTER + c0_: kpar * INTER + c0_ + cw_],
                                    in0=p_sr[ch % 2][:, 0:cw_],
                                    scalar1=s_pw2[:], scalar2=None, op0=ALU.mult)
                                v = ev.inc(i1)
                                psr_free[ch % 2] = (ev, v)
                                if ch == nch - 1:
                                    sr_done[kpar] = v
                                    srow_free[kpar] = (ev, v)
                        SR_ADJ = sr_done[kpar]

                        for off, hw in halves:
                            pidx = seqc[0] % 2
                            vp = seqc[0] % 2
                            seqc[0] += 1

                            @on("sync" if pidx % 2 == 0 else "scalar")
                            def _(eng, kc=kc, off=off, hw=hw, pidx=pidx, qw_i=qw_i):
                                if qw_free.get(pidx) is not None:
                                    sm, val = qw_free[pidx]
                                    eng.wait_ge(sm.h, val)
                                qwl[pidx].inc(eng.dma_start(
                                    out=qwr.ap()[:, pidx * 5504: pidx * 5504 + hw],
                                    in_=qw_i[16 * kc:16 * kc + 16, off:off + hw]
                                        .unsqueeze(1).broadcast_to([16, 8, hw])), 16)

                            @on("vector")
                            def _(ve, off=off, hw=hw, pidx=pidx, vp=vp, kpar=kpar, SR_ADJ=SR_ADJ):
                                ve.wait_ge(qwl[pidx].h, qwl[pidx].v)
                                ve.wait_ge(ev.h, SR_ADJ)
                                ve.wait_ge(w4s_[vp].h, w4s_[vp].v)
                                ve.wait_ge(ev.h, ev.v)
                                ev.inc(ve.tensor_scalar(
                                    out=nibb.ap()[:, 0:hw],
                                    in0=qwr.ap()[:, pidx * 5504: pidx * 5504 + hw],
                                    scalar1=s_maskv[:], scalar2=None,
                                    op0=ALU.bitwise_and))
                                qw_free[pidx] = (ev, ev.v)
                                ve.wait_ge(ev.h, ev.v)
                                i2 = ve.tensor_tensor(
                                    out=w4o.ap()[:, vp * 5504: vp * 5504 + hw],
                                    in0=nibb.ap()[:, 0:hw],
                                    in1=srep[:, kpar * INTER + off: kpar * INTER + off + hw],
                                    op=ALU.mult)
                                srep_free[kpar] = (ev, ev.inc(i2))

                            @on("sync")
                            def _(sy, kc=kc, off=off, hw=hw, vp=vp, kpar=kpar, wdst=wdst, tiled=tiled):
                                sm, val = srep_free[kpar]
                                sy.wait_ge(sm.h, val)
                                if tiled:
                                    ic0 = off // 128
                                    nic = hw // 128
                                    w4s_[vp].inc(sy.dma_start(
                                        out=wdst[ic0:ic0 + nic, :, kc, :]
                                            .rearrange("a p q -> p a q"),
                                        in_=w4o.ap()[:, vp * 5504: vp * 5504 + hw]
                                            .rearrange("p (a q) -> p a q", q=128)), 16)
                                else:
                                    w4s_[vp].inc(sy.dma_start(
                                        out=wdst[128 * kc:128 * kc + 128, :],
                                        in_=w4o.ap()[:, vp * 5504: vp * 5504 + hw]), 16)

                dq_matrix(g_qw, gsc_bf, w4g_dram, NHC, INTER, gmap_g, True)
                dq_matrix(u_qw, usc_bf, w4u_dram, NHC, INTER, gmap_u, True)
                dq_matrix(d_qw, dsc_bf, w4d_dram, NG_INT, D, gmap_d, False)

            if "dw4g" in dbg:
                @on("sync")
                def _(sy):
                    for sm in w4s_:
                        sy.wait_ge(sm.h, sm.v)
                    sy.wait_ge(zst.h, zst.v)
                    qst.inc(sy.dma_start(out=dbg_out["dw4g"], in_=w4g_dram), 16)
                    qst.inc(sy.dma_start(out=dbg_out["dw4d"], in_=w4d_dram), 16)
                    qst.inc(sy.dma_start(out=dbg_out["dzsg"], in_=zsg_dram), 16)
                    qst.inc(sy.dma_start(out=dbg_out["dzsd"], in_=zsd_dram), 16)

            with ExitStack() as cctx:
                phase_barrier()
                y1tl = mksem("y1tl")
                wgl = [mksem("wgl0"), mksem("wgl1")]
                wul = [mksem("wul0"), mksem("wul1")]
                wdl = [mksem(f"wdl{i}") for i in range(4)]
                y1l = [mksem("y1l0"), mksem("y1l1")]
                zgl = [mksem("zgl0"), mksem("zgl1")]
                zdl = mksem("zdl")
                indl = mksem("indl")
                y2s = [mksem("y2s0"), mksem("y2s1")]
                all_dma_sems.extend([y1tl] + wgl + wul + wdl + y1l + zgl + [zdl, indl] + y2s)

                s_indg = sb("s_indg", [128, NHC * NG_HID], BF16, cctx)
                s_indd = sb("s_indd", [128, NIC * NG_INT], BF16, cctx)
                wg_buf = sb("wg_buf", [128, 2 * D], BF16, cctx)
                wu_buf = sb("wu_buf", [128, 2 * D], BF16, cctx)
                wd_buf = sb("wd_buf", [128, 4 * 512], BF16, cctx)
                h_buf = sb("h_buf", [128, NIC * 512], BF16, cctx)
                sg_buf = sb("sg_buf", [128, 512], BF16, cctx)
                gt_s = sb("gt_s", [NG_HID, 512], BF16, cctx)
                gd_s = sb("gd_s", [NG_INT, 512], BF16, cctx)
                zs_buf = sb("zs_buf", [NG_HID, 2 * 2 * 128], BF16, cctx)
                zsd_buf = sb("zsd_buf", [NG_INT, 2 * 512], BF16, cctx)
                y1r_buf = sb("y1r_buf", [128, 2 * 512], BF16, cctx)
                lnv2 = sb("lnv2", [128, 8], F32, cctx)
                p_cg = psm("p_cg", [128, 512], F32, cctx)
                p_cu = psm("p_cu", [128, 512], F32, cctx)
                p_gt = psm("p_gt", [NG_HID, 512], F32, cctx)
                p_gd = psm("p_gd", [NG_INT, 512], F32, cctx)
                p_d4 = [psm(f"p_d{i}", [128, 512], F32, cctx) for i in range(4)]

                @on("sync")
                def _(sy):
                    indl.inc(sy.dma_start(
                        out=s_indg.ap().rearrange("p (c g) -> p c g", g=NG_HID),
                        in_=indg_i.rearrange("(c p) g -> p c g", p=128)), 16)
                    indl.inc(sy.dma_start(
                        out=s_indd.ap().rearrange("p (c g) -> p c g", g=NG_INT),
                        in_=indd_i.rearrange("(c p) g -> p c g", p=128)), 16)
                    sy.wait_ge(indl.h, indl.v)
                IND_LD = indl.v

                c3_done = [None]
                gt_mm = [None]; gt_ev = [None]; gt_free = [None]
                cg_mm = [None]; cu_mm = [None]; sil_ea = [None]
                pcg_free = [None]; pcu_free = [None]
                gd_mm = [None]; gd_ev = [None]
                wd_mm = [None]; dcor_mm = [None]
                wfree = {}; h_done = {}; wdfree = {}; pd_free = {}; y1rfree = {}
                zs_free = {}; zsd_free = {}

                for rs in range(4):
                    a2_ev = {}
                    with ExitStack() as rsctx:
                        c2ctx = ExitStack()
                        y1t_buf = c2ctx.enter_context(nc.sbuf_tensor(f"y1t_{rs}", [128, NHC * 512], BF16))

                        @on("sync")
                        def _(sy, rs=rs):
                            if c3_done[0] is not None:
                                sy.wait_ge(et.h, c3_done[0])
                            for dc in range(NHC):
                                y1tl.inc(sy.dma_start(
                                    out=y1t_buf.ap()[:, dc * 512:(dc + 1) * 512],
                                    in_=y1_dram[rs * 512:(rs + 1) * 512, dc * 128:(dc + 1) * 128],
                                    transpose=True), 16)
                            sy.wait_ge(y1tl.h, y1tl.v)
                        Y1T_LD = y1tl.v

                        @on("tensor")
                        def _(te, rs=rs):
                            te.wait_ge(y1tl.h, Y1T_LD)
                            te.wait_ge(indl.h, IND_LD)
                            if gt_free[0] is not None:
                                sm, val = gt_free[0]
                                te.wait_ge(sm.h, val)
                            inst = None
                            for dc in range(NHC):
                                inst = te.matmul(
                                    p_gt[:], s_indg.ap()[:, dc * NG_HID:(dc + 1) * NG_HID],
                                    y1t_buf.ap()[:, dc * 512:(dc + 1) * 512],
                                    start=(dc == 0), stop=(dc == NHC - 1))
                            gt_mm[0] = et.inc(inst)

                        @on("vector")
                        def _(ve):
                            ve.wait_ge(et.h, gt_mm[0])
                            ve.wait_ge(ev.h, ev.v)
                            gt_ev[0] = ev.inc(ve.tensor_copy(gt_s[:], p_gt[:]))
                        if "dgt" in dbg and rs == 0:
                            @on("sync")
                            def _(sy):
                                sy.wait_ge(ev.h, gt_ev[0])
                                qst.inc(sy.dma_start(out=dbg_out["dgt"], in_=gt_s[:]), 16)

                        for ic in range(NIC):
                            pidx = ic % 2
                            for wl, wbuf, wdr, which in [(wgl, wg_buf, w4g_dram, "g"),
                                                         (wul, wu_buf, w4u_dram, "u")]:
                                @on("sync")
                                def _(sy, ic=ic, pidx=pidx, wl=wl, wbuf=wbuf, wdr=wdr, which=which):
                                    if wfree.get((which, pidx)) is not None:
                                        sm, val = wfree[(which, pidx)]
                                        sy.wait_ge(sm.h, val)
                                    wl[pidx].inc(sy.dma_start(
                                        out=wbuf.ap()[:, pidx * D:(pidx + 1) * D]
                                            .rearrange("p (c q) -> p c q", q=128),
                                        in_=wdr[ic]), 16)

                            @on("sync")
                            def _(sy, ic=ic, pidx=pidx):
                                if zs_free.get(pidx) is not None:
                                    sm, val = zs_free[pidx]
                                    sy.wait_ge(sm.h, val)
                                zgl[pidx].inc(sy.dma_start(
                                    out=zs_buf[0:NG_HID, pidx * 256: pidx * 256 + 128],
                                    in_=zsg_dram[:, ic * 128:(ic + 1) * 128]), 16)
                                zgl[pidx].inc(sy.dma_start(
                                    out=zs_buf[0:NG_HID, pidx * 256 + 128: pidx * 256 + 256],
                                    in_=zsu_dram[:, ic * 128:(ic + 1) * 128]), 16)

                            @on("tensor")
                            def _(te, rs=rs, ic=ic, pidx=pidx):
                                te.wait_ge(wgl[pidx].h, wgl[pidx].v)
                                te.wait_ge(wul[pidx].h, wul[pidx].v)
                                te.wait_ge(zgl[pidx].h, zgl[pidx].v)
                                te.wait_ge(ev.h, gt_ev[0])
                                if pcg_free[0] is not None:
                                    sm, val = pcg_free[0]
                                    te.wait_ge(sm.h, val)
                                inst = None
                                for dc in range(NHC):
                                    inst = te.matmul(
                                        p_cg[:], wg_buf.ap()[:, pidx * D + dc * 128: pidx * D + (dc + 1) * 128],
                                        y1t_buf.ap()[:, dc * 512:(dc + 1) * 512],
                                        start=(dc == 0), stop=False)
                                inst = te.matmul(
                                    p_cg[:], zs_buf[0:NG_HID, pidx * 256: pidx * 256 + 128],
                                    gt_s[:], start=False, stop=True)
                                cg_mm[0] = et.inc(inst)
                                if pcu_free[0] is not None:
                                    sm, val = pcu_free[0]
                                    te.wait_ge(sm.h, val)
                                inst = None
                                for dc in range(NHC):
                                    inst = te.matmul(
                                        p_cu[:], wu_buf.ap()[:, pidx * D + dc * 128: pidx * D + (dc + 1) * 128],
                                        y1t_buf.ap()[:, dc * 512:(dc + 1) * 512],
                                        start=(dc == 0), stop=False)
                                inst = te.matmul(
                                    p_cu[:], zs_buf[0:NG_HID, pidx * 256 + 128: pidx * 256 + 256],
                                    gt_s[:], start=False, stop=True)
                                cu_mm[0] = et.inc(inst)
                                wfree[("g", pidx)] = (et, cg_mm[0])
                                wfree[("u", pidx)] = (et, cu_mm[0])
                                zs_free[pidx] = (et, cu_mm[0])

                            @on("scalar")
                            def _(sc, ic=ic):
                                sc.wait_ge(et.h, cg_mm[0])
                                sc.wait_ge(ea.h, ea.v)
                                sil_ea[0] = ea.inc(sc.activation(out=sg_buf[:], in_=p_cg[:], func=AF.Silu))
                                pcg_free[0] = (ea, sil_ea[0])

                            @on("vector")
                            def _(ve, ic=ic):
                                ve.wait_ge(ea.h, sil_ea[0])
                                ve.wait_ge(et.h, cu_mm[0])
                                ve.wait_ge(ev.h, ev.v)
                                i1 = ve.tensor_tensor(
                                    out=h_buf.ap()[:, ic * 512:(ic + 1) * 512],
                                    in0=sg_buf[:], in1=p_cu[:], op=ALU.mult)
                                hev = ev.inc(i1)
                                pcu_free[0] = (ev, hev)
                                h_done[ic] = hev

                            @on("tensor")
                            def _(te, ic=ic):
                                te.wait_ge(ev.h, h_done[ic])
                                inst = te.matmul(
                                    p_gd[:], s_indd.ap()[:, ic * NG_INT:(ic + 1) * NG_INT],
                                    h_buf.ap()[:, ic * 512:(ic + 1) * 512],
                                    start=(ic == 0), stop=(ic == NIC - 1))
                                gd_mm[0] = et.inc(inst)

                        @on("vector")
                        def _(ve):
                            ve.wait_ge(et.h, gd_mm[0])
                            ve.wait_ge(ev.h, ev.v)
                            gd_ev[0] = ev.inc(ve.tensor_copy(gd_s[:], p_gd[:]))
                            gt_free[0] = (ev, gd_ev[0])

                        if "dh" in dbg and rs == 0:
                            @on("sync")
                            def _(sy):
                                sy.wait_ge(ev.h, ev.v)
                                qst.inc(sy.dma_start(
                                    out=dbg_out["dh"],
                                    in_=h_buf.ap().rearrange("p (a q) -> p a q", q=512)), 16)

                        c2ctx.close()
                        a2g = sb(f"a2g_{rs}", [128, 4 * D], BF16, rsctx)
                        y2_stg = sb(f"y2stg_{rs}", [128, D], F32, rsctx)
                        for js in range(8):
                            @on("sync")
                            def _(sy, js=js):
                                pz = js % 2
                                if zsd_free.get(pz) is not None:
                                    sm, val = zsd_free[pz]
                                    sy.wait_ge(sm.h, val)
                                zdl.inc(sy.dma_start(
                                    out=zsd_buf[0:NG_INT, (js % 2) * 512:(js % 2) * 512 + 512],
                                    in_=zsd_dram[:, js * 512:(js + 1) * 512]), 16)
                            ZD_LD = zdl.v

                            for ic in range(NIC):
                                pidx = ic % 4

                                @on("sync")
                                def _(sy, rs=rs, js=js, ic=ic, pidx=pidx):
                                    if wdfree.get(pidx) is not None:
                                        sm, val = wdfree[pidx]
                                        sy.wait_ge(sm.h, val)
                                    wdl[pidx].inc(sy.dma_start(
                                        out=wd_buf.ap()[:, pidx * 512:(pidx + 1) * 512],
                                        in_=w4d_dram[128 * ic:128 * ic + 128, 512 * js:512 * js + 512]), 16)

                                @on("tensor")
                                def _(te, js=js, ic=ic, pidx=pidx):
                                    te.wait_ge(wdl[pidx].h, wdl[pidx].v)
                                    if ic == 0:
                                        te.wait_ge(ev.h, gd_ev[0])
                                        for rt in range(4):
                                            if pd_free.get(rt) is not None:
                                                sm, val = pd_free[rt]
                                                te.wait_ge(sm.h, val)
                                    inst = None
                                    for rt in range(4):
                                        inst = te.matmul(
                                            p_d4[rt][:],
                                            h_buf.ap()[:, ic * 512 + rt * 128: ic * 512 + (rt + 1) * 128],
                                            wd_buf.ap()[:, pidx * 512:(pidx + 1) * 512],
                                            start=(ic == 0), stop=False)
                                    wd_mm[0] = et.inc(inst)
                                    wdfree[pidx] = (et, wd_mm[0])

                            @on("tensor")
                            def _(te, js=js, ZD_LD=ZD_LD):
                                te.wait_ge(zdl.h, ZD_LD)
                                inst = None
                                for rt in range(4):
                                    inst = te.matmul(
                                        p_d4[rt][:], gd_s[0:NG_INT, rt * 128:(rt + 1) * 128],
                                        zsd_buf[0:NG_INT, (js % 2) * 512:(js % 2) * 512 + 512],
                                        start=False, stop=True)
                                dcor_mm[0] = et.inc(inst)
                                c3_done[0] = dcor_mm[0]
                                zsd_free[js % 2] = (et, dcor_mm[0])

                            for rt in range(4):
                                pidx = (js * 4 + rt) % 2

                                @on("sync")
                                def _(sy, rs=rs, js=js, rt=rt, pidx=pidx):
                                    if y1rfree.get(pidx) is not None:
                                        sm, val = y1rfree[pidx]
                                        sy.wait_ge(sm.h, val)
                                    y1l[pidx].inc(sy.dma_start(
                                        out=y1r_buf.ap()[:, pidx * 512:(pidx + 1) * 512],
                                        in_=y1_dram[rs * 512 + rt * 128: rs * 512 + (rt + 1) * 128,
                                                    512 * js:512 * js + 512]), 16)

                                @on("vector")
                                def _(ve, js=js, rt=rt, pidx=pidx):
                                    ve.wait_ge(et.h, dcor_mm[0])
                                    ve.wait_ge(y1l[pidx].h, y1l[pidx].v)
                                    ve.wait_ge(ev.h, ev.v)
                                    i1 = ve.tensor_tensor(
                                        out=a2g.ap()[:, rt * D + 512 * js: rt * D + 512 * js + 512],
                                        in0=p_d4[rt][:],
                                        in1=y1r_buf.ap()[:, pidx * 512:(pidx + 1) * 512],
                                        op=ALU.add)
                                    lm = ev.inc(i1)
                                    pd_free[rt] = (ev, lm)
                                    y1rfree[pidx] = (ev, lm)
                                    a2_ev[(js, rt)] = lm

                        for rt in range(4):
                            @on("scalar")
                            def _(sc, rt=rt):
                                for js in range(8):
                                    sc.wait_ge(ev.h, a2_ev[(js, rt)])
                                sc.wait_ge(y2s[0].h, y2s[0].v)
                                sc.wait_ge(ea.h, ea.v)
                                a2sl = a2g.ap()[:, rt * D:(rt + 1) * D]
                                ea.inc(sc.activation(out=y2_stg[:], in_=a2sl,
                                                     func=AF.Identity, accum_out=lnv2[:, 0:1]))
                                sc.wait_ge(ea.h, ea.v)
                                ea.inc(sc.activation(out=y2_stg[:], in_=a2sl,
                                                     func=AF.Square, accum_out=lnv2[:, 1:2]))
                            STATS2 = ea.v

                            @on("vector")
                            def _(ve):
                                ve.wait_ge(ea.h, STATS2)
                                ve.wait_ge(ev.h, ev.v)
                                ev.inc(ve.tensor_scalar(out=lnv2[:, 2:3], in0=lnv2[:, 0:1], scalar1=1.0 / D, scalar2=None, op0=ALU.mult))
                                ve.wait_ge(ev.h, ev.v)
                                ev.inc(ve.tensor_scalar(out=lnv2[:, 3:4], in0=lnv2[:, 1:2], scalar1=1.0 / D, scalar2=None, op0=ALU.mult))
                                ve.wait_ge(ev.h, ev.v)
                                ev.inc(ve.tensor_tensor(out=lnv2[:, 4:5], in0=lnv2[:, 2:3], in1=lnv2[:, 2:3], op=ALU.mult))
                                ve.wait_ge(ev.h, ev.v)
                                ev.inc(ve.tensor_tensor(out=lnv2[:, 5:6], in0=lnv2[:, 3:4], in1=lnv2[:, 4:5], op=ALU.subtract))
                                ve.wait_ge(ev.h, ev.v)
                                ev.inc(ve.tensor_scalar(out=lnv2[:, 5:6], in0=lnv2[:, 5:6], scalar1=EPS, scalar2=None, op0=ALU.add))
                                ve.wait_ge(ev.h, ev.v)
                                ev.inc(ve.reciprocal(lnv2[:, 6:7], lnv2[:, 5:6]))
                            RECV2 = ev.v

                            @on("scalar")
                            def _(sc):
                                sc.wait_ge(ev.h, RECV2)
                                sc.wait_ge(ea.h, ea.v)
                                ea.inc(sc.activation(out=lnv2[:, 6:7], in_=lnv2[:, 6:7], func=AF.Sqrt))
                            RSTD2 = ea.v

                            @on("vector")
                            def _(ve):
                                ve.wait_ge(ea.h, RSTD2)
                                ve.wait_ge(ev.h, ev.v)
                                ev.inc(ve.tensor_tensor(out=lnv2[:, 7:8], in0=lnv2[:, 2:3], in1=lnv2[:, 6:7], op=ALU.mult))
                                ve.wait_ge(ev.h, ev.v)
                                ev.inc(ve.tensor_scalar(out=lnv2[:, 7:8], in0=lnv2[:, 7:8], scalar1=-1.0, scalar2=None, op0=ALU.mult))
                            NB2 = ev.v

                            @on("scalar")
                            def _(sc, rt=rt):
                                sc.wait_ge(ev.h, NB2)
                                sc.wait_ge(y2s[0].h, y2s[0].v)
                                sc.wait_ge(ea.h, ea.v)
                                ea.inc(sc.activation(
                                    out=y2_stg[:],
                                    in_=a2g.ap()[:, rt * D:(rt + 1) * D],
                                    func=AF.Identity, bias=lnv2[:, 7:8], scale=lnv2[:, 6:7]))
                            Y2A = ea.v

                            @on("sync")
                            def _(sy, rs=rs, rt=rt):
                                sy.wait_ge(ea.h, Y2A)
                                y2s[0].inc(sy.dma_start(
                                    out=out[rs * 512 + rt * 128: rs * 512 + (rt + 1) * 128, :],
                                    in_=y2_stg[:]), 16)
                        @on("sync")
                        def _(sy):
                            sy.wait_ge(y2s[0].h, y2s[0].v)

        @on("sync")
        def _(sy):
            for sm in [qld] + all_dma_sems:
                if sm.v > 0:
                    sy.wait_ge(sm.h, sm.v)
            if qst.v > 0:
                sy.wait_ge(qst.h, qst.v)

    return nc


def kernel(**inputs):
    hs = np.ascontiguousarray(inputs["L0_attn_hidden_states"])
    Wq = inputs["L0_attn_Wq"]; Wk = inputs["L0_attn_Wk"]
    Wv = inputs["L0_attn_Wv"]; Wo = inputs["L0_attn_Wo"]
    gi_g = np.asarray(inputs["L0_ffn_gate_g_idx"])
    gi_u = np.asarray(inputs["L0_ffn_up_g_idx"])
    gi_d = np.asarray(inputs["L0_ffn_down_g_idx"])

    def tile_groups(gi):
        g2 = gi.reshape(-1, 128)
        assert (g2 == g2[:, :1]).all(), "g_idx not tile-constant"
        return g2[:, 0].astype(int)

    gmap_g, gmap_u, gmap_d = tile_groups(gi_g), tile_groups(gi_u), tile_groups(gi_d)
    t = host_tables()
    nc = build_program(gmap_g, gmap_u, gmap_d)

    base = make_base_inputs(inputs, t, gi_g, gi_d)
    in_maps = []
    for b in range(B):
        xb = np.ascontiguousarray(hs[b])
        m = dict(base)
        m["x_nat"] = xb
        m["x_t"] = np.ascontiguousarray(xb.T)
        in_maps.append(m)

    res = run_bass_kernel_spmd(nc, in_maps, core_ids=list(range(B)))
    return np.stack([np.asarray(r["out"], dtype=np.float32) for r in res.results])


def make_base_inputs(inputs, t, gi_g, gi_d):
    Wq = inputs["L0_attn_Wq"]; Wk = inputs["L0_attn_Wk"]
    Wv = inputs["L0_attn_Wv"]; Wo = inputs["L0_attn_Wo"]
    wqkvT = np.concatenate([Wq.T, Wk.T, Wv.T], axis=1)
    wqkv_t = np.ascontiguousarray(
        wqkvT.reshape(NHC, 128, 48, 128).transpose(2, 1, 0, 3))
    wo_t = np.ascontiguousarray(
        Wo.T.reshape(NHC, 128, 8, 512).transpose(2, 1, 0, 3))
    return {
        "wqkv_t": wqkv_t, "wo_t": wo_t,
        "g_qw": inputs["L0_ffn_gate_qweight"], "g_sc": inputs["L0_ffn_gate_scales"],
        "g_qz": inputs["L0_ffn_gate_qzeros"],
        "u_qw": inputs["L0_ffn_up_qweight"], "u_sc": inputs["L0_ffn_up_scales"],
        "u_qz": inputs["L0_ffn_up_qzeros"],
        "d_qw": inputs["L0_ffn_down_qweight"], "d_sc": inputs["L0_ffn_down_scales"],
        "d_qz": inputs["L0_ffn_down_qzeros"],
        "cosT": t["cosT"], "sinF": t["sinF"], "diagm": t["diagm"], "diagmT": t["diagmT"],
        "caus01": t["caus01"],
        "z0c": t["z0"], "sscale": t["sscale"], "keepc": t["keep"], "cinvl": t["cinvl"],
        "iota8": t["iota8"], "idb": t["idb"], "idf": t["idf"],
        "onesc": t["onesc"], "onesrf": t["onesrf"], "shv": t["shv"],
        "maskv": t["maskv"], "pw2": t["pw2"],
        "ind_g": one_hot_ind(gi_g, NG_HID), "ind_d": one_hot_ind(gi_d, NG_INT),
    }
